# version 13
# baseline (speedup 1.0000x reference)
import numpy as np
import concourse.bass as bass
import concourse.mybir as mybir

F32 = mybir.dt.float32
BF16 = mybir.dt.bfloat16
AF = mybir.ActivationFunctionType
ALU = mybir.AluOpType
AX = mybir.AxisListType

P = 128
SEQ = 2048
D = 1024
NT = SEQ // P
FC = D // P
NB_PER_CORE = 2



class KB:
    ENGINES = ("T", "V", "S", "G", "Y")

    NPOOL = 6

    def __init__(self, nc):
        self.nc = nc
        self.plan = {e: [] for e in self.ENGINES}
        self.sem = {}
        self.cnt = {}
        for name in ("T", "V", "S", "G", "Y"):
            self.sem[name] = nc.alloc_semaphore("sem_" + name)
            self.cnt[name] = 0
        self.dma_pool = {}
        self.dma_rr = {}
        for q in ("G", "Y", "S"):
            names = []
            for i in range(self.NPOOL):
                nm = f"D{q}{i}"
                self.sem[nm] = nc.alloc_semaphore("sem_" + nm)
                self.cnt[nm] = 0
                names.append(nm)
            self.dma_pool[q] = names
            self.dma_rr[q] = 0
        self.hwm = {}

    def op(self, E, fn, waits=(), sem=None, inc=1):
        if sem is None:
            sem = E
        ws = []
        for w in waits:
            if w is None:
                continue
            s, v = w
            k = (E, s)
            if self.hwm.get(k, 0) >= v:
                continue
            if not s.startswith("D"):
                self.hwm[k] = v
            ws.append((s, v))
        self.plan[E].append((fn, ws, sem, inc))
        self.cnt[sem] += inc
        return (sem, self.cnt[sem])

    def dma(self, E, out, in_, waits=(), **kw):
        pool = self.dma_pool[E]
        sem = pool[self.dma_rr[E] % len(pool)]
        self.dma_rr[E] += 1
        w = tuple(waits)
        if self.cnt[sem]:
            w = w + ((sem, self.cnt[sem]),)
        return self.op(E, lambda e: e.dma_start(out=out, in_=in_, **kw),
                       waits=w, sem=sem, inc=16)

    CHAIN = {"tensor_scalar", "tensor_copy", "tensor_tensor", "tensor_add",
             "tensor_sub", "tensor_mul", "tensor_max", "reduce_max", "reduce_sum",
             "tensor_reduce", "reciprocal", "memset", "memzero", "bn_stats",
             "bn_aggr", "scalar_tensor_tensor", "select", "activation",
             "tensor_single_scalar", "tensor_scalar_mul", "tensor_scalar_add",
             "iota", "affine_select", "tensor_relu", "copy", "mul", "add",
             "square", "sqrt"}

    class _Proxy:
        def __init__(self, kb, eng, name):
            self._kb, self._eng, self._name = kb, eng, name

        def __getattr__(self, attr):
            f = getattr(self._eng, attr)
            if attr not in KB.CHAIN:
                return f
            kb, name = self._kb, self._name

            def wrapped(*a, **k):
                inst = f(*a, **k)
                c = kb.chain_cnt[name]
                if c:
                    inst._wait_ge(kb.chain_sem[name], c)
                inst.then_inc(kb.chain_sem[name], 1)
                kb.chain_cnt[name] += 1
                return inst
            return wrapped

    def emit(self):
        nc = self.nc
        self.chain_sem = {}
        self.chain_cnt = {}
        for e in ("V", "S", "G"):
            self.chain_sem[e] = nc.alloc_semaphore("chain_" + e)
            self.chain_cnt[e] = 0
        with nc.Block() as block:
            def run(eng_name):
                def body(raw):
                    chained = eng_name in ("V", "S", "G")
                    eng = KB._Proxy(self, raw, eng_name) if chained else raw
                    for fn, ws, sem, inc in self.plan[eng_name]:
                        for s, v in ws:
                            raw.wait_ge(self.sem[s], v)
                        c0 = self.chain_cnt.get(eng_name, 0)
                        inst = fn(eng)
                        if inst is not None and inc:
                            if chained and self.chain_cnt[eng_name] > c0:
                                tail = raw.nop()
                                tail._wait_ge(self.chain_sem[eng_name],
                                              self.chain_cnt[eng_name])
                                tail.then_inc(self.sem[sem], inc)
                            else:
                                inst.then_inc(self.sem[sem], inc)
                return body
            block.tensor(run("T"))
            block.vector(run("V"))
            block.scalar(run("S"))
            block.gpsimd(run("G"))
            block.sync(run("Y"))


class Arena:
    def __init__(self, nc, name, kib_per_part=None):
        if kib_per_part is None:
            self.bytes = nc.sbuf_bytes_remaining & ~63
        else:
            self.bytes = kib_per_part * 1024
        self.h = nc.alloc_sbuf_tensor(name, [P, self.bytes // 4], F32)
        self.f32 = self.h.ap()
        self.b16 = self.f32.bitcast(BF16)
        self.off = 0
        self.marks = []

    def alloc_f32(self, cols):
        o = (self.off + 3) & ~3
        self.off = o + 4 * cols
        assert self.off <= self.bytes, f"arena overflow {self.off}>{self.bytes}"
        return self.f32[:, o // 4: o // 4 + cols]

    def alloc_b16(self, cols):
        o = (self.off + 3) & ~3
        self.off = o + 2 * cols
        assert self.off <= self.bytes, f"arena overflow {self.off}>{self.bytes}"
        return self.b16[:, o // 2: o // 2 + cols]

    def push(self):
        self.marks.append(self.off)

    def pop(self):
        self.off = self.marks.pop()


class Ctx:
    def __init__(self, kb, arena, pb, consts, pbig=None):
        self.kb = kb
        self.arena = arena
        self.pb = pb
        self.pbig = pbig
        self.pbb = [b.bitcast(BF16) for b in pb]
        self.c = consts
        self.pb_tok = [None] * 8
        self.pb_wtok = [None] * 8

    def mm(self, bank, stat_list, mov_list, waits=(), n=None, start=True,
           stop=True):
        kb = self.kb
        ps = self.pb[bank]
        if n is None:
            n = mov_list[0].shape[-1]

        mo = stat_list[0].shape[-1]

        def fn(t):
            last = None
            m = len(stat_list)
            for i, (st, mv) in enumerate(zip(stat_list, mov_list)):
                last = t.matmul(ps[0:mo, 0:n], st, mv,
                                start=(start and i == 0),
                                stop=(stop and i == m - 1))
            return last
        w = tuple(waits)
        if start:
            w = w + (self.pb_tok[bank],)
        tok = kb.op("T", fn, waits=w)
        self.pb_wtok[bank] = tok
        return tok

    def transposes(self, bank, pairs, waits=()):
        kb = self.kb
        psb = self.pbb[bank]
        ident = self.c["ident_b16"]

        def fn(t):
            last = None
            for off, src in pairs:
                ncols = src.shape[-1]
                last = t.transpose(psb[0:ncols, off:off + src.shape[0]], src, ident)
            return last
        tok = kb.op("T", fn, waits=tuple(waits) + (self.pb_tok[bank],))
        self.pb_wtok[bank] = tok
        return tok

    def evac(self, E, bank, fn, waits=()):
        tok = self.kb.op(E, fn, waits=tuple(waits) + (self.pb_wtok[bank],))
        self.pb_tok[bank] = tok
        return tok



def emit_ln(ctx, stats, x_ap, out_cb, waits=(), width=D, eps=1e-5, prev_s=None,
            sq_scratch=None):
    kb = ctx.kb
    rw = 1.0 / width
    tok_sum = kb.op("V", lambda v: v.reduce_sum(stats[:, 4:5], x_ap, axis=AX.X),
                    waits=tuple(waits) + (prev_s,))
    tok_sq = kb.op("S", lambda s: s.activation(
        sq_scratch[:, 0:width], x_ap, AF.Square, bias=0.0, scale=1.0,
        accum_out=stats[:, 5:6]), waits=tuple(waits) + (prev_s,))

    def fn_var(v):
        v.tensor_scalar(stats[:, 0:1], stats[:, 4:5], rw, None, ALU.mult)
        v.tensor_scalar(stats[:, 9:10], stats[:, 4:5], -rw, None, ALU.mult)
        v.tensor_scalar(stats[:, 6:7], stats[:, 5:6], rw, None, ALU.mult)
        v.tensor_scalar(stats[:, 7:8], stats[:, 0:1], stats[:, 0:1], None, ALU.mult)
        return v.tensor_sub(stats[:, 1:2], stats[:, 6:7], stats[:, 7:8])
    tok_v = kb.op("V", fn_var, waits=(tok_sum, tok_sq))

    def fn_rstd(s):
        s.activation(stats[:, 2:3], stats[:, 1:2], AF.Ln, bias=ctx.c["c_eps"], scale=1.0)
        s.activation(stats[:, 3:4], stats[:, 2:3], AF.Exp, bias=0.0, scale=-0.5)
        return s.activation(stats[:, 10:11], stats[:, 9:10], AF.Identity,
                            bias=0.0, scale=stats[:, 3:4])
    tok_s = kb.op("S", fn_rstd, waits=(tok_v,))
    out_cb(stats[:, 0:1], stats[:, 3:4], tok_s)
    return tok_s



def emit_l0_attn(ctx, x_sb, x_tok, wq_in, wo_in, entry=None, stop_after=None):
    kb, ar = ctx.kb, ctx.arena
    NH, DK = 16, 64
    ar.push()
    qT = ar.alloc_b16(NT * D)
    kT = ar.alloc_b16(NT * D)
    vv = ar.alloc_b16(NT * D)
    stats = ar.alloc_f32(64)
    xnt = ar.alloc_b16(D)
    wqk = [ar.alloc_b16(FC * P) for _ in range(2)]
    lnT = ar.alloc_b16(NT * D)
    ot2 = ar.alloc_b16(4 * D)
    wo_sb = lnT[:, 0:FC * D]
    ptile2 = [lnT[:, FC * D:FC * D + SEQ],
              lnT[:, FC * D + 2 * SEQ + 2 * D:FC * D + 3 * SEQ + 2 * D]]
    ptT = lnT[:, FC * D + SEQ:FC * D + 2 * SEQ]
    otile2 = [ot2[:, 0:D], ot2[:, 2 * D:3 * D]]
    otT2 = [ot2[:, D:2 * D], ot2[:, 3 * D:4 * D]]

    lnT3 = lnT.rearrange("p (c s) -> p c s", c=FC)
    qT3 = qT.rearrange("p (h s) -> p h s", h=FC)
    kT3 = kT.rearrange("p (h s) -> p h s", h=FC)
    vv3 = vv.rearrange("p (t f) -> p t f", t=NT)

    ln_toks = []
    prev_s = None
    for tt in range(NT):
        xap = x_sb[:, tt * D:(tt + 1) * D]

        def cb(mean, rstd, tok_s, tt=tt, xap=xap):
            tok_a = kb.op("V", lambda v: v.tensor_scalar(
                xnt, xap, mean, rstd, ALU.subtract, ALU.mult),
                waits=(tok_s, ln_toks[-1] if ln_toks else None))
            bank = 6 + (tt % 2)
            pairs = [(c * P, xnt[:, c * P:(c + 1) * P]) for c in range(FC)]
            tok_t = ctx.transposes(bank, pairs, waits=(tok_a, ctx.c["load_tok"]))
            tok_e = ctx.evac("V", bank, lambda v, tt=tt, bank=bank: v.tensor_copy(
                lnT3[:, :, tt * P:(tt + 1) * P],
                ctx.pbb[bank][:, 0:FC * P].rearrange("p (c s) -> p c s", s=P)))
            ln_toks.append(tok_e)
        prev_s = emit_ln(ctx, stats, xap, cb,
                         waits=(x_tok, entry, ln_toks[-1] if ln_toks else None),
                         prev_s=prev_s, sq_scratch=xnt)
    lnT_done = ln_toks[-1]
    if stop_after == "ln":
        ctx.dbg = dict(lnT=lnT)
        ar.pop()
        return lnT_done

    wv4 = qT[:, 0:FC * D].rearrange("p (c f) -> p c f", c=FC)
    wv_toks = []
    for c in range(FC):
        wv_toks.append(kb.dma("G", wv4[:, c, :].rearrange("p (h n) -> p h n", h=NH),
                       wq_in[:, 2, c * P:(c + 1) * P, :].rearrange("h p n -> p h n"),
                       waits=(entry,)))
    v_toks = []
    for tt in range(NT):
        for half in range(2):
            bank = 2 + (tt * 2 + half) % 2
            stat = [lnT3[:, c, tt * P:(tt + 1) * P] for c in range(FC)]
            mov = [wv4[:, c, half * 512:(half + 1) * 512] for c in range(FC)]
            tok_m = ctx.mm(bank, stat, mov, waits=tuple(wv_toks) + (lnT_done,))
            if half == 0:
                tok_e = ctx.evac("V", bank, lambda v, tt=tt, half=half, bank=bank:
                                 v.tensor_copy(vv3[:, tt, half * 512:(half + 1) * 512],
                                               ctx.pb[bank][:, 0:512]))
            else:
                tok_e = ctx.evac("S", bank, lambda s_, tt=tt, half=half, bank=bank:
                                 s_.copy(vv3[:, tt, half * 512:(half + 1) * 512],
                                         ctx.pb[bank][:, 0:512]))
            v_toks.append(tok_e)
    v_done = v_toks[-1]

    wk_tok = {}
    proj_done = None
    for ai, dT3 in ((1, kT3), (0, qT3)):
        for hp in range(NH // 2):
            wb = wqk[hp % 2]
            wb4 = wb.rearrange("p (c h n) -> p c h n", c=FC, h=2)
            w_toks = []
            for h2 in range(2):
                w_toks.append(kb.dma(
                    "G", wb4[:, :, h2, :],
                    wq_in[2 * hp + h2, ai, :, :].rearrange("(c p) n -> p c n", p=P),
                    waits=(wk_tok.get(hp % 2), entry)))
            for tn in range(SEQ // 512):
                bank = tn % 2
                stat = [wb[:, c * P:(c + 1) * P] for c in range(FC)]
                mov = [lnT3[:, c, tn * 512:(tn + 1) * 512] for c in range(FC)]
                tok_m = ctx.mm(bank, stat, mov,
                               waits=tuple(w_toks) + (lnT_done, v_done))
                if tn % 2 == 0:
                    tok_e = ctx.evac("V", bank, lambda v, hp=hp, tn=tn, bank=bank,
                                     dT3=dT3: v.tensor_copy(
                        dT3[:, hp, tn * 512:(tn + 1) * 512], ctx.pb[bank][:, 0:512]))
                else:
                    tok_e = ctx.evac("S", bank, lambda s_, hp=hp, tn=tn, bank=bank,
                                     dT3=dT3: s_.copy(
                        dT3[:, hp, tn * 512:(tn + 1) * 512], ctx.pb[bank][:, 0:512]))
                wk_tok[hp % 2] = tok_e
                proj_done = tok_e

    wo4 = wo_sb.rearrange("p (c f) -> p c f", c=FC)
    tok_wo = kb.dma("G", wo4, wo_in.rearrange("(c p) f -> p c f", p=P),
                    waits=(proj_done,))

    SC = 0.125
    x_done = None

    def tail_l0(hh_state):
        h, ptile, ex_toks, tok_r, bs, otile = hh_state
        pt_toks = []
        for g in range(2):
            bank = bs + g
            pairs = [(i * P, ptile[:, (g * 8 + i) * P:(g * 8 + i + 1) * P])
                     for i in range(8)]
            ctx.transposes(bank, pairs, waits=tuple(ex_toks))
            if g == 0:
                tok_e = ctx.evac("V", bank, lambda v, g=g, bank=bank: v.tensor_copy(
                    ptT[:, g * 8 * P:(g + 1) * 8 * P], ctx.pbb[bank][:, 0:8 * P]))
            else:
                tok_e = ctx.evac("S", bank, lambda s, g=g, bank=bank: s.copy(
                    ptT[:, g * 8 * P:(g + 1) * 8 * P], ctx.pbb[bank][:, 0:8 * P]))
            pt_toks.append(tok_e)
        stat = [ptT[:, kc * P:(kc + 1) * P] for kc in range(NT)]
        mov = [vv3[:, kc, h * DK:(h + 1) * DK] for kc in range(NT)]
        ctx.mm(bs + 2, stat, mov, waits=tuple(pt_toks) + (v_done,), n=DK)
        return ctx.evac("S", bs + 2, lambda s, h=h, otile=otile: s.activation(
            otile[:, h * DK:(h + 1) * DK], ctx.pb[bs + 2][:, 0:DK],
            AF.Identity, bias=0.0, scale=stats[:, 51 + h % 2:52 + h % 2]),
            waits=(tok_r,))

    qt_tail = None
    for qt in range(NT):
        hstate = None
        for h in range(NH):
            hp, hh = h // 2, h % 2
            ptile = ptile2[h % 2]
            bs = 4 * (h % 2)
            mb = 16 + 12 * (h % 2)
            sc_toks = []
            for kc4 in range(4):
                bank = bs + kc4
                stat = [qT3[hh * 64:(hh + 1) * 64, hp, qt * P:(qt + 1) * P]]
                mov = [kT3[hh * 64:(hh + 1) * 64, hp, kc4 * 512:(kc4 + 1) * 512]]
                sc_toks.append(ctx.mm(bank, stat, mov, waits=(proj_done,)))
            tok_e = kb.op("S", lambda s, bs=bs, ptile=ptile, mb=mb: s.activation(
                ptile, ctx.pbig[:, bs * 512:(bs + 4) * 512],
                AF.Exp, bias=0.0, scale=SC,
                accum_out=stats[:, mb + 10:mb + 11]),
                waits=tuple(sc_toks) + (tok_wo,))
            for kc4 in range(4):
                ctx.pb_tok[bs + kc4] = tok_e
            ex_toks = [tok_e]
            tok_r = kb.op("V", lambda v, h=h, mb=mb: v.reciprocal(
                stats[:, 51 + h % 2:52 + h % 2], stats[:, mb + 10:mb + 11]),
                waits=(tok_e,))
            if h == 2 and qt_tail is not None:
                x_done = qt_tail()
                qt_tail = None
            if hstate is not None:
                tail_l0(hstate)
            hstate = (h, ptile, ex_toks, tok_r, bs, otile2[qt % 2])
        tok_o = tail_l0(hstate)

        def mk_tail(qt=qt, tok_o=tok_o):
            otile, otT = otile2[qt % 2], otT2[qt % 2]

            def emit():
                pairs = [(c * P, otile[:, c * P:(c + 1) * P]) for c in range(FC)]
                ctx.transposes(3, pairs, waits=(tok_o,))
                tok_e = ctx.evac("V", 3, lambda v: v.tensor_copy(
                    otT, ctx.pbb[3][:, 0:FC * P]))
                for half in range(2):
                    bank = half
                    stat = [otT[:, c * P:(c + 1) * P] for c in range(FC)]
                    mov = [wo4[:, c, half * 512:(half + 1) * 512] for c in range(FC)]
                    ctx.mm(bank, stat, mov, waits=(tok_e, tok_wo))
                    xap = x_sb[:, qt * D + half * 512: qt * D + (half + 1) * 512]
                    tok_x = ctx.evac("V", bank, lambda v, xap=xap, bank=bank:
                                     v.tensor_add(xap, xap, ctx.pb[bank][:, 0:512]))
                return tok_x
            return emit
        if qt == NT - 1:
            x_done = mk_tail()()
        else:
            qt_tail = mk_tail()
    ar.pop()
    return x_done




def emit_lnT_phase(ctx, x_sb, lnT3, xnt, stats, x_tok, entry, save_mr=None):
    kb = ctx.kb
    ln_toks = []
    prev_s = None
    for tt in range(NT):
        xap = x_sb[:, tt * D:(tt + 1) * D]

        def cb(mean, rstd, tok_s, tt=tt, xap=xap):
            if save_mr is not None:
                def fn_mr(v, tt=tt):
                    v.tensor_copy(save_mr[:, 2 * tt:2 * tt + 1], stats[:, 0:1])
                    return v.tensor_copy(save_mr[:, 2 * tt + 1:2 * tt + 2], stats[:, 3:4])
                kb.op("V", fn_mr, waits=(tok_s,))
            tok_a = kb.op("S", lambda s_, rstd=rstd: s_.activation(
                xnt, xap, AF.Identity, bias=stats[:, 10:11], scale=rstd),
                waits=(tok_s, ln_toks[-1] if ln_toks else None))
            bank = 6 + (tt % 2)
            pairs = [(c * P, xnt[:, c * P:(c + 1) * P]) for c in range(FC)]
            tok_t = ctx.transposes(bank, pairs, waits=(tok_a, ctx.c["load_tok"]))
            eng = "V" if tt % 2 == 0 else "S"
            if eng == "V":
                tok_e = ctx.evac("V", bank, lambda v, tt=tt, bank=bank: v.tensor_copy(
                    lnT3[:, :, tt * P:(tt + 1) * P],
                    ctx.pbb[bank][:, 0:FC * P].rearrange("p (c s) -> p c s", s=P)))
            else:
                tok_e = ctx.evac("S", bank, lambda s_, tt=tt, bank=bank: s_.copy(
                    lnT3[:, :, tt * P:(tt + 1) * P],
                    ctx.pbb[bank][:, 0:FC * P].rearrange("p (c s) -> p c s", s=P)))
            ln_toks.append(tok_e)
        prev_s = emit_ln(ctx, stats, xap, cb,
                         waits=(x_tok, entry, ln_toks[-1] if ln_toks else None),
                         prev_s=prev_s, sq_scratch=xnt)
    return ln_toks[-1]


def _l0f_flush(ctx, kb, act3, unit, prev_down):
    ic, tc, A, Bb, Cc, tok_sg, tok_u2 = unit
    tok_gl = kb.op("G", lambda g, A=A, Bb=Bb: g.tensor_mul(A, A, Bb),
                   waits=(tok_sg, tok_u2))
    return kb.op("G", lambda g, ic=ic, tc=tc, A=A, Cc=Cc: g.tensor_mul(
        act3[:, ic, tc * 512:(tc + 1) * 512], A, Cc),
        waits=(tok_gl, prev_down))


def emit_l0_ffn(ctx, x_sb, x_tok, rw_in, rb_in, gup_in, gub_in, dp_in, db_in,
                entry=None):
    kb, ar = ctx.kb, ctx.arena
    NE, NIC = 8, 16
    ar.push()
    lnT = ar.alloc_b16(NT * D)
    act = ar.alloc_b16(NIC * 1024)
    dpsb = ar.alloc_b16(NIC * 1024)
    stats = ar.alloc_f32(64)
    xnt = ar.alloc_b16(D)
    mr = ar.alloc_f32(2 * NT)
    A2 = [ar.alloc_f32(512) for _ in range(2)]
    B2 = [ar.alloc_f32(512) for _ in range(2)]
    C2 = [ar.alloc_f32(512) for _ in range(2)]
    wrouter = ar.alloc_f32(NT * 8)
    rwT = ar.alloc_f32(FC * 8)
    rw_stage = ar.alloc_f32(1024)
    rb_row = ar.alloc_f32(8)
    Sb = ar.alloc_f32(8)
    rb_b = ar.alloc_f32(8)
    db_all = ar.alloc_f32(1024)
    xT = ar.alloc_f32(FC * P)
    wtb = ar.alloc_f32(P)
    rsc = ar.alloc_f32(40)
    gwb = [ar.alloc_b16(FC * P) for _ in range(4)]
    gub_stage = ar.alloc_f32(P)
    gub_sb = ar.alloc_f32(32)
    lnT3 = lnT.rearrange("p (c s) -> p c s", c=FC)
    act3 = act.rearrange("p (i s) -> p i s", i=NIC)
    dp3 = dpsb.rearrange("p (i s) -> p i s", i=NIC)
    rwT3 = rwT.rearrange("p (c e) -> p c e", c=FC)
    xT3 = xT.rearrange("p (c s) -> p c s", c=FC)
    probs0, probs, t1, t2, lgt = (rsc[:, 0:8], rsc[:, 8:16], rsc[:, 16:24],
                                  rsc[:, 24:32], rsc[:, 32:40])

    lnT_done = emit_lnT_phase(ctx, x_sb, lnT3, xnt, stats, x_tok, entry, save_mr=mr)

    tok_rw = kb.dma("Y", rw_stage[0:8, :], rw_in, waits=(entry,))
    tok_rb = kb.dma("Y", rb_row[0:1, :], rb_in.rearrange("(o e) -> o e", o=1), waits=(entry,))
    tok_dbl = kb.dma("Y", db_all[0:8, :], db_in, waits=(entry,))

    def fn_trw(t):
        last = None
        for c in range(FC):
            last = t.transpose(ctx.pb[2][:, c * 8:(c + 1) * 8],
                               rw_stage[0:8, c * P:(c + 1) * P],
                               ctx.c["ident_f32"][0:8, 0:8])
        return last
    tok_trw = kb.op("T", fn_trw, waits=(tok_rw, ctx.c["load_tok"], ctx.pb_tok[2],
                                        ctx.pb_wtok[2]))
    ctx.pb_wtok[2] = tok_trw
    tok_rwT = ctx.evac("V", 2, lambda v: v.tensor_copy(rwT, ctx.pb[2][:, 0:FC * 8]))
    tok_S1 = ctx.mm(3, [rwT3[:, c, :] for c in range(FC)],
                    [ctx.c["ones_col"] for _ in range(FC)], waits=(tok_rwT,), n=1)
    tok_S2 = ctx.evac("V", 3, lambda v: v.tensor_copy(stats[0:8, 32:33],
                                                      ctx.pb[3][0:8, 0:1]))

    def fn_tS(t):
        return t.transpose(ctx.pb[2][0:1, 0:8], stats[0:8, 32:33],
                           ctx.c["ident_f32"][0:8, 0:8])
    tok_tS = kb.op("T", fn_tS, waits=(tok_S2, ctx.pb_tok[2], ctx.pb_wtok[2]))
    ctx.pb_wtok[2] = tok_tS
    tok_Srow = ctx.evac("V", 2, lambda v: v.tensor_copy(stats[0:1, 33:41],
                                                        ctx.pb[2][0:1, 0:8]))
    tok_bc = ctx.mm(3, [ctx.c["ones_row"]], [stats[0:1, 33:41]], waits=(tok_Srow,), n=8)
    tok_Sb = ctx.evac("V", 3, lambda v: v.tensor_copy(Sb, ctx.pb[3][:, 0:8]))
    tok_bc2 = ctx.mm(3, [ctx.c["ones_row"]], [rb_row[0:1, 0:8]], waits=(tok_rb,), n=8)
    tok_rbb = ctx.evac("V", 3, lambda v: v.tensor_copy(rb_b, ctx.pb[3][:, 0:8]))

    wdb_toks = []
    for tt in range(NT):
        for bk, cs in ((0, (0, 1, 2, 3)), (1, (4, 5, 6, 7))):
            def fn_tx(t, tt=tt, bk=bk, cs=cs):
                last = None
                for i, c in enumerate(cs):
                    last = t.transpose(ctx.pb[bk][:, i * P:(i + 1) * P],
                                       x_sb[:, tt * D + c * P:tt * D + (c + 1) * P],
                                       ctx.c["ident_f32"])
                return last
            tok_t = kb.op("T", fn_tx, waits=(x_tok, ctx.pb_tok[bk], ctx.pb_wtok[bk],
                                             ctx.c["load_tok"]))
            ctx.pb_wtok[bk] = tok_t
            tok_e = ctx.evac("V", bk, lambda v, bk=bk, cs=cs: v.tensor_copy(
                xT[:, cs[0] * P:(cs[0] + 4) * P], ctx.pb[bk][:, 0:4 * P]))
        tok_xT = tok_e
        tok_G = ctx.mm(3, [xT3[:, c, :] for c in range(FC)],
                       [rwT3[:, c, :] for c in range(FC)], waits=(tok_xT,), n=8)
        tok_nr = kb.op("V", lambda v, tt=tt: v.tensor_scalar(
            stats[:, 42:43], mr[:, 2 * tt:2 * tt + 1], mr[:, 2 * tt + 1:2 * tt + 2],
            -1.0, ALU.mult, ALU.mult))
        tok_l1 = ctx.evac("V", 3, lambda v, tt=tt: v.tensor_scalar(
            lgt, ctx.pb[3][:, 0:8], mr[:, 2 * tt + 1:2 * tt + 2], None, ALU.mult))
        tok_l2 = kb.op("V", lambda v: v.scalar_tensor_tensor(
            t1, Sb, stats[:, 42:43], lgt, ALU.mult, ALU.add),
            waits=(tok_l1, tok_nr, tok_Sb))
        tok_l3 = kb.op("V", lambda v: v.tensor_add(lgt, t1, rb_b),
                       waits=(tok_l2, tok_rbb))
        tok_m = kb.op("V", lambda v: v.reduce_max(stats[:, 43:44], lgt, axis=AX.X,
                                                  negate=True), waits=(tok_l3,))
        tok_ex = kb.op("S", lambda s: s.activation(
            probs0, lgt, AF.Exp, bias=stats[:, 43:44], scale=1.0,
            accum_out=stats[:, 44:45]), waits=(tok_m,))
        tok_ri = kb.op("V", lambda v: v.reciprocal(stats[:, 45:46], stats[:, 44:45]),
                       waits=(tok_ex,))
        tok_p = kb.op("V", lambda v: v.tensor_scalar(
            probs, probs0, stats[:, 45:46], None, ALU.mult), waits=(tok_ri,))
        tok_pm = kb.op("V", lambda v: v.reduce_max(stats[:, 46:47], probs, axis=AX.X),
                       waits=(tok_p,))
        tok_eq = kb.op("V", lambda v: v.tensor_scalar(
            t1, probs, stats[:, 46:47], None, ALU.is_equal), waits=(tok_pm,))
        tok_iv = kb.op("V", lambda v: v.tensor_scalar(
            t2, t1, -1.0, 1.0, ALU.mult, ALU.add), waits=(tok_eq,))
        tok_p2 = kb.op("V", lambda v: v.tensor_mul(t1, probs, t2), waits=(tok_iv,))
        tok_m2 = kb.op("V", lambda v: v.reduce_max(stats[:, 47:48], t1, axis=AX.X),
                       waits=(tok_p2,))
        tok_ge = kb.op("V", lambda v: v.tensor_scalar(
            t2, probs, stats[:, 47:48], None, ALU.is_ge), waits=(tok_m2,))
        tok_w = kb.op("V", lambda v, tt=tt: v.tensor_mul(
            wrouter[:, tt * 8:(tt + 1) * 8], probs, t2), waits=(tok_ge,))
        def fn_tw(t, tt=tt):
            return t.transpose(ctx.pb[2][0:8, 0:P], wrouter[:, tt * 8:(tt + 1) * 8],
                               ctx.c["ident_f32"])
        tok_tw = kb.op("T", fn_tw, waits=(tok_w, ctx.pb_tok[2], ctx.pb_wtok[2]))
        ctx.pb_wtok[2] = tok_tw
        tok_wt = ctx.evac("V", 2, lambda v: v.tensor_copy(wtb[0:8, :],
                                                          ctx.pb[2][0:8, 0:P]))
        for half in range(2):
            bk = 4 + half
            tok_mdb = ctx.mm(bk, [wtb[0:8, :]],
                             [db_all[0:8, half * 512:(half + 1) * 512]],
                             waits=(tok_wt, tok_dbl))
            xap = x_sb[:, tt * D + half * 512:tt * D + (half + 1) * 512]
            tok_xadd = ctx.evac("V", bk, lambda v, xap=xap, bk=bk: v.tensor_add(
                xap, xap, ctx.pb[bk][:, 0:512]))
            wdb_toks.append(tok_xadd)
    w_done = wdb_toks[-1]

    gw_tok = {}
    dp_tok = None
    x_done = w_done
    for e in range(NE):
        tok_gr = kb.dma("Y", gub_stage[0:32, :],
                        gub_in[e].rearrange("(g p) -> g p", p=P),
                        waits=(gw_tok.get("gub"), entry))

        def fn_tg(t):
            return t.transpose(ctx.pb[2][:, 0:32], gub_stage[0:32, :],
                               ctx.c["ident_f32"][0:32, 0:32])
        tok_tg = kb.op("T", fn_tg, waits=(tok_gr, ctx.pb_tok[2], ctx.pb_wtok[2]))
        ctx.pb_wtok[2] = tok_tg
        tok_gub = ctx.evac("V", 2, lambda v: v.tensor_copy(gub_sb,
                                                           ctx.pb[2][:, 0:32]))
        gw_tok["gub"] = tok_gub
        tok_dp = kb.dma("G", dp3, dp_in[e].rearrange("(i p) h -> p i h", p=P),
                        waits=(dp_tok, entry))
        act_done = []
        prev_down = gw_tok.get("prev_down")
        pend = gw_tok.setdefault("pend", {})
        for th in range(2):
            t0 = th * 1024
            for ic in range(NIC):
                bufs = {}
                for gi, guc in ((0, ic), (1, NIC + ic)):
                    wb = gwb[2 * (ic % 2) + gi]
                    tok_lw = kb.dma("G",
                                    wb.rearrange("p (c s) -> p c s", c=FC),
                                    gup_in[e, :, guc * P:(guc + 1) * P].rearrange(
                                        "(c p) n -> p c n", p=P),
                                    waits=(gw_tok.get((ic % 2, gi)), entry))
                    bufs[gi] = (wb, tok_lw)
                for tc in range(2):
                    A, Bb, Cc = A2[tc], B2[tc], C2[tc]
                    mov = [lnT3[:, c, t0 + tc * 512:t0 + (tc + 1) * 512]
                           for c in range(FC)]
                    wb, tok_lw = bufs[0]
                    tok_g = ctx.mm(tc, [wb[:, c * P:(c + 1) * P] for c in range(FC)],
                                   mov, waits=(tok_lw, lnT_done))
                    wb, tok_lw = bufs[1]
                    tok_u = ctx.mm(2 + tc,
                                   [wb[:, c * P:(c + 1) * P] for c in range(FC)],
                                   mov, waits=(tok_lw, lnT_done))
                    gub_ap = gub_sb[:, ic:ic + 1]
                    gub_ap2 = gub_sb[:, NIC + ic:NIC + ic + 1]
                    pa = None
                    if tc in pend:
                        pa = _l0f_flush(ctx, kb, act3, pend.pop(tc), prev_down)
                        act_done.append(pa)
                    tok_mv = ctx.evac("V", tc, lambda v, tc=tc, gub_ap=gub_ap, A=A:
                                      v.tensor_scalar(A, ctx.pb[tc][:, 0:512],
                                                      gub_ap, 7.0, ALU.add, ALU.min),
                                      waits=(tok_gub, pa))
                    tok_sg = kb.op("S", lambda s, A=A, Bb=Bb: s.activation(
                        Bb, A, AF.Sigmoid, bias=0.0, scale=1.702), waits=(tok_mv,))
                    tok_u1 = ctx.evac("V", 2 + tc,
                                      lambda v, tc=tc, gub_ap2=gub_ap2, Cc=Cc:
                                      v.tensor_scalar(Cc, ctx.pb[2 + tc][:, 0:512],
                                                      gub_ap2, 7.0, ALU.add, ALU.min),
                                      waits=(tok_gub, pa))
                    tok_u22 = kb.op("G", lambda g, Cc=Cc: g.tensor_scalar(
                        Cc, Cc, -7.0, 1.0, ALU.max, ALU.add), waits=(tok_u1,))
                    pend[tc] = (ic, tc, A, Bb, Cc, tok_sg, tok_u22)
                    gw_tok[(ic % 2, 0)] = tok_g
                    gw_tok[(ic % 2, 1)] = tok_u
            for tc in list(pend):
                act_done.append(_l0f_flush(ctx, kb, act3, pend.pop(tc), prev_down))
            for tti in range(8):
                tt = th * 8 + tti
                for half in range(2):
                    bk = 4 + (tti % 2) * 2 + half
                    stat = [act3[:, ic, tti * P:(tti + 1) * P] for ic in range(NIC)]
                    mov = [dp3[:, ic, half * 512:(half + 1) * 512] for ic in range(NIC)]
                    tok_d = ctx.mm(bk, stat, mov,
                                   waits=(act_done[-1], tok_dp))
                    prev_down = tok_d
                    xap = x_sb[:, tt * D + half * 512:tt * D + (half + 1) * 512]
                    wap = wrouter[:, tt * 8 + e:tt * 8 + e + 1]
                    x_done = ctx.evac("V", bk, lambda v, xap=xap, bk=bk, wap=wap:
                                      v.scalar_tensor_tensor(
                                          xap, ctx.pb[bk][:, 0:512], wap, xap,
                                          ALU.mult, ALU.add), waits=(w_done,))
            act_done = []
        gw_tok["prev_down"] = prev_down
        dp_tok = tok_d
    ar.pop()
    return x_done



MAGIC = 12582912.0
INV2PI = float(np.float32(1.0 / (2.0 * np.pi)))
C1 = 6.28125
C2 = float(np.float32(2.0 * np.pi - 6.28125))
PI2 = float(np.float32(np.pi / 2))
PI_LO = float(np.nextafter(np.float32(np.pi), np.float32(0.0)))


def emit_pe_add(ctx, xnt, tt, A0, S1, S3, s_ap, ang0_in, prev_tok, x_ready):
    kb = ctx.kb
    tok_a = kb.dma("Y", A0, ang0_in[:, tt, :], waits=(prev_tok, x_ready))
    tok_s = kb.op("V", lambda v: v.tensor_scalar(A0, A0, s_ap, None, ALU.mult),
                  waits=(tok_a, prev_tok))
    out_toks = []
    for trig in range(2):
        def fn_n(v, trig=trig):
            if trig == 0:
                v.tensor_scalar(S3, A0, INV2PI, MAGIC, ALU.mult, ALU.add)
            else:
                v.tensor_scalar(S3, A0, INV2PI, 0.25, ALU.mult, ALU.add)
                v.tensor_scalar(S3, S3, MAGIC, None, ALU.add)
            v.tensor_scalar(S3, S3, MAGIC, None, ALU.subtract)
            v.scalar_tensor_tensor(S1, S3, C1, A0, ALU.mult, ALU.subtract)
            v.scalar_tensor_tensor(S1, S3, -C2, S1, ALU.mult, ALU.subtract)
            if trig == 1:
                v.tensor_scalar(S1, S1, PI2, None, ALU.add)
            return v.tensor_scalar(S1, S1, PI_LO, -PI_LO, ALU.min, ALU.max)
        tok_r = kb.op("V", fn_n, waits=(tok_s,) + tuple(out_toks))
        tok_sin = kb.op("S", lambda s: s.activation(
            S3, S1, AF.Sin, bias=0.0, scale=1.0), waits=(tok_r,))
        tok_add = kb.op("V", lambda v, trig=trig: v.tensor_add(
            xnt.rearrange("p (i two) -> p i two", two=2)[:, :, trig],
            xnt.rearrange("p (i two) -> p i two", two=2)[:, :, trig], S3),
            waits=(tok_sin,))
        out_toks.append(tok_add)
    return out_toks[-1]


def emit_pe_add3(ctx, xnt, A0, S3, pbank, s_ap, dma_toks, x_ready, pb_guard):
    kb = ctx.kb
    pn = ctx.pb[pbank][:, 0:512]
    out = None
    for trig, src in ((0, A0), (1, S3)):
        def fn_n(v, trig=trig, src=src):
            v.tensor_scalar(src, src, s_ap, None, ALU.mult)
            if trig == 0:
                v.tensor_scalar(pn, src, INV2PI, MAGIC, ALU.mult, ALU.add)
            else:
                v.tensor_scalar(pn, src, INV2PI, 0.25, ALU.mult, ALU.add)
                v.tensor_scalar(pn, pn, MAGIC, None, ALU.add)
            v.tensor_scalar(pn, pn, MAGIC, None, ALU.subtract)
            v.scalar_tensor_tensor(src, pn, -C1, src, ALU.mult, ALU.add)
            v.scalar_tensor_tensor(src, pn, -C2, src, ALU.mult, ALU.add)
            if trig == 1:
                v.tensor_scalar(src, src, PI2, None, ALU.add)
            return v.tensor_scalar(src, src, PI_LO, -PI_LO, ALU.min, ALU.max)
        tok_r = kb.op("V", fn_n, waits=dma_toks + (x_ready, out, pb_guard))
        tok_sin = kb.op("S", lambda s, src=src: s.activation(
            pn, src, AF.Sin, bias=0.0, scale=1.0), waits=(tok_r,))
        out = kb.op("V", lambda v, trig=trig: v.tensor_add(
            xnt.rearrange("p (i two) -> p i two", two=2)[:, :, trig],
            xnt.rearrange("p (i two) -> p i two", two=2)[:, :, trig], pn),
            waits=(tok_sin,))
    return out


def emit_pe_add2(ctx, xnt, tt, A0, S3, s_ap, ang0_in, prev_tok, x_ready):
    kb = ctx.kb
    out = None
    for trig in range(2):
        tok_a = kb.dma("Y", A0, ang0_in[:, tt, :],
                       waits=(prev_tok, x_ready, out))
        tok_s = kb.op("V", lambda v: v.tensor_scalar(A0, A0, s_ap, None, ALU.mult),
                      waits=(tok_a,))

        def fn_n(v, trig=trig):
            if trig == 0:
                v.tensor_scalar(S3, A0, INV2PI, MAGIC, ALU.mult, ALU.add)
            else:
                v.tensor_scalar(S3, A0, INV2PI, 0.25, ALU.mult, ALU.add)
                v.tensor_scalar(S3, S3, MAGIC, None, ALU.add)
            v.tensor_scalar(S3, S3, MAGIC, None, ALU.subtract)
            v.scalar_tensor_tensor(A0, S3, -C1, A0, ALU.mult, ALU.add)
            v.scalar_tensor_tensor(A0, S3, -C2, A0, ALU.mult, ALU.add)
            if trig == 1:
                v.tensor_scalar(A0, A0, PI2, None, ALU.add)
            return v.tensor_scalar(A0, A0, PI_LO, -PI_LO, ALU.min, ALU.max)
        tok_r = kb.op("V", fn_n, waits=(tok_s,))
        tok_sin = kb.op("S", lambda s: s.activation(
            S3, A0, AF.Sin, bias=0.0, scale=1.0), waits=(tok_r,))
        out = kb.op("V", lambda v, trig=trig: v.tensor_add(
            xnt.rearrange("p (i two) -> p i two", two=2)[:, :, trig],
            xnt.rearrange("p (i two) -> p i two", two=2)[:, :, trig], S3),
            waits=(tok_sin,))
    return out


def emit_l1_attn(ctx, x_sb, x_tok, wqT_in, wkT_in, wvT_in, woT_in, bq_cm_in,
                 bk_cm_in, bv_b_in, bo_b_in, gam_b_in, bet_b_in, rby_in, ang0_in,
                 entry=None, stop_after=None):
    kb, ar = ctx.kb, ctx.arena
    NH, DK = 16, 64
    ar.push()
    qT = ar.alloc_b16(NT * D)
    kT = ar.alloc_b16(NT * D)
    vv = ar.alloc_b16(NT * D)
    lnpeT = ar.alloc_b16(NT * D)
    stats = ar.alloc_f32(62)
    mr = ar.alloc_f32(2 * NT)
    wqk = ar.alloc_b16(FC * P)
    bqcm = ar.alloc_f32(8)
    bkcm = ar.alloc_f32(8)
    rbuf = ar.alloc_f32(16)
    rsb = ar.alloc_f32(8)

    qf = qT.bitcast(F32)
    xnt1 = qf[:, 0:D]
    A0a = qf[:, D:D + 512]
    S1a = qf[:, D + 512:D + 1024]
    S3a = qf[:, D + 1024:D + 1536]
    tpeb = qT[:, 5120:6144]
    wvT_sb = qT[:, 0:FC * D]
    bvb = qT[:, FC * D:FC * D + 2 * D].bitcast(F32)
    wx = ar.alloc_b16(1024)
    S3b = ar.alloc_f32(512)
    A0b = wx.bitcast(F32)
    woT_buf = [lnpeT[:, i * 512:(i + 1) * 512] for i in range(4)] + [
        lnpeT[:, 7168 + i * 512:7168 + (i + 1) * 512] for i in range(2)]
    ptile2 = [lnpeT[:, 2048:4096], lnpeT[:, 4096:6144]]
    ptT = lnpeT[:, 6144:7168]
    ot2b = ar.alloc_b16(2 * D)
    xn2 = ar.alloc_f32(D)
    otile2 = [wqk[:, 0:D], ot2b[:, 0:D]]
    otT2 = [tpeb, ot2b[:, D:2 * D]]
    gamb = lnpeT[:, 8192:10240].bitcast(F32)
    betb = lnpeT[:, 10240:12288].bitcast(F32)
    bob = lnpeT[:, 12288:14336].bitcast(F32)
    xnt6_2 = [lnpeT[:, 14336:16384].bitcast(F32), xn2]

    lnpeT3 = lnpeT.rearrange("p (c s) -> p c s", c=FC)
    qT3 = qT.rearrange("p (h s) -> p h s", h=FC)
    kT3 = kT.rearrange("p (h s) -> p h s", h=FC)
    vv3 = vv.rearrange("p (t f) -> p t f", t=NT)
    wvT4 = wvT_sb.rearrange("p (c f) -> p c f", c=FC)

    tok_rby = kb.dma("Y", rbuf[0:1, 0:3], rby_in.rearrange("(o r) -> o r", o=1),
                     waits=(entry,))

    def fn_sc(v):
        r, b, y = rbuf[0:1, 0:1], rbuf[0:1, 1:2], rbuf[0:1, 2:3]
        row = rbuf[0:1, 4:12]
        v.memset(row, 0.0)
        v.tensor_scalar(row[:, 0:1], y, 0.1, 1.0, ALU.mult, ALU.add)
        v.reciprocal(row[:, 0:1], row[:, 0:1])
        v.tensor_scalar(rbuf[0:1, 12:13], r, 0.1, 1.0, ALU.mult, ALU.add)
        v.scalar_tensor_tensor(rbuf[0:1, 12:13], b, 0.05, rbuf[0:1, 12:13],
                               ALU.mult, ALU.add)
        v.scalar_tensor_tensor(rbuf[0:1, 12:13], y, 0.15, rbuf[0:1, 12:13],
                               ALU.mult, ALU.add)
        v.tensor_scalar(row[:, 1:2], rbuf[0:1, 12:13], 0.125, None, ALU.mult)
        v.tensor_scalar(row[:, 2:3], y, 1.2246467991473532e-18, 1.0,
                        ALU.mult, ALU.add)
        v.tensor_scalar(row[:, 3:4], b, 0.1, 1.0, ALU.mult, ALU.add)
        v.tensor_scalar(row[:, 3:4], row[:, 3:4], 1e-5, None, ALU.mult)
        v.tensor_scalar(row[:, 4:5], r, 0.1, 1.0, ALU.mult, ALU.add)
        return v.scalar_tensor_tensor(row[:, 4:5], y, 0.05, row[:, 4:5],
                                      ALU.mult, ALU.add)
    tok_sc = kb.op("V", fn_sc, waits=(tok_rby,))
    tok_bc = ctx.mm(5, [ctx.c["ones_row"]], [rbuf[0:1, 4:12]], waits=(tok_sc,), n=8)
    tok_rsb = ctx.evac("V", 5, lambda v: v.tensor_copy(rsb, ctx.pb[5][:, 0:8]))
    s_pe = rsb[:, 0:1]
    s_ls = rsb[:, 1:2]
    s_vf = rsb[:, 2:3]
    s_eps = rsb[:, 3:4]
    s_gs = rsb[:, 4:5]

    tok_bq = kb.dma("Y", bqcm, bq_cm_in, waits=(entry,))
    tok_bk = kb.dma("Y", bkcm, bk_cm_in, waits=(entry,))

    ln_toks = []
    prev_s = None
    for tt in range(NT):
        xap = x_sb[:, tt * D:(tt + 1) * D]

        def cb(mean, rstd, tok_s, tt=tt, xap=xap):
            def fn_mr(v, tt=tt):
                v.tensor_copy(mr[:, 2 * tt:2 * tt + 1], stats[:, 0:1])
                return v.tensor_copy(mr[:, 2 * tt + 1:2 * tt + 2], stats[:, 3:4])
            kb.op("V", fn_mr, waits=(tok_s,))
            tok_a = kb.op("V", lambda v: v.tensor_scalar(
                xnt1, xap, mean, rstd, ALU.subtract, ALU.mult),
                waits=(tok_s, ln_toks[-1] if ln_toks else None))
            tok_pe = emit_pe_add(ctx, xnt1, tt, A0a, S1a, S3a, s_pe, ang0_in,
                                 ln_toks[-1] if ln_toks else tok_rsb, tok_a)
            tok_c = kb.op("V", lambda v: v.tensor_copy(tpeb, xnt1), waits=(tok_pe,))
            bank = 6 + (tt % 2)
            pairs = [(c * P, tpeb[:, c * P:(c + 1) * P]) for c in range(FC)]
            tok_t = ctx.transposes(bank, pairs, waits=(tok_c, ctx.c["load_tok"]))
            tok_e = ctx.evac("V", bank, lambda v, tt=tt, bank=bank: v.tensor_copy(
                lnpeT3[:, :, tt * P:(tt + 1) * P],
                ctx.pbb[bank][:, 0:FC * P].rearrange("p (c s) -> p c s", s=P)))
            ln_toks.append(tok_e)
        prev_s = emit_ln(ctx, stats, xap, cb,
                         waits=(x_tok, entry, ln_toks[-1] if ln_toks else None),
                         prev_s=prev_s, sq_scratch=xnt1)
    lnpeT_done = ln_toks[-1]
    ctx.dbg = dict(lnpeT=lnpeT, kT=kT, vv=vv, qT=qT)
    if stop_after == "ln":
        ar.pop()
        return lnpeT_done

    tok_wv = kb.dma("G", wvT4, wvT_in.rearrange("(c p) f -> p c f", p=P),
                    waits=(entry, lnpeT_done))
    tok_bv = kb.dma("Y", bvb, bv_b_in, waits=(entry, lnpeT_done))
    v_toks = []
    for tt in range(NT):
        for half in range(2):
            bank = 2 + (tt * 2 + half) % 2
            stat = [lnpeT3[:, c, tt * P:(tt + 1) * P] for c in range(FC)]
            mov = [wvT4[:, c, half * 512:(half + 1) * 512] for c in range(FC)]
            tok_m = ctx.mm(bank, stat, mov, waits=(tok_wv, lnpeT_done))
            tok_e = ctx.evac("V", bank, lambda v, tt=tt, half=half, bank=bank:
                             v.tensor_add(vv3[:, tt, half * 512:(half + 1) * 512],
                                          ctx.pb[bank][:, 0:512],
                                          bvb[:, half * 512:(half + 1) * 512]),
                             waits=(tok_bv,))
            v_toks.append(tok_e)
    v_done = v_toks[-1]

    wk_tok = None
    proj_done = None
    for wT_in, dT3, bcm, btok in ((wkT_in, kT3, bkcm, tok_bk),
                                  (wqT_in, qT3, bqcm, tok_bq)):
        for oc in range(FC):
            src = wT_in[:, oc * P:(oc + 1) * P].rearrange("(c p) n -> p c n", p=P)
            tok_w = kb.dma("G", wqk.rearrange("p (c s) -> p c s", c=FC), src,
                           waits=(wk_tok, entry))
            for tn in range(SEQ // 512):
                bank = tn % 2
                stat = [wqk[:, c * P:(c + 1) * P] for c in range(FC)]
                mov = [lnpeT3[:, c, tn * 512:(tn + 1) * 512] for c in range(FC)]
                tok_m = ctx.mm(bank, stat, mov, waits=(tok_w, lnpeT_done, v_done))
                tok_e = ctx.evac("S", bank, lambda s_, oc=oc, tn=tn, bank=bank,
                                 dT3=dT3, bcm=bcm: s_.activation(
                    dT3[:, oc, tn * 512:(tn + 1) * 512], ctx.pb[bank][:, 0:512],
                    AF.Identity, bias=bcm[:, oc:oc + 1], scale=1.0), waits=(btok,))
                wk_tok = tok_e
                proj_done = tok_e

    if stop_after == "proj":
        ar.pop()
        return proj_done

    tok_gm = kb.dma("Y", gamb, gam_b_in, waits=(proj_done,))
    tok_gm2 = kb.op("V", lambda v: v.tensor_scalar(gamb, gamb, s_gs, None, ALU.mult),
                    waits=(tok_gm,))
    tok_bt = kb.dma("Y", betb, bet_b_in, waits=(proj_done,))
    tok_bo = kb.dma("Y", bob, bo_b_in, waits=(proj_done,))

    wo_tok = {}
    x_done = None

    def tail_l1(hh_state):
        h, ptile, ex_toks, tok_r, bs, otile = hh_state
        av_tok = None
        for g in range(2):
            bank = bs + g
            pairs = [(i * P, ptile[:, (g * 8 + i) * P:(g * 8 + i + 1) * P])
                     for i in range(8)]
            tok_t = ctx.transposes(bank, pairs, waits=tuple(ex_toks))
            if g == 0:
                tok_pte = ctx.evac("V", bank, lambda v, bank=bank: v.tensor_copy(
                    ptT, ctx.pbb[bank][:, 0:8 * P]))
            else:
                tok_pte = ctx.evac("S", bank, lambda s_, bank=bank: s_.copy(
                    ptT, ctx.pbb[bank][:, 0:8 * P]))
            stat = [ptT[:, i * P:(i + 1) * P] for i in range(8)]
            mov = [vv3[:, g * 8 + i, h * DK:(h + 1) * DK] for i in range(8)]
            av_tok = ctx.mm(bs + 2, stat, mov, waits=(tok_pte, v_done), n=DK,
                            start=(g == 0), stop=(g == 1))
        return ctx.evac("V", bs + 2, lambda v, h=h, otile=otile: v.tensor_scalar(
            otile[:, h * DK:(h + 1) * DK], ctx.pb[bs + 2][:, 0:DK],
            stats[:, 60 + h % 2:61 + h % 2], s_vf, ALU.mult, ALU.mult),
            waits=(tok_r,))

    pe_tok = None
    pe_dma = (kb.dma("Y", A0b, ang0_in[:, 0, :], waits=(proj_done,)),
              kb.dma("Y", S3b, ang0_in[:, 0, :], waits=(proj_done,)))
    qt_tail = None
    for qt in range(NT):
        xap = x_sb[:, qt * D:(qt + 1) * D]
        xnt6 = xnt6_2[qt % 2]

        def fn_ln1(v, qt=qt, xap=xap, xnt6=xnt6):
            return v.tensor_scalar(xnt6, xap, mr[:, 2 * qt:2 * qt + 1],
                                   mr[:, 2 * qt + 1:2 * qt + 2],
                                   ALU.subtract, ALU.mult)
        tok_a1 = kb.op("V", fn_ln1, waits=(pe_tok, proj_done))
        pe_tok = emit_pe_add3(ctx, xnt6, A0b, S3b, 7, s_pe, pe_dma, tok_a1,
                              ctx.pb_tok[7])
        ctx.pb_tok[7] = pe_tok
        ctx.pb_wtok[7] = pe_tok
        if qt + 1 < NT:
            pe_dma = (kb.dma("Y", A0b, ang0_in[:, qt + 1, :], waits=(pe_tok,)),
                      kb.dma("Y", S3b, ang0_in[:, qt + 1, :], waits=(pe_tok,)))
        hstate = None
        for h in range(NH):
            hp, hh = h // 2, h % 2
            ptile = ptile2[h % 2]
            bs = 4 * (h % 2)
            mb = 32 + 12 * (h % 2)
            sc_toks = []
            for kc4 in range(4):
                bank = bs + kc4
                stat = [qT3[hh * 64:(hh + 1) * 64, hp, qt * P:(qt + 1) * P]]
                mov = [kT3[hh * 64:(hh + 1) * 64, hp, kc4 * 512:(kc4 + 1) * 512]]
                sc_toks.append(ctx.mm(bank, stat, mov, waits=(proj_done,)))
            tok_e = kb.op("S", lambda s, bs=bs, ptile=ptile, mb=mb: s.activation(
                ptile, ctx.pbig[:, bs * 512:(bs + 4) * 512],
                AF.Exp, bias=0.0, scale=s_ls,
                accum_out=stats[:, mb + 10:mb + 11]),
                waits=tuple(sc_toks) + (tok_rsb, tok_gm2, tok_bt, tok_bo))
            for kc4 in range(4):
                ctx.pb_tok[bs + kc4] = tok_e
            ex_toks = [tok_e]
            tok_r = kb.op("V", lambda v, h=h, mb=mb: v.reciprocal(
                stats[:, 60 + h % 2:61 + h % 2], stats[:, mb + 10:mb + 11]),
                waits=(tok_e,))
            if h == 2 and qt_tail is not None:
                x_done = qt_tail()
                qt_tail = None
            if hstate is not None:
                tail_l1(hstate)
            hstate = (h, ptile, ex_toks, tok_r, bs, otile2[qt % 2])
        tok_o = tail_l1(hstate)

        def mk_tail(qt=qt, tok_o=tok_o, xap=xap, xnt6=xnt6, pe_tok=pe_tok):
            otile, otT = otile2[qt % 2], otT2[qt % 2]

            def emit():
                pairs = [(c * P, otile[:, c * P:(c + 1) * P]) for c in range(FC)]
                ctx.transposes(3, pairs, waits=(tok_o,))
                tok_e = ctx.evac("V", 3, lambda v: v.tensor_copy(
                    otT, ctx.pbb[3][:, 0:FC * P]))
                u_evac = []
                for half in range(2):
                    bank = half
                    for c in range(FC):
                        wb = woT_buf[(c % 3) * 2 + half]
                        tok_w = kb.dma(
                            "G", wb,
                            woT_in[c * P:(c + 1) * P, half * 512:(half + 1) * 512],
                            waits=(wo_tok.get((c % 3, half)), proj_done))
                        tok_m = ctx.mm(bank, [otT[:, c * P:(c + 1) * P]], [wb],
                                       waits=(tok_e, tok_w), start=(c == 0),
                                       stop=(c == FC - 1))
                        wo_tok[(c % 3, half)] = tok_m
                    u_evac.append((half, bank))
                u_toks = []
                for half, bank in u_evac:
                    tok_u = ctx.evac("V", bank, lambda v, half=half, bank=bank:
                                     v.tensor_add(
                        xnt6[:, half * 512:(half + 1) * 512],
                        xnt6[:, half * 512:(half + 1) * 512],
                        ctx.pb[bank][:, 0:512]), waits=(pe_tok,))
                    u_toks.append(tok_u)
                tok_u2 = kb.op("V", lambda v: v.tensor_add(xnt6, xnt6, bob),
                               waits=tuple(u_toks) + (tok_bo,))

                def fn_bn(v):
                    v.bn_stats(stats[:, 8:14], xnt6[:, 0:512])
                    v.bn_aggr(stats[:, 20:22], stats[:, 8:14])
                    v.bn_stats(stats[:, 8:14], xnt6[:, 512:1024])
                    v.bn_aggr(stats[:, 22:24], stats[:, 8:14])
                    v.tensor_add(stats[:, 0:1], stats[:, 20:21], stats[:, 22:23])
                    v.tensor_scalar(stats[:, 0:1], stats[:, 0:1], 0.5, None,
                                    ALU.mult)
                    v.tensor_sub(stats[:, 6:7], stats[:, 20:21], stats[:, 22:23])
                    v.tensor_scalar(stats[:, 6:7], stats[:, 6:7], stats[:, 6:7],
                                    0.25, ALU.mult, ALU.mult)
                    v.tensor_add(stats[:, 1:2], stats[:, 21:22], stats[:, 23:24])
                    v.tensor_scalar(stats[:, 1:2], stats[:, 1:2], 0.5, None,
                                    ALU.mult)
                    return v.tensor_add(stats[:, 1:2], stats[:, 1:2], stats[:, 6:7])
                tok_v = kb.op("V", fn_bn, waits=(tok_u2,))

                def fn_rstd(s):
                    s.activation(stats[:, 2:3], stats[:, 1:2], AF.Ln, bias=s_eps,
                                 scale=1.0)
                    return s.activation(stats[:, 3:4], stats[:, 2:3], AF.Exp,
                                        bias=0.0, scale=-0.5)
                tok_rs = kb.op("S", fn_rstd, waits=(tok_v, tok_rsb))

                def fn_z(v, xap=xap):
                    v.tensor_scalar(xnt6, xnt6, stats[:, 0:1], stats[:, 3:4],
                                    ALU.subtract, ALU.mult)
                    v.tensor_mul(xnt6, xnt6, gamb)
                    v.tensor_add(xnt6, xnt6, betb)
                    return v.tensor_add(xap, xap, xnt6)
                return kb.op("V", fn_z, waits=(tok_rs, tok_gm2, tok_bt))
            return emit
        if qt == NT - 1:
            x_done = mk_tail()()
        else:
            qt_tail = mk_tail()

    ar.pop()
    return x_done



def emit_l1_ffn(ctx, x_sb, x_tok, wqkv_in, qs_b_in, ks_b_in, wo_in, lg_b_in,
                lb_b_in, w1_in, b1_b_in, w2_in, b2_b_in, rot_in, mask_in,
                xspill, entry=None, stop_after=None):
    kb, ar = ctx.kb, ctx.arena
    ar.push()
    stats0 = ar.alloc_f32(32)
    sq0 = ar.alloc_b16(D)
    tok_sp = kb.dma("Y", xspill.rearrange("(t p) f -> p t f", p=P),
                    x_sb.rearrange("p (t f) -> p t f", t=NT),
                    waits=(x_tok, entry))
    prev_s = None
    ln_toks = []
    for tt in range(NT):
        xap = x_sb[:, tt * D:(tt + 1) * D]

        def cb(mean, rstd, tok_s, xap=xap):
            tok_a = kb.op("V", lambda v, xap=xap: v.tensor_scalar(
                xap, xap, mean, rstd, ALU.subtract, ALU.mult),
                waits=(tok_s, tok_sp))
            ln_toks.append(tok_a)
        prev_s = emit_ln(ctx, stats0, xap, cb, waits=(x_tok, entry, tok_sp),
                         prev_s=prev_s, sq_scratch=sq0)
    ar.pop()
    done = ln_toks[-1]
    for layer in range(2):
        done = _l1f_layer(ctx, x_sb, done, layer, wqkv_in, qs_b_in, ks_b_in,
                          wo_in, lg_b_in, lb_b_in, w1_in, b1_b_in, w2_in,
                          b2_b_in, rot_in, mask_in, done, stop_after)
        if stop_after is not None:
            return done
    ar.push()
    xb = ar.alloc_f32(D)
    prev = None
    for tt in range(NT):
        tok_l = kb.dma("Y", xb, xspill[tt * P:(tt + 1) * P, :],
                       waits=(done, prev))
        prev = kb.op("V", lambda v, tt=tt: v.tensor_add(
            x_sb[:, tt * D:(tt + 1) * D], x_sb[:, tt * D:(tt + 1) * D], xb),
            waits=(tok_l, done))
    ar.pop()
    return prev


def _l1f_layer(ctx, x_sb, x_tok, layer, wqkv_in, qs_b_in, ks_b_in, wo_in,
               lg_b_in, lb_b_in, w1_in, b1_b_in, w2_in, b2_b_in, rot_in,
               mask_in, entry, stop_after):
    kb, ar = ctx.kb, ctx.arena
    NHH, DKH, INNER, HID, CH = 8, 64, 512, 2730, 455
    ar.push()
    qkv = ar.alloc_b16(NT * 1536)
    xTr = ar.alloc_b16(NT * D)
    scr = ar.alloc_b16(16 * 1024)
    sh = ar.alloc_f32(D)
    shb = ar.alloc_b16(D)
    stats = ar.alloc_f32(64)
    qsb = ar.alloc_f32(DKH)
    ksb = ar.alloc_f32(DKH)
    scr2 = ar.alloc_b16(2048)
    qkv3 = qkv.rearrange("p (t f) -> p t f", t=NT)
    xT3 = xTr.rearrange("p (c s) -> p c s", c=FC)
    qrT = xTr[:, 0:4 * SEQ]
    krT = xTr[:, 4 * SEQ:8 * SEQ]
    qrT3 = qrT.rearrange("p (c s) -> p c s", c=4)
    krT3 = krT.rearrange("p (c s) -> p c s", c=4)
    wqkv_sb = scr[:, 0:FC * 1536]
    sqs = scr[:, 0:1024].bitcast(F32)
    qh = scr[:, 1024:2048].bitcast(F32)
    qr_t = scr[:, 2048:3072].bitcast(F32)
    rtile = scr[:, 3072:3584].bitcast(F32)
    norms = scr[:, 3584:3600].bitcast(F32)
    qrb = scr[:, 4096:4608]
    mask_buf = [scr[:, 4096:6144].bitcast(F32), scr[:, 6144:8192].bitcast(F32)]
    sf32_2 = [scr[:, 8192:10240].bitcast(F32), scr2.bitcast(F32)]
    ptile2_c = [scr[:, 10240:11264], scr[:, 13312:14336]]
    ptT2_c = [scr[:, 11264:12288], scr[:, 14336:15360]]
    otile2_c = [scr[:, 12288:12800], scr[:, 15360:15872]]
    otT2_c = [scr[:, 12800:13312], scr[:, 15872:16384]]
    wo_sb = scr[:, 0:4 * 1024]
    wo4 = wo_sb.rearrange("p (c f) -> p c f", c=4)

    W = 512

    sh_toks = []
    tok_wq = kb.dma("G", wqkv_sb.rearrange("p (c f) -> p c f", c=FC),
                    wqkv_in[layer].rearrange("(c p) f -> p c f", p=P),
                    waits=(entry,))
    tok_qs = kb.dma("Y", qsb, qs_b_in[layer], waits=(entry,))
    tok_ks = kb.dma("Y", ksb, ks_b_in[layer], waits=(entry,))
    qkv_toks = []
    prev_sh = [None, None]
    for tt in range(NT):
        tok_c1 = kb.op("V", lambda v, tt=tt: v.tensor_copy(
            sh[:, 0:W], x_sb[:, tt * D:tt * D + W]),
            waits=(x_tok, entry, prev_sh[0]))
        if tt == 0:
            tok_c2 = kb.op("V", lambda v: v.memset(sh[0:1, W:D], 0.0),
                           waits=(prev_sh[1],))
            tok_c3 = kb.dma("G", sh[1:P, W:D], x_sb[0:P - 1, W:D],
                            waits=(x_tok, entry, tok_c2))
        else:
            tok_c2 = kb.dma("G", sh[0:1, W:D],
                            x_sb[P - 1:P, (tt - 1) * D + W:(tt - 1) * D + D],
                            waits=(x_tok, prev_sh[1]))
            tok_c3 = kb.dma("G", sh[1:P, W:D],
                            x_sb[0:P - 1, tt * D + W:tt * D + D],
                            waits=(x_tok, tok_c2))
        tok_cast = kb.op("V", lambda v: v.tensor_copy(shb, sh),
                         waits=(tok_c1, tok_c3))
        prev_sh[0] = tok_cast
        prev_sh[1] = tok_cast
        bank = 6 + tt % 2
        pairs = [(c * P, shb[:, c * P:(c + 1) * P]) for c in range(FC)]
        tok_t = ctx.transposes(bank, pairs, waits=(tok_cast, ctx.c["load_tok"]))
        tok_e = ctx.evac("V", bank, lambda v, tt=tt, bank=bank: v.tensor_copy(
            xT3[:, :, tt * P:(tt + 1) * P],
            ctx.pbb[bank][:, 0:FC * P].rearrange("p (c s) -> p c s", s=P)))
        sh_toks.append(tok_e)
        for ch in range(3):
            bank = (tt % 2) * 3 + ch
            stat = [xT3[:, c, tt * P:(tt + 1) * P] for c in range(FC)]
            mov = [wqkv_sb.rearrange("p (c f) -> p c f", c=FC)[
                :, c, ch * 512:(ch + 1) * 512] for c in range(FC)]
            tok_m = ctx.mm(bank, stat, mov, waits=(tok_e, tok_wq))
            if ch % 2 == 0:
                tok_q = ctx.evac("V", bank, lambda v, tt=tt, ch=ch, bank=bank:
                                 v.tensor_copy(qkv3[:, tt, ch * 512:(ch + 1) * 512],
                                               ctx.pb[bank][:, 0:512]))
            else:
                tok_q = ctx.evac("S", bank, lambda s_, tt=tt, ch=ch, bank=bank:
                                 s_.copy(qkv3[:, tt, ch * 512:(ch + 1) * 512],
                                         ctx.pb[bank][:, 0:512]))
            qkv_toks.append(tok_q)
    qkv_done = qkv_toks[-1]
    if stop_after == "qkv":
        ctx.dbg = dict(qkv=qkv)
        ar.pop()
        return qkv_done

    rot_toks = []
    prev_r = None
    for tt in range(NT):
        tok_rt = kb.dma("Y", rtile.rearrange("p (r k) -> p r k", r=4),
                        rot_in.rearrange("r (t p) k -> p t r k", p=P)[:, tt, :, :],
                        waits=(prev_r, qkv_done))
        for qk in range(2):
            base = qk * 512
            srcap = qkv3[:, tt, base:base + 512]
            tok_sq = kb.op("S", lambda s, srcap=srcap: s.activation(
                sqs, srcap, AF.Square, bias=0.0, scale=1.0),
                waits=(qkv_done, prev_r))

            def fn_ss(v, tt=tt):
                last = None
                for h in range(NHH):
                    last = v.reduce_sum(stats[:, 16 + h:17 + h],
                                        sqs[:, h * DKH:(h + 1) * DKH], axis=AX.X)
                last = v.tensor_scalar(stats[:, 16:24], stats[:, 16:24], 1e-24,
                                       None, ALU.max)
                return last
            tok_ss = kb.op("V", fn_ss, waits=(tok_sq,))

            def fn_rn(s):
                s.activation(stats[:, 24:32], stats[:, 16:24], AF.Ln, bias=0.0,
                             scale=1.0)
                return s.activation(stats[:, 24:32], stats[:, 24:32], AF.Exp,
                                    bias=0.0, scale=-0.5)
            tok_rn = kb.op("S", fn_rn, waits=(tok_ss,))
            def fn_qh(v, srcap=srcap, qk=qk):
                last = None
                for h in range(NHH):
                    last = v.tensor_scalar(qh[:, h * DKH:(h + 1) * DKH],
                                           srcap[:, h * DKH:(h + 1) * DKH],
                                           stats[:, 24 + h:25 + h], None, ALU.mult)
                return last
            tok_qh = kb.op("V", fn_qh, waits=(tok_rn,))
            scb = qsb if qk == 0 else ksb
            sc_tok = tok_qs if qk == 0 else tok_ks
            qh3 = qh.rearrange("p (h k) -> p h k", h=NHH)
            qr3 = qr_t.rearrange("p (h k) -> p h k", h=NHH)
            cs = rtile[:, qk * 128:qk * 128 + 64]
            ss = rtile[:, qk * 128 + 64:qk * 128 + 128]

            def fn_rot2(v, qh3=qh3, qr3=qr3, cs=cs, ss=ss, scb=scb):
                scb3 = scb.rearrange("p (o k) -> p o k", o=1).broadcast_to(
                    (P, NHH, DKH))
                cs3 = cs.rearrange("p (o k) -> p o k", o=1).broadcast_to(
                    (P, NHH, DKH))
                ss3 = ss.rearrange("p (o k) -> p o k", o=1).broadcast_to(
                    (P, NHH, DKH))
                v.tensor_mul(qh3, qh3, scb3)
                v.tensor_mul(qr3, qh3, cs3)
                v.tensor_mul(sqs.rearrange("p (h k) -> p h k", h=NHH)[:, :, 0:32],
                             qh3[:, :, 32:64], ss3[:, :, 0:32])
                v.tensor_mul(sqs.rearrange("p (h k) -> p h k", h=NHH)[:, :, 32:64],
                             qh3[:, :, 0:32], ss3[:, :, 32:64])
                v.tensor_sub(qr3[:, :, 0:32], qr3[:, :, 0:32],
                             sqs.rearrange("p (h k) -> p h k", h=NHH)[:, :, 0:32])
                return v.tensor_add(
                    qr3[:, :, 32:64], qr3[:, :, 32:64],
                    sqs.rearrange("p (h k) -> p h k", h=NHH)[:, :, 32:64])
            tok_rot = kb.op("V", fn_rot2, waits=(tok_qh, tok_rt, sc_tok))
            tok_cb = kb.op("V", lambda v: v.tensor_copy(qrb, qr_t),
                           waits=(tok_rot,))
            bank = 6 + qk
            pairs = [(c * P, qrb[:, c * P:(c + 1) * P]) for c in range(4)]
            tok_t = ctx.transposes(bank, pairs, waits=(tok_cb,))
            dstT = qrT3 if qk == 0 else krT3
            tok_e = ctx.evac("V", bank, lambda v, tt=tt, bank=bank, dstT=dstT:
                             v.tensor_copy(
                dstT[:, :, tt * P:(tt + 1) * P],
                ctx.pbb[bank][:, 0:4 * P].rearrange("p (c s) -> p c s", s=P)))
            rot_toks.append(tok_e)
            prev_r = tok_e
    rot_done = rot_toks[-1]
    if stop_after == "rot":
        ctx.dbg = dict(qrT=qrT, krT=krT, qkv=qkv)
        ar.pop()
        return rot_done

    tok_wo = kb.dma("G", wo4, wo_in[layer].rearrange("(c p) f -> p c f", p=P),
                    waits=(rot_done,))
    QKS = 8.0
    x_done = None
    m_toks = {}
    war_sf = [None, None]
    war_pt = [None, None]
    qt_tail = None

    def tail_c(hh_state):
        h, par, ex_toks, tok_r, otile_p, w0_, kb0_ = hh_state
        ptile_p, ptT_p = ptile2_c[par], ptT2_c[par]
        tb, ab = 4 + 2 * par, 5 + 2 * par
        pairs = [(i * P, ptile_p[:, i * P:(i + 1) * P]) for i in range(8)]
        tok_t = ctx.transposes(tb, pairs, waits=tuple(ex_toks))
        war_pt[par] = tok_t
        tok_pte = ctx.evac("V", tb, lambda v, tb=tb, ptT_p=ptT_p: v.tensor_copy(
            ptT_p, ctx.pbb[tb][:, 0:8 * P]))
        cstart = 0 if w0_ > 0 else 4
        stat = [ptT_p[:, i * P:(i + 1) * P] for i in range(cstart, 8)]
        mov = [qkv3[:, kb0_ // P + i, 1024 + h * DKH:1024 + (h + 1) * DKH]
               for i in range(cstart, 8)]
        ctx.mm(ab, stat, mov, waits=(tok_pte, qkv_done), n=DKH)
        return ctx.evac("V", ab, lambda v, h=h, ab=ab, par=par, otile_p=otile_p:
                        v.tensor_scalar(
            otile_p[:, h * DKH:(h + 1) * DKH], ctx.pb[ab][:, 0:DKH],
            stats[:, 60 + par:61 + par], None, ALU.mult), waits=(tok_r,))

    for qt in range(NT):
        w0 = qt // 4
        kb0 = (w0 - 1) * 512
        valid_g = (0, 1) if w0 > 0 else (1,)
        mb_m = mask_buf[qt % 2]
        tok_mk = kb.dma("Y", mb_m, mask_in[qt], waits=(m_toks.get(qt % 2), rot_done))
        hstate = None
        last_s = None
        for h in range(NHH):
            hp, hh = h // 2, h % 2
            par = h % 2
            sb = 2 * par
            cb = 16 + 12 * par
            sf = sf32_2[par]
            ptile_p = ptile2_c[par]
            ms_toks = []
            for g in valid_g:
                bank = sb + g
                stat = [qrT3[hh * 64:(hh + 1) * 64, hp, qt * P:(qt + 1) * P]]
                kbase = kb0 + g * 512
                mov = [krT3[hh * 64:(hh + 1) * 64, hp, kbase:kbase + 512]]
                tok_m = ctx.mm(bank, stat, mov, waits=(rot_done,))
                tok_s = ctx.evac("V", bank, lambda v, g=g, bank=bank, mb_m=mb_m,
                                 sf=sf: v.tensor_add(
                    sf[:, g * 512:(g + 1) * 512], ctx.pb[bank][:, 0:512],
                    mb_m[:, g * 512:(g + 1) * 512]),
                    waits=(tok_mk, war_sf[par]))
                ms_toks.append(tok_s)
                last_s = tok_s
            ex_toks = []
            if w0 == 0:
                ex_toks.append(kb.op("S", lambda s, ptile_p=ptile_p: s.memzero(
                    ptile_p[:, 0:512]), waits=(war_pt[par],)))
            for g in valid_g:
                tok_e = kb.op("S", lambda s, g=g, sf=sf, cb=cb, ptile_p=ptile_p:
                              s.activation(
                    ptile_p[:, g * 512:(g + 1) * 512],
                    sf[:, g * 512:(g + 1) * 512],
                    AF.Exp, bias=0.0, scale=QKS,
                    accum_out=stats[:, cb + 6 + g:cb + 7 + g]),
                    waits=tuple(ms_toks) + (war_pt[par],))
                ex_toks.append(tok_e)
            war_sf[par] = ex_toks[-1]
            if w0 > 0:
                tok_r = kb.op("V", lambda v, cb=cb: v.tensor_add(
                    stats[:, cb + 10:cb + 11], stats[:, cb + 6:cb + 7],
                    stats[:, cb + 7:cb + 8]), waits=tuple(ex_toks))
            else:
                tok_r = kb.op("V", lambda v, cb=cb: v.tensor_copy(
                    stats[:, cb + 10:cb + 11], stats[:, cb + 7:cb + 8]),
                    waits=tuple(ex_toks))
            tok_r = kb.op("V", lambda v, cb=cb, par=par: v.reciprocal(
                stats[:, 60 + par:61 + par], stats[:, cb + 10:cb + 11]),
                waits=(tok_r,))
            if h == 2 and qt_tail is not None:
                x_done = qt_tail()
                qt_tail = None
            if hstate is not None:
                tail_c(hstate)
            hstate = (h, par, ex_toks, tok_r, otile2_c[qt % 2], w0, kb0)
        tok_o = tail_c(hstate)
        m_toks[qt % 2] = last_s

        def mk_tail(qt=qt, tok_o=tok_o):
            otile_p, otT_p = otile2_c[qt % 2], otT2_c[qt % 2]

            def emit():
                pairs = [(c * P, otile_p[:, c * P:(c + 1) * P]) for c in range(4)]
                ctx.transposes(6, pairs, waits=(tok_o,))
                tok_e = ctx.evac("V", 6, lambda v: v.tensor_copy(
                    otT_p, ctx.pbb[6][:, 0:4 * P]))
                xd = None
                for half in range(2):
                    bank = 5 if half == 0 else 7
                    stat = [otT_p[:, c * P:(c + 1) * P] for c in range(4)]
                    mov = [wo4[:, c, half * 512:(half + 1) * 512] for c in range(4)]
                    ctx.mm(bank, stat, mov, waits=(tok_e, tok_wo))
                    xap = x_sb[:, qt * D + half * 512:qt * D + (half + 1) * 512]
                    xd = ctx.evac("V", bank, lambda v, xap=xap, bank=bank:
                                  v.tensor_add(xap, xap, ctx.pb[bank][:, 0:512]))
                return xd
            return emit
        if qt == NT - 1:
            x_done = mk_tail()()
        else:
            qt_tail = mk_tail()
    attn_done = x_done
    if stop_after == "attn":
        ar.pop()
        return attn_done

    ar.pop()
    ar.push()
    xnT = ar.alloc_b16(NT * D)
    b1b = ar.alloc_f32(2 * HID)
    lgb = ar.alloc_f32(D)
    lbb = ar.alloc_f32(D)
    b2b = ar.alloc_f32(D)
    w1buf = ar.alloc_b16(FC * 2 * CH)
    w2buf = ar.alloc_b16(4 * D)
    Aa = ar.alloc_f32(CH)
    Bg = ar.alloc_f32(CH)
    Cg = ar.alloc_f32(CH)
    h4 = ar.alloc_b16(CH)
    h4T = ar.alloc_b16(4 * P)
    sh2 = ar.alloc_f32(D)
    sh2b = ar.alloc_b16(D)
    stats2 = ar.alloc_f32(32)
    xnT3 = xnT.rearrange("p (c s) -> p c s", c=FC)
    w1b3 = w1buf.rearrange("p (c f) -> p c f", c=FC)

    tok_b1 = kb.dma("Y", b1b, b1_b_in[layer], waits=(attn_done,))
    tok_lg = kb.dma("Y", lgb, lg_b_in[layer], waits=(attn_done,))
    tok_lb = kb.dma("Y", lbb, lb_b_in[layer], waits=(attn_done,))
    tok_b2 = kb.dma("Y", b2b, b2_b_in[layer], waits=(attn_done,))

    xn_toks = []
    prev = [None, None]
    prev_s = None
    for tt in range(NT):
        tok_c1 = kb.op("V", lambda v, tt=tt: v.tensor_copy(
            sh2[:, 0:512], x_sb[:, tt * D:tt * D + 512]),
            waits=(attn_done, prev[0]))
        if tt == 0:
            tok_c2 = kb.op("V", lambda v: v.memset(sh2[0:1, 512:D], 0.0),
                           waits=(prev[1],))
            tok_c3 = kb.dma("G", sh2[1:P, 512:D], x_sb[0:P - 1, 512:D],
                            waits=(attn_done, tok_c2))
        else:
            tok_c2 = kb.dma("G", sh2[0:1, 512:D],
                            x_sb[P - 1:P, (tt - 1) * D + 512:(tt - 1) * D + D],
                            waits=(attn_done, prev[1]))
            tok_c3 = kb.dma("G", sh2[1:P, 512:D],
                            x_sb[0:P - 1, tt * D + 512:tt * D + D],
                            waits=(attn_done, tok_c2))
        sh_ready = (tok_c1, tok_c3)

        def cb(mean, rstd, tok_s, tt=tt):
            def fn_ap(v):
                v.tensor_scalar(sh2, sh2, mean, rstd, ALU.subtract, ALU.mult)
                v.tensor_mul(sh2, sh2, lgb)
                v.tensor_add(sh2, sh2, lbb)
                return v.tensor_copy(sh2b, sh2)
            tok_a = kb.op("V", fn_ap, waits=(tok_s, tok_lg, tok_lb))
            bank = 6 + tt % 2
            pairs = [(c * P, sh2b[:, c * P:(c + 1) * P]) for c in range(FC)]
            tok_t = ctx.transposes(bank, pairs, waits=(tok_a,))
            tok_e = ctx.evac("V", bank, lambda v, tt=tt, bank=bank: v.tensor_copy(
                xnT3[:, :, tt * P:(tt + 1) * P],
                ctx.pbb[bank][:, 0:FC * P].rearrange("p (c s) -> p c s", s=P)))
            xn_toks.append(tok_e)
            prev[0] = tok_e
            prev[1] = tok_e
        prev_s = emit_ln(ctx, stats2, sh2, cb, waits=sh_ready, prev_s=prev_s,
                         sq_scratch=sh2b)
    xn_done = xn_toks[-1]

    w1_tok = None
    w2_tok = None
    x_done2 = xn_done
    prev_h4 = None
    prev_ht = None
    for j in range(6):
        tok_w1a = kb.dma("G",
                         w1b3[:, :, 0:CH],
                         w1_in[layer, :, j * CH:(j + 1) * CH].rearrange(
                             "(c p) f -> p c f", p=P), waits=(w1_tok, xn_done))
        tok_w1g = kb.dma("G",
                         w1b3[:, :, CH:2 * CH],
                         w1_in[layer, :, HID + j * CH:HID + (j + 1) * CH].rearrange(
                             "(c p) f -> p c f", p=P), waits=(w1_tok, xn_done))
        rb_sizes = (P, P, P, CH - 3 * P)
        roff = 0
        w2aps = []
        w2l_toks = []
        for bi, rsz in enumerate(rb_sizes):
            w2l_toks.append(kb.dma("G", w2buf.rearrange("p (b f) -> p b f", b=4)[
                0:rsz, bi, :],
                w2_in[layer, j * CH + roff:j * CH + roff + rsz, :],
                waits=(w2_tok, xn_done)))
            w2aps.append((rsz, w2buf.rearrange("p (b f) -> p b f", b=4)[0:rsz, bi, :]))
            roff += rsz
        for tt in range(NT):
            stat = [xnT3[:, c, tt * P:(tt + 1) * P] for c in range(FC)]
            movA = [w1b3[:, c, 0:CH] for c in range(FC)]
            movG = [w1b3[:, c, CH:2 * CH] for c in range(FC)]
            tok_mA = ctx.mm(0, stat, movA, waits=(tok_w1a, xn_done))
            tok_mG = ctx.mm(1, stat, movG, waits=(tok_w1g, xn_done))
            tok_a = ctx.evac("V", 0, lambda v, j=j: v.tensor_add(
                Aa, ctx.pb[0][:, 0:CH],
                b1b[:, j * CH:(j + 1) * CH]), waits=(tok_b1, prev_h4))
            tok_g = ctx.evac("V", 1, lambda v, j=j: v.tensor_add(
                Bg, ctx.pb[1][:, 0:CH],
                b1b[:, HID + j * CH:HID + (j + 1) * CH]), waits=(tok_b1, prev_h4))
            tok_gl = kb.op("S", lambda s: s.activation(
                Cg, Bg, AF.Gelu, bias=0.0, scale=1.0), waits=(tok_g,))
            tok_h4 = kb.op("G", lambda g: g.tensor_mul(h4, Aa, Cg),
                           waits=(tok_gl, tok_a, prev_ht))
            prev_h4 = tok_h4
            pairs = []
            roff = 0
            for bi, rsz in enumerate(rb_sizes):
                pairs.append((bi * P, h4[:, roff:roff + rsz]))
                roff += rsz
            tok_t = ctx.transposes(2, pairs, waits=(tok_h4,))
            prev_ht = tok_t

            def fn_he(s_, rb_sizes=rb_sizes):
                last = None
                for bi, rsz in enumerate(rb_sizes):
                    last = s_.copy(h4T[0:rsz, bi * P:bi * P + P],
                                   ctx.pbb[2][0:rsz, bi * P:bi * P + P])
                return last
            tok_he = ctx.evac("S", 2, fn_he)
            for half in range(2):
                bank = 3 + (tt % 2) * 2 + half
                stat2 = []
                mov2 = []
                for bi, rsz in enumerate(rb_sizes):
                    stat2.append(h4T[0:rsz, bi * P:bi * P + P])
                    mov2.append(w2aps[bi][1][:, half * 512:(half + 1) * 512])
                tok_m2 = ctx.mm(bank, stat2, mov2,
                                waits=(tok_he,) + tuple(w2l_toks))
                xap = x_sb[:, tt * D + half * 512:tt * D + (half + 1) * 512]
                x_done2 = ctx.evac("V", bank, lambda v, xap=xap, bank=bank:
                                   v.tensor_add(xap, xap, ctx.pb[bank][:, 0:512]),
                                   waits=(x_done2,))
            w1_tok = tok_mG if tt == NT - 1 else w1_tok
            w2_tok = tok_m2 if tt == NT - 1 else w2_tok
    fin = x_done2
    for tt in range(NT):
        fin = kb.op("V", lambda v, tt=tt: v.tensor_add(
            x_sb[:, tt * D:(tt + 1) * D], x_sb[:, tt * D:(tt + 1) * D], b2b),
            waits=(fin, tok_b2))
    ar.pop()
    return fin




def build_program(n_batch=NB_PER_CORE, blocks=("l0a",), heap_kib=None, dbg_dump=(), stop_after=None):
    nc = bass.Bass("TRN2", target_bir_lowering=False)
    kb = KB(nc)
    ar = Arena(nc, "heap", heap_kib)
    pbig = nc.alloc_psum_tensor("pbig", [P, 4096], F32).ap()
    pb = [pbig[:, i * 512:(i + 1) * 512] for i in range(8)]

    x_in = nc.dram_tensor("x", [n_batch, SEQ, D], F32, kind="ExternalInput").ap()
    y_out = nc.dram_tensor("y", [n_batch, SEQ, D], F32, kind="ExternalOutput").ap()
    ident_in = nc.dram_tensor("cst_ident", [P, P], F32, kind="ExternalInput").ap()

    wq_in = wo_in = None
    if "l0a" in blocks:
        wq_in = nc.dram_tensor("L0_attn_w_qkv", [16, 3, D, 64], F32,
                               kind="ExternalInput").ap()
        wo_in = nc.dram_tensor("L0_attn_w_o", [D, D], F32, kind="ExternalInput").ap()
    if "l1a" in blocks:
        wqT_in = nc.dram_tensor("L1_WqT", [D, D], F32, kind="ExternalInput").ap()
        wkT_in = nc.dram_tensor("L1_WkT", [D, D], F32, kind="ExternalInput").ap()
        wvT_in = nc.dram_tensor("L1_WvT", [D, D], F32, kind="ExternalInput").ap()
        woT_in = nc.dram_tensor("L1_WoT", [D, D], F32, kind="ExternalInput").ap()
        bqcm_in = nc.dram_tensor("L1_bq_cm", [P, 8], F32, kind="ExternalInput").ap()
        bkcm_in = nc.dram_tensor("L1_bk_cm", [P, 8], F32, kind="ExternalInput").ap()
        bvb_in = nc.dram_tensor("L1_bv_b", [P, D], F32, kind="ExternalInput").ap()
        bob_in = nc.dram_tensor("L1_bo_b", [P, D], F32, kind="ExternalInput").ap()
        gamb_in = nc.dram_tensor("L1_gam_b", [P, D], F32, kind="ExternalInput").ap()
        betb_in = nc.dram_tensor("L1_bet_b", [P, D], F32, kind="ExternalInput").ap()
        rby_in = nc.dram_tensor("L1_attn_rby", [3], F32, kind="ExternalInput").ap()
        ang0_in = nc.dram_tensor("cst_ang0", [SEQ, 512], F32,
                                 kind="ExternalInput").ap().rearrange(
            "(t p) i -> p t i", p=P)
    if "l1f" in blocks:
        wqkv_in = nc.dram_tensor("L1_ffn_wqkv", [2, D, 1536], F32,
                                 kind="ExternalInput").ap()
        qsb_in = nc.dram_tensor("L1f_qs_b", [2, P, 64], F32,
                                kind="ExternalInput").ap()
        ksb_in = nc.dram_tensor("L1f_ks_b", [2, P, 64], F32,
                                kind="ExternalInput").ap()
        wol_in = nc.dram_tensor("L1_ffn_wo", [2, 512, D], F32,
                                kind="ExternalInput").ap()
        lgb_in = nc.dram_tensor("L1f_lg_b", [2, P, D], F32,
                                kind="ExternalInput").ap()
        lbb_in = nc.dram_tensor("L1f_lb_b", [2, P, D], F32,
                                kind="ExternalInput").ap()
        w1l_in = nc.dram_tensor("L1_ffn_w1", [2, D, 5460], F32,
                                kind="ExternalInput").ap()
        b1b_in = nc.dram_tensor("L1f_b1_b", [2, P, 5460], F32,
                                kind="ExternalInput").ap()
        w2l_in = nc.dram_tensor("L1_ffn_w2", [2, 2730, D], F32,
                                kind="ExternalInput").ap()
        b2b_in = nc.dram_tensor("L1f_b2_b", [2, P, D], F32,
                                kind="ExternalInput").ap()
        rot_in = nc.dram_tensor("cst_rot", [4, SEQ, 64], F32,
                                kind="ExternalInput").ap()
        mask_in = nc.dram_tensor("cst_mask", [NT, P, 1024], F32,
                                 kind="ExternalInput").ap()
        xspill = nc.dram_tensor("x_spill", [SEQ, D], F32, kind="Internal").ap()
    if "l0f" in blocks:
        rw_in = nc.dram_tensor("L0_ffn_router_weight", [8, D], F32,
                               kind="ExternalInput").ap()
        rb_in = nc.dram_tensor("L0_ffn_router_bias", [8], F32,
                               kind="ExternalInput").ap()
        gup_in = nc.dram_tensor("L0_ffn_gate_up_proj", [8, D, 4096], F32,
                                kind="ExternalInput").ap()
        gub_in = nc.dram_tensor("L0_ffn_gate_up_bias", [8, 4096], F32,
                                kind="ExternalInput").ap()
        dp_in = nc.dram_tensor("L0_ffn_down_proj", [8, 2048, D], F32,
                               kind="ExternalInput").ap()
        db_in = nc.dram_tensor("L0_ffn_down_bias", [8, D], F32,
                               kind="ExternalInput").ap()

    consts = {}
    ident_sb = ar.alloc_b16(P)
    consts["ident_b16"] = ident_sb
    t1 = kb.dma("G", ident_sb, ident_in)
    ident_f32 = ar.alloc_f32(P)
    consts["ident_f32"] = ident_f32
    t2 = kb.dma("Y", ident_f32, ident_in)
    cbuf = ar.alloc_f32(8)
    consts["c_eps"] = cbuf[:, 0:1]
    onescol = ar.alloc_f32(1)
    onesrow = ar.alloc_f32(P)
    consts["ones_col"] = onescol
    consts["ones_row"] = onesrow[0:1, 0:P]

    consts["c_pi2"] = cbuf[:, 1:2]

    def fn_c(v):
        v.memset(cbuf[:, 0:1], 1e-5)
        v.memset(cbuf[:, 1:2], PI2)
        v.memset(onescol, 1.0)
        return v.memset(onesrow[0:1, 0:P], 1.0)
    tok_eps = kb.op("V", fn_c, waits=(t1, t2))
    consts["load_tok"] = tok_eps

    x_sb = ar.alloc_f32(NT * D)
    ctx = Ctx(kb, ar, pb, consts, pbig=pbig)

    store_toks = []
    done = tok_eps
    for b in range(n_batch):
        x_tok = kb.dma("Y", x_sb.rearrange("p (t f) -> p t f", t=NT),
                       x_in[b].rearrange("(t p) f -> p t f", p=P),
                       waits=(store_toks[-1] if store_toks else None, done))
        entry = done
        for blk in blocks:
            if blk == "l0a":
                done = emit_l0_attn(ctx, x_sb, x_tok, wq_in, wo_in, entry=entry,
                                    stop_after=stop_after)
            elif blk == "l0f":
                done = emit_l0_ffn(ctx, x_sb, x_tok, rw_in, rb_in, gup_in, gub_in,
                                   dp_in, db_in, entry=entry)
            elif blk == "l1a":
                done = emit_l1_attn(ctx, x_sb, x_tok, wqT_in, wkT_in, wvT_in,
                                    woT_in, bqcm_in, bkcm_in, bvb_in, bob_in,
                                    gamb_in, betb_in, rby_in, ang0_in, entry=entry,
                                    stop_after=stop_after)
            elif blk == "l1f":
                done = emit_l1_ffn(ctx, x_sb, x_tok, wqkv_in, qsb_in, ksb_in,
                                   wol_in, lgb_in, lbb_in, w1l_in, b1b_in,
                                   w2l_in, b2b_in, rot_in, mask_in, xspill,
                                   entry=entry, stop_after=stop_after)
            build_program.last_ctx = ctx
            x_tok = done
            entry = done
        st = kb.dma("Y", y_out[b].rearrange("(t p) f -> p t f", p=P),
                    x_sb.rearrange("p (t f) -> p t f", t=NT), waits=(done,))
        store_toks.append(st)

    dbg_aps = getattr(build_program, "last_ctx", None)
    for nm in dbg_dump:
        ap = dbg_aps.dbg[nm]
        dt = nc.dram_tensor("dbg_" + nm, [P, ap.shape[-1]],
                            BF16 if str(ap.dtype) == str(BF16) else F32,
                            kind="ExternalOutput").ap()
        store_toks.append(kb.dma("Y", dt, ap, waits=(done,)))

    kb.op("Y", lambda e: None, waits=(store_toks[-1],), inc=0)
    kb.emit()
    return nc


def kernel(**inputs):
    from concourse.bass_utils import run_bass_kernel_spmd
    x = inputs["L0_attn_inputs"]
    shared = make_shared_inmap(inputs)
    nc = build_program(n_batch=2, blocks=("l0a", "l0f", "l1a", "l1f"))
    in_maps = [dict(shared, x=np.ascontiguousarray(x[2 * c:2 * c + 2]))
               for c in range(8)]
    res = run_bass_kernel_spmd(nc, in_maps, core_ids=list(range(8)))
    return np.concatenate([r["y"] for r in res.results], axis=0)


def make_shared_inmap(inputs):
    f32 = np.float32
    rep = lambda a, shape: np.ascontiguousarray(np.broadcast_to(a, shape))
    cm = lambda b: np.ascontiguousarray(b.reshape(8, 128).T)
    ident = np.eye(128, dtype=f32)
    denom0 = np.power(10000.0, 2.0 * np.arange(0, 1024, 2, dtype=f32) / 1024.0
                      ).astype(f32)
    ang0 = (np.arange(2048, dtype=f32)[:, None] / denom0[None, :]).astype(f32)
    DIM_HEAD, WINDOW = 64, 512
    half_idx = np.arange(0, DIM_HEAD, 2, dtype=f32)
    inv_freq = (1.0 / 10000.0 ** (half_idx / DIM_HEAD)).astype(f32)
    t = np.arange(2048, dtype=f32)
    freqs = t[:, None] * inv_freq[None, :]
    freqs = np.concatenate([freqs, freqs], axis=-1)
    base = ((half_idx + 0.4 * DIM_HEAD) / (1.4 * DIM_HEAD)).astype(f32)
    power = (t - 1024) / float(WINDOW // 2)
    scale = (base[None, :] ** power[:, None]).astype(f32)
    scale = np.concatenate([scale, scale], axis=-1)
    cosf, sinf = np.cos(freqs), np.sin(freqs)
    rot = np.stack([cosf * scale, sinf * scale, cosf / scale, sinf / scale]
                   ).astype(f32)
    masks = np.zeros((16, 128, 1024), f32)
    for qt in range(16):
        i = qt * 128 + np.arange(128)[:, None]
        jg = (qt // 4 - 1) * 512 + np.arange(1024)[None, :]
        valid = (jg >= 0) & (jg <= i) & (i - jg <= 512)
        masks[qt] = np.where(valid, 0.0, -1e30).astype(f32)
    ii = inputs
    return {
        "L0_attn_w_qkv": ii["L0_attn_w_qkv"],
        "L0_attn_w_o": ii["L0_attn_w_o"],
        "L0_ffn_router_weight": ii["L0_ffn_router_weight"],
        "L0_ffn_router_bias": ii["L0_ffn_router_bias"],
        "L0_ffn_gate_up_proj": ii["L0_ffn_gate_up_proj"],
        "L0_ffn_gate_up_bias": ii["L0_ffn_gate_up_bias"],
        "L0_ffn_down_proj": ii["L0_ffn_down_proj"],
        "L0_ffn_down_bias": ii["L0_ffn_down_bias"],
        "L1_WqT": np.ascontiguousarray(ii["L1_attn_Wq"].T),
        "L1_WkT": np.ascontiguousarray(ii["L1_attn_Wk"].T),
        "L1_WvT": np.ascontiguousarray(ii["L1_attn_Wv"].T),
        "L1_WoT": np.ascontiguousarray(ii["L1_attn_Wo"].T),
        "L1_bq_cm": cm(ii["L1_attn_bq"]),
        "L1_bk_cm": cm(ii["L1_attn_bk"]),
        "L1_bv_b": rep(ii["L1_attn_bv"], (128, 1024)),
        "L1_bo_b": rep(ii["L1_attn_bo"], (128, 1024)),
        "L1_gam_b": rep(ii["L1_attn_ln_gamma"], (128, 1024)),
        "L1_bet_b": rep(ii["L1_attn_ln_beta"], (128, 1024)),
        "L1_attn_rby": ii["L1_attn_rby"],
        "cst_ang0": ang0,
        "L1_ffn_wqkv": ii["L1_ffn_wqkv"],
        "L1f_qs_b": rep(ii["L1_ffn_q_scale"][:, None, :], (2, 128, 64)),
        "L1f_ks_b": rep(ii["L1_ffn_k_scale"][:, None, :], (2, 128, 64)),
        "L1_ffn_wo": ii["L1_ffn_wo"],
        "L1f_lg_b": rep(ii["L1_ffn_ln_g"][:, None, :], (2, 128, 1024)),
        "L1f_lb_b": rep(ii["L1_ffn_ln_b"][:, None, :], (2, 128, 1024)),
        "L1_ffn_w1": ii["L1_ffn_w1"],
        "L1f_b1_b": rep(ii["L1_ffn_b1"][:, None, :], (2, 128, 5460)),
        "L1_ffn_w2": ii["L1_ffn_w2"],
        "L1f_b2_b": rep(ii["L1_ffn_b2"][:, None, :], (2, 128, 1024)),
        "cst_rot": rot,
        "cst_mask": masks,
        "cst_ident": ident,
    }


# version 14
# speedup vs baseline: 1.0246x; 1.0246x over previous
import numpy as np
import concourse.bass as bass
import concourse.mybir as mybir

F32 = mybir.dt.float32
BF16 = mybir.dt.bfloat16
AF = mybir.ActivationFunctionType
ALU = mybir.AluOpType
AX = mybir.AxisListType

P = 128
SEQ = 2048
D = 1024
NT = SEQ // P
FC = D // P
NB_PER_CORE = 2



class KB:
    ENGINES = ("T", "V", "S", "G", "Y")

    NPOOL = 6

    def __init__(self, nc):
        self.nc = nc
        self.plan = {e: [] for e in self.ENGINES}
        self.sem = {}
        self.cnt = {}
        for name in ("T", "V", "S", "G", "Y"):
            self.sem[name] = nc.alloc_semaphore("sem_" + name)
            self.cnt[name] = 0
        self.dma_pool = {}
        self.dma_rr = {}
        for q in ("G", "Y", "S"):
            names = []
            for i in range(self.NPOOL):
                nm = f"D{q}{i}"
                self.sem[nm] = nc.alloc_semaphore("sem_" + nm)
                self.cnt[nm] = 0
                names.append(nm)
            self.dma_pool[q] = names
            self.dma_rr[q] = 0
        self.hwm = {}

    def op(self, E, fn, waits=(), sem=None, inc=1):
        if sem is None:
            sem = E
        ws = []
        for w in waits:
            if w is None:
                continue
            s, v = w
            k = (E, s)
            if self.hwm.get(k, 0) >= v:
                continue
            if not s.startswith("D"):
                self.hwm[k] = v
            ws.append((s, v))
        self.plan[E].append((fn, ws, sem, inc))
        self.cnt[sem] += inc
        return (sem, self.cnt[sem])

    def dma(self, E, out, in_, waits=(), **kw):
        pool = self.dma_pool[E]
        sem = pool[self.dma_rr[E] % len(pool)]
        self.dma_rr[E] += 1
        w = tuple(waits)
        if self.cnt[sem]:
            w = w + ((sem, self.cnt[sem]),)
        return self.op(E, lambda e: e.dma_start(out=out, in_=in_, **kw),
                       waits=w, sem=sem, inc=16)

    CHAIN = {"tensor_scalar", "tensor_copy", "tensor_tensor", "tensor_add",
             "tensor_sub", "tensor_mul", "tensor_max", "reduce_max", "reduce_sum",
             "tensor_reduce", "reciprocal", "memset", "memzero", "bn_stats",
             "bn_aggr", "scalar_tensor_tensor", "select", "activation",
             "tensor_single_scalar", "tensor_scalar_mul", "tensor_scalar_add",
             "iota", "affine_select", "tensor_relu", "copy", "mul", "add",
             "square", "sqrt"}

    class _Proxy:
        def __init__(self, kb, eng, name):
            self._kb, self._eng, self._name = kb, eng, name

        def __getattr__(self, attr):
            f = getattr(self._eng, attr)
            if attr not in KB.CHAIN:
                return f
            kb, name = self._kb, self._name

            def wrapped(*a, **k):
                inst = f(*a, **k)
                c = kb.chain_cnt[name]
                if c:
                    inst._wait_ge(kb.chain_sem[name], c)
                inst.then_inc(kb.chain_sem[name], 1)
                kb.chain_cnt[name] += 1
                return inst
            return wrapped

    def emit(self):
        nc = self.nc
        self.chain_sem = {}
        self.chain_cnt = {}
        for e in ("V", "S", "G"):
            self.chain_sem[e] = nc.alloc_semaphore("chain_" + e)
            self.chain_cnt[e] = 0
        with nc.Block() as block:
            def run(eng_name):
                def body(raw):
                    chained = eng_name in ("V", "S", "G")
                    eng = KB._Proxy(self, raw, eng_name) if chained else raw
                    for fn, ws, sem, inc in self.plan[eng_name]:
                        for s, v in ws:
                            raw.wait_ge(self.sem[s], v)
                        c0 = self.chain_cnt.get(eng_name, 0)
                        inst = fn(eng)
                        if inst is not None and inc:
                            if chained and self.chain_cnt[eng_name] > c0:
                                tail = raw.nop()
                                tail._wait_ge(self.chain_sem[eng_name],
                                              self.chain_cnt[eng_name])
                                tail.then_inc(self.sem[sem], inc)
                            else:
                                inst.then_inc(self.sem[sem], inc)
                return body
            block.tensor(run("T"))
            block.vector(run("V"))
            block.scalar(run("S"))
            block.gpsimd(run("G"))
            block.sync(run("Y"))


class Arena:
    def __init__(self, nc, name, kib_per_part=None):
        if kib_per_part is None:
            self.bytes = nc.sbuf_bytes_remaining & ~63
        else:
            self.bytes = kib_per_part * 1024
        self.h = nc.alloc_sbuf_tensor(name, [P, self.bytes // 4], F32)
        self.f32 = self.h.ap()
        self.b16 = self.f32.bitcast(BF16)
        self.off = 0
        self.marks = []

    def alloc_f32(self, cols):
        o = (self.off + 3) & ~3
        self.off = o + 4 * cols
        assert self.off <= self.bytes, f"arena overflow {self.off}>{self.bytes}"
        return self.f32[:, o // 4: o // 4 + cols]

    def alloc_b16(self, cols):
        o = (self.off + 3) & ~3
        self.off = o + 2 * cols
        assert self.off <= self.bytes, f"arena overflow {self.off}>{self.bytes}"
        return self.b16[:, o // 2: o // 2 + cols]

    def push(self):
        self.marks.append(self.off)

    def pop(self):
        self.off = self.marks.pop()


class Ctx:
    def __init__(self, kb, arena, pb, consts, pbig=None):
        self.kb = kb
        self.arena = arena
        self.pb = pb
        self.pbig = pbig
        self.pbb = [b.bitcast(BF16) for b in pb]
        self.c = consts
        self.pb_tok = [None] * 8
        self.pb_wtok = [None] * 8

    def mm(self, bank, stat_list, mov_list, waits=(), n=None, start=True,
           stop=True):
        kb = self.kb
        ps = self.pb[bank]
        if n is None:
            n = mov_list[0].shape[-1]

        mo = stat_list[0].shape[-1]

        def fn(t):
            last = None
            m = len(stat_list)
            for i, (st, mv) in enumerate(zip(stat_list, mov_list)):
                last = t.matmul(ps[0:mo, 0:n], st, mv,
                                start=(start and i == 0),
                                stop=(stop and i == m - 1))
            return last
        w = tuple(waits)
        if start:
            w = w + (self.pb_tok[bank],)
        tok = kb.op("T", fn, waits=w)
        self.pb_wtok[bank] = tok
        return tok

    def transposes(self, bank, pairs, waits=()):
        kb = self.kb
        psb = self.pbb[bank]
        ident = self.c["ident_b16"]

        def fn(t):
            last = None
            for off, src in pairs:
                ncols = src.shape[-1]
                last = t.transpose(psb[0:ncols, off:off + src.shape[0]], src, ident)
            return last
        tok = kb.op("T", fn, waits=tuple(waits) + (self.pb_tok[bank],))
        self.pb_wtok[bank] = tok
        return tok

    def evac(self, E, bank, fn, waits=()):
        tok = self.kb.op(E, fn, waits=tuple(waits) + (self.pb_wtok[bank],))
        self.pb_tok[bank] = tok
        return tok



def emit_ln(ctx, stats, x_ap, out_cb, waits=(), width=D, eps=1e-5, prev_s=None,
            sq_scratch=None):
    kb = ctx.kb
    rw = 1.0 / width
    tok_sum = kb.op("V", lambda v: v.reduce_sum(stats[:, 4:5], x_ap, axis=AX.X),
                    waits=tuple(waits) + (prev_s,))
    tok_sq = kb.op("S", lambda s: s.activation(
        sq_scratch[:, 0:width], x_ap, AF.Square, bias=0.0, scale=1.0,
        accum_out=stats[:, 5:6]), waits=tuple(waits) + (prev_s,))

    def fn_var(v):
        v.tensor_scalar(stats[:, 0:1], stats[:, 4:5], rw, None, ALU.mult)
        v.tensor_scalar(stats[:, 9:10], stats[:, 4:5], -rw, None, ALU.mult)
        v.tensor_scalar(stats[:, 6:7], stats[:, 5:6], rw, None, ALU.mult)
        v.tensor_scalar(stats[:, 7:8], stats[:, 0:1], stats[:, 0:1], None, ALU.mult)
        return v.tensor_sub(stats[:, 1:2], stats[:, 6:7], stats[:, 7:8])
    tok_v = kb.op("V", fn_var, waits=(tok_sum, tok_sq))

    def fn_rstd(s):
        s.activation(stats[:, 2:3], stats[:, 1:2], AF.Ln, bias=ctx.c["c_eps"], scale=1.0)
        s.activation(stats[:, 3:4], stats[:, 2:3], AF.Exp, bias=0.0, scale=-0.5)
        return s.activation(stats[:, 10:11], stats[:, 9:10], AF.Identity,
                            bias=0.0, scale=stats[:, 3:4])
    tok_s = kb.op("S", fn_rstd, waits=(tok_v,))
    out_cb(stats[:, 0:1], stats[:, 3:4], tok_s)
    return tok_s



def emit_l0_attn(ctx, x_sb, x_tok, wq_in, wo_in, entry=None, stop_after=None):
    kb, ar = ctx.kb, ctx.arena
    NH, DK = 16, 64
    ar.push()
    qT = ar.alloc_b16(NT * D)
    kT = ar.alloc_b16(NT * D)
    vv = ar.alloc_b16(NT * D)
    stats = ar.alloc_f32(64)
    xnt = ar.alloc_b16(D)
    wqk = [ar.alloc_b16(FC * P) for _ in range(2)]
    lnT = ar.alloc_b16(NT * D)
    ot2 = ar.alloc_b16(4 * D)
    wo_sb = lnT[:, 0:FC * D]
    ptile2 = [lnT[:, FC * D:FC * D + SEQ],
              lnT[:, FC * D + 2 * SEQ + 2 * D:FC * D + 3 * SEQ + 2 * D]]
    ptT = lnT[:, FC * D + SEQ:FC * D + 2 * SEQ]
    otile2 = [ot2[:, 0:D], ot2[:, 2 * D:3 * D]]
    otT2 = [ot2[:, D:2 * D], ot2[:, 3 * D:4 * D]]

    lnT3 = lnT.rearrange("p (c s) -> p c s", c=FC)
    qT3 = qT.rearrange("p (h s) -> p h s", h=FC)
    kT3 = kT.rearrange("p (h s) -> p h s", h=FC)
    vv3 = vv.rearrange("p (t f) -> p t f", t=NT)

    ln_toks = []
    prev_s = None
    for tt in range(NT):
        xap = x_sb[:, tt * D:(tt + 1) * D]

        def cb(mean, rstd, tok_s, tt=tt, xap=xap):
            tok_a = kb.op("V", lambda v: v.tensor_scalar(
                xnt, xap, mean, rstd, ALU.subtract, ALU.mult),
                waits=(tok_s, ln_toks[-1] if ln_toks else None))
            bank = 6 + (tt % 2)
            pairs = [(c * P, xnt[:, c * P:(c + 1) * P]) for c in range(FC)]
            tok_t = ctx.transposes(bank, pairs, waits=(tok_a, ctx.c["load_tok"]))
            tok_e = ctx.evac("V", bank, lambda v, tt=tt, bank=bank: v.tensor_copy(
                lnT3[:, :, tt * P:(tt + 1) * P],
                ctx.pbb[bank][:, 0:FC * P].rearrange("p (c s) -> p c s", s=P)))
            ln_toks.append(tok_e)
        prev_s = emit_ln(ctx, stats, xap, cb,
                         waits=(x_tok, entry, ln_toks[-1] if ln_toks else None),
                         prev_s=prev_s, sq_scratch=xnt)
    lnT_done = ln_toks[-1]
    if stop_after == "ln":
        ctx.dbg = dict(lnT=lnT)
        ar.pop()
        return lnT_done

    wv4 = qT[:, 0:FC * D].rearrange("p (c f) -> p c f", c=FC)
    wv_toks = []
    for c in range(FC):
        wv_toks.append(kb.dma("G", wv4[:, c, :].rearrange("p (h n) -> p h n", h=NH),
                       wq_in[:, 2, c * P:(c + 1) * P, :].rearrange("h p n -> p h n"),
                       waits=(entry,)))
    v_toks = []
    for tt in range(NT):
        for half in range(2):
            bank = 2 + (tt * 2 + half) % 2
            stat = [lnT3[:, c, tt * P:(tt + 1) * P] for c in range(FC)]
            mov = [wv4[:, c, half * 512:(half + 1) * 512] for c in range(FC)]
            tok_m = ctx.mm(bank, stat, mov, waits=tuple(wv_toks) + (lnT_done,))
            if half == 0:
                tok_e = ctx.evac("V", bank, lambda v, tt=tt, half=half, bank=bank:
                                 v.tensor_copy(vv3[:, tt, half * 512:(half + 1) * 512],
                                               ctx.pb[bank][:, 0:512]))
            else:
                tok_e = ctx.evac("S", bank, lambda s_, tt=tt, half=half, bank=bank:
                                 s_.copy(vv3[:, tt, half * 512:(half + 1) * 512],
                                         ctx.pb[bank][:, 0:512]))
            v_toks.append(tok_e)
    v_done = v_toks[-1]

    wk_tok = {}
    proj_done = None
    for ai, dT3 in ((1, kT3), (0, qT3)):
        for hp in range(NH // 2):
            wb = wqk[hp % 2]
            wb4 = wb.rearrange("p (c h n) -> p c h n", c=FC, h=2)
            w_toks = []
            for h2 in range(2):
                w_toks.append(kb.dma(
                    "G", wb4[:, :, h2, :],
                    wq_in[2 * hp + h2, ai, :, :].rearrange("(c p) n -> p c n", p=P),
                    waits=(wk_tok.get(hp % 2), entry)))
            for tn in range(SEQ // 512):
                bank = tn % 2
                stat = [wb[:, c * P:(c + 1) * P] for c in range(FC)]
                mov = [lnT3[:, c, tn * 512:(tn + 1) * 512] for c in range(FC)]
                tok_m = ctx.mm(bank, stat, mov,
                               waits=tuple(w_toks) + (lnT_done, v_done))
                if tn % 2 == 0:
                    tok_e = ctx.evac("V", bank, lambda v, hp=hp, tn=tn, bank=bank,
                                     dT3=dT3: v.tensor_copy(
                        dT3[:, hp, tn * 512:(tn + 1) * 512], ctx.pb[bank][:, 0:512]))
                else:
                    tok_e = ctx.evac("S", bank, lambda s_, hp=hp, tn=tn, bank=bank,
                                     dT3=dT3: s_.copy(
                        dT3[:, hp, tn * 512:(tn + 1) * 512], ctx.pb[bank][:, 0:512]))
                wk_tok[hp % 2] = tok_e
                proj_done = tok_e

    wo4 = wo_sb.rearrange("p (c f) -> p c f", c=FC)
    tok_wo = kb.dma("G", wo4, wo_in.rearrange("(c p) f -> p c f", p=P),
                    waits=(proj_done,))

    SC = 0.125
    x_done = None

    def tail_l0(hh_state):
        h, ptile, ex_toks, tok_r, bs, otile = hh_state
        pt_toks = []
        for g in range(2):
            bank = bs + g
            pairs = [(i * P, ptile[:, (g * 8 + i) * P:(g * 8 + i + 1) * P])
                     for i in range(8)]
            ctx.transposes(bank, pairs, waits=tuple(ex_toks))
            if g == 0:
                tok_e = ctx.evac("V", bank, lambda v, g=g, bank=bank: v.tensor_copy(
                    ptT[:, g * 8 * P:(g + 1) * 8 * P], ctx.pbb[bank][:, 0:8 * P]))
            else:
                tok_e = ctx.evac("S", bank, lambda s, g=g, bank=bank: s.copy(
                    ptT[:, g * 8 * P:(g + 1) * 8 * P], ctx.pbb[bank][:, 0:8 * P]))
            pt_toks.append(tok_e)
        stat = [ptT[:, kc * P:(kc + 1) * P] for kc in range(NT)]
        mov = [vv3[:, kc, h * DK:(h + 1) * DK] for kc in range(NT)]
        ctx.mm(bs + 2, stat, mov, waits=tuple(pt_toks) + (v_done,), n=DK)
        return ctx.evac("S", bs + 2, lambda s, h=h, otile=otile: s.activation(
            otile[:, h * DK:(h + 1) * DK], ctx.pb[bs + 2][:, 0:DK],
            AF.Identity, bias=0.0, scale=stats[:, 51 + h % 2:52 + h % 2]),
            waits=(tok_r,))

    qt_tail = None
    for qt in range(NT):
        hstate = None
        for h in range(NH):
            hp, hh = h // 2, h % 2
            ptile = ptile2[h % 2]
            bs = 4 * (h % 2)
            mb = 16 + 12 * (h % 2)
            sc_toks = []
            for kc4 in range(4):
                bank = bs + kc4
                stat = [qT3[hh * 64:(hh + 1) * 64, hp, qt * P:(qt + 1) * P]]
                mov = [kT3[hh * 64:(hh + 1) * 64, hp, kc4 * 512:(kc4 + 1) * 512]]
                sc_toks.append(ctx.mm(bank, stat, mov, waits=(proj_done,)))
            tok_e = kb.op("S", lambda s, bs=bs, ptile=ptile, mb=mb: s.activation(
                ptile, ctx.pbig[:, bs * 512:(bs + 4) * 512],
                AF.Exp, bias=0.0, scale=SC,
                accum_out=stats[:, mb + 10:mb + 11]),
                waits=tuple(sc_toks) + (tok_wo,))
            for kc4 in range(4):
                ctx.pb_tok[bs + kc4] = tok_e
            ex_toks = [tok_e]
            tok_r = kb.op("V", lambda v, h=h, mb=mb: v.reciprocal(
                stats[:, 51 + h % 2:52 + h % 2], stats[:, mb + 10:mb + 11]),
                waits=(tok_e,))
            if h == 2 and qt_tail is not None:
                x_done = qt_tail()
                qt_tail = None
            if hstate is not None:
                tail_l0(hstate)
            hstate = (h, ptile, ex_toks, tok_r, bs, otile2[qt % 2])
        tok_o = tail_l0(hstate)

        def mk_tail(qt=qt, tok_o=tok_o):
            otile, otT = otile2[qt % 2], otT2[qt % 2]

            def emit():
                pairs = [(c * P, otile[:, c * P:(c + 1) * P]) for c in range(FC)]
                ctx.transposes(3, pairs, waits=(tok_o,))
                tok_e = ctx.evac("V", 3, lambda v: v.tensor_copy(
                    otT, ctx.pbb[3][:, 0:FC * P]))
                for half in range(2):
                    bank = half
                    stat = [otT[:, c * P:(c + 1) * P] for c in range(FC)]
                    mov = [wo4[:, c, half * 512:(half + 1) * 512] for c in range(FC)]
                    ctx.mm(bank, stat, mov, waits=(tok_e, tok_wo))
                    xap = x_sb[:, qt * D + half * 512: qt * D + (half + 1) * 512]
                    tok_x = ctx.evac("V", bank, lambda v, xap=xap, bank=bank:
                                     v.tensor_add(xap, xap, ctx.pb[bank][:, 0:512]))
                return tok_x
            return emit
        if qt == NT - 1:
            x_done = mk_tail()()
        else:
            qt_tail = mk_tail()
    ar.pop()
    return x_done




def emit_lnT_phase(ctx, x_sb, lnT3, xnt, stats, x_tok, entry, save_mr=None):
    kb = ctx.kb
    ln_toks = []
    prev_s = None
    for tt in range(NT):
        xap = x_sb[:, tt * D:(tt + 1) * D]

        def cb(mean, rstd, tok_s, tt=tt, xap=xap):
            if save_mr is not None:
                def fn_mr(v, tt=tt):
                    v.tensor_copy(save_mr[:, 2 * tt:2 * tt + 1], stats[:, 0:1])
                    return v.tensor_copy(save_mr[:, 2 * tt + 1:2 * tt + 2], stats[:, 3:4])
                kb.op("V", fn_mr, waits=(tok_s,))
            tok_a = kb.op("S", lambda s_, rstd=rstd: s_.activation(
                xnt, xap, AF.Identity, bias=stats[:, 10:11], scale=rstd),
                waits=(tok_s, ln_toks[-1] if ln_toks else None))
            bank = 6 + (tt % 2)
            pairs = [(c * P, xnt[:, c * P:(c + 1) * P]) for c in range(FC)]
            tok_t = ctx.transposes(bank, pairs, waits=(tok_a, ctx.c["load_tok"]))
            eng = "V" if tt % 2 == 0 else "S"
            if eng == "V":
                tok_e = ctx.evac("V", bank, lambda v, tt=tt, bank=bank: v.tensor_copy(
                    lnT3[:, :, tt * P:(tt + 1) * P],
                    ctx.pbb[bank][:, 0:FC * P].rearrange("p (c s) -> p c s", s=P)))
            else:
                tok_e = ctx.evac("S", bank, lambda s_, tt=tt, bank=bank: s_.copy(
                    lnT3[:, :, tt * P:(tt + 1) * P],
                    ctx.pbb[bank][:, 0:FC * P].rearrange("p (c s) -> p c s", s=P)))
            ln_toks.append(tok_e)
        prev_s = emit_ln(ctx, stats, xap, cb,
                         waits=(x_tok, entry, ln_toks[-1] if ln_toks else None),
                         prev_s=prev_s, sq_scratch=xnt)
    return ln_toks[-1]


def _l0f_flush(ctx, kb, act3, unit, prev_down):
    ic, tc, A, Bb, Cc, tok_sg, tok_u2 = unit
    tok_gl = kb.op("V", lambda v, A=A, Bb=Bb: v.tensor_mul(A, A, Bb),
                   waits=(tok_sg, tok_u2))
    return kb.op("V", lambda v, ic=ic, tc=tc, A=A, Cc=Cc: v.tensor_mul(
        act3[:, ic, tc * 512:(tc + 1) * 512], A, Cc),
        waits=(tok_gl, prev_down))


def emit_l0_ffn(ctx, x_sb, x_tok, rw_in, rb_in, gup_in, gub_in, dp_in, db_in,
                entry=None):
    kb, ar = ctx.kb, ctx.arena
    NE, NIC = 8, 16
    ar.push()
    lnT = ar.alloc_b16(NT * D)
    act = ar.alloc_b16(NIC * 1024)
    dpsb = ar.alloc_b16(NIC * 1024)
    stats = ar.alloc_f32(64)
    xnt = ar.alloc_b16(D)
    mr = ar.alloc_f32(2 * NT)
    A2 = [ar.alloc_f32(512) for _ in range(2)]
    B2 = [ar.alloc_f32(512) for _ in range(2)]
    C2 = [ar.alloc_f32(512) for _ in range(2)]
    wrouter = ar.alloc_f32(NT * 8)
    rwT = ar.alloc_f32(FC * 8)
    rw_stage = ar.alloc_f32(1024)
    rb_row = ar.alloc_f32(8)
    Sb = ar.alloc_f32(8)
    rb_b = ar.alloc_f32(8)
    db_all = ar.alloc_f32(1024)
    xT = ar.alloc_f32(FC * P)
    wtb = ar.alloc_f32(P)
    rsc = ar.alloc_f32(40)
    gwb = [ar.alloc_b16(FC * P) for _ in range(4)]
    gub_stage = ar.alloc_f32(P)
    gub_sb = ar.alloc_f32(32)
    lnT3 = lnT.rearrange("p (c s) -> p c s", c=FC)
    act3 = act.rearrange("p (i s) -> p i s", i=NIC)
    dp3 = dpsb.rearrange("p (i s) -> p i s", i=NIC)
    rwT3 = rwT.rearrange("p (c e) -> p c e", c=FC)
    xT3 = xT.rearrange("p (c s) -> p c s", c=FC)
    probs0, probs, t1, t2, lgt = (rsc[:, 0:8], rsc[:, 8:16], rsc[:, 16:24],
                                  rsc[:, 24:32], rsc[:, 32:40])

    lnT_done = emit_lnT_phase(ctx, x_sb, lnT3, xnt, stats, x_tok, entry, save_mr=mr)

    tok_rw = kb.dma("Y", rw_stage[0:8, :], rw_in, waits=(entry,))
    tok_rb = kb.dma("Y", rb_row[0:1, :], rb_in.rearrange("(o e) -> o e", o=1), waits=(entry,))
    tok_dbl = kb.dma("Y", db_all[0:8, :], db_in, waits=(entry,))

    def fn_trw(t):
        last = None
        for c in range(FC):
            last = t.transpose(ctx.pb[2][:, c * 8:(c + 1) * 8],
                               rw_stage[0:8, c * P:(c + 1) * P],
                               ctx.c["ident_f32"][0:8, 0:8])
        return last
    tok_trw = kb.op("T", fn_trw, waits=(tok_rw, ctx.c["load_tok"], ctx.pb_tok[2],
                                        ctx.pb_wtok[2]))
    ctx.pb_wtok[2] = tok_trw
    tok_rwT = ctx.evac("V", 2, lambda v: v.tensor_copy(rwT, ctx.pb[2][:, 0:FC * 8]))
    tok_S1 = ctx.mm(3, [rwT3[:, c, :] for c in range(FC)],
                    [ctx.c["ones_col"] for _ in range(FC)], waits=(tok_rwT,), n=1)
    tok_S2 = ctx.evac("V", 3, lambda v: v.tensor_copy(stats[0:8, 32:33],
                                                      ctx.pb[3][0:8, 0:1]))

    def fn_tS(t):
        return t.transpose(ctx.pb[2][0:1, 0:8], stats[0:8, 32:33],
                           ctx.c["ident_f32"][0:8, 0:8])
    tok_tS = kb.op("T", fn_tS, waits=(tok_S2, ctx.pb_tok[2], ctx.pb_wtok[2]))
    ctx.pb_wtok[2] = tok_tS
    tok_Srow = ctx.evac("V", 2, lambda v: v.tensor_copy(stats[0:1, 33:41],
                                                        ctx.pb[2][0:1, 0:8]))
    tok_bc = ctx.mm(3, [ctx.c["ones_row"]], [stats[0:1, 33:41]], waits=(tok_Srow,), n=8)
    tok_Sb = ctx.evac("V", 3, lambda v: v.tensor_copy(Sb, ctx.pb[3][:, 0:8]))
    tok_bc2 = ctx.mm(3, [ctx.c["ones_row"]], [rb_row[0:1, 0:8]], waits=(tok_rb,), n=8)
    tok_rbb = ctx.evac("V", 3, lambda v: v.tensor_copy(rb_b, ctx.pb[3][:, 0:8]))

    wdb_toks = []
    for tt in range(NT):
        for bk, cs in ((0, (0, 1, 2, 3)), (1, (4, 5, 6, 7))):
            def fn_tx(t, tt=tt, bk=bk, cs=cs):
                last = None
                for i, c in enumerate(cs):
                    last = t.transpose(ctx.pb[bk][:, i * P:(i + 1) * P],
                                       x_sb[:, tt * D + c * P:tt * D + (c + 1) * P],
                                       ctx.c["ident_f32"])
                return last
            tok_t = kb.op("T", fn_tx, waits=(x_tok, ctx.pb_tok[bk], ctx.pb_wtok[bk],
                                             ctx.c["load_tok"]))
            ctx.pb_wtok[bk] = tok_t
            tok_e = ctx.evac("V", bk, lambda v, bk=bk, cs=cs: v.tensor_copy(
                xT[:, cs[0] * P:(cs[0] + 4) * P], ctx.pb[bk][:, 0:4 * P]))
        tok_xT = tok_e
        tok_G = ctx.mm(3, [xT3[:, c, :] for c in range(FC)],
                       [rwT3[:, c, :] for c in range(FC)], waits=(tok_xT,), n=8)
        tok_nr = kb.op("V", lambda v, tt=tt: v.tensor_scalar(
            stats[:, 42:43], mr[:, 2 * tt:2 * tt + 1], mr[:, 2 * tt + 1:2 * tt + 2],
            -1.0, ALU.mult, ALU.mult))
        tok_l1 = ctx.evac("V", 3, lambda v, tt=tt: v.tensor_scalar(
            lgt, ctx.pb[3][:, 0:8], mr[:, 2 * tt + 1:2 * tt + 2], None, ALU.mult))
        tok_l2 = kb.op("V", lambda v: v.scalar_tensor_tensor(
            t1, Sb, stats[:, 42:43], lgt, ALU.mult, ALU.add),
            waits=(tok_l1, tok_nr, tok_Sb))
        tok_l3 = kb.op("V", lambda v: v.tensor_add(lgt, t1, rb_b),
                       waits=(tok_l2, tok_rbb))
        tok_m = kb.op("V", lambda v: v.reduce_max(stats[:, 43:44], lgt, axis=AX.X,
                                                  negate=True), waits=(tok_l3,))
        tok_ex = kb.op("S", lambda s: s.activation(
            probs0, lgt, AF.Exp, bias=stats[:, 43:44], scale=1.0,
            accum_out=stats[:, 44:45]), waits=(tok_m,))
        tok_ri = kb.op("V", lambda v: v.reciprocal(stats[:, 45:46], stats[:, 44:45]),
                       waits=(tok_ex,))
        tok_p = kb.op("V", lambda v: v.tensor_scalar(
            probs, probs0, stats[:, 45:46], None, ALU.mult), waits=(tok_ri,))
        tok_pm = kb.op("V", lambda v: v.reduce_max(stats[:, 46:47], probs, axis=AX.X),
                       waits=(tok_p,))
        tok_eq = kb.op("V", lambda v: v.tensor_scalar(
            t1, probs, stats[:, 46:47], None, ALU.is_equal), waits=(tok_pm,))
        tok_iv = kb.op("V", lambda v: v.tensor_scalar(
            t2, t1, -1.0, 1.0, ALU.mult, ALU.add), waits=(tok_eq,))
        tok_p2 = kb.op("V", lambda v: v.tensor_mul(t1, probs, t2), waits=(tok_iv,))
        tok_m2 = kb.op("V", lambda v: v.reduce_max(stats[:, 47:48], t1, axis=AX.X),
                       waits=(tok_p2,))
        tok_ge = kb.op("V", lambda v: v.tensor_scalar(
            t2, probs, stats[:, 47:48], None, ALU.is_ge), waits=(tok_m2,))
        tok_w = kb.op("V", lambda v, tt=tt: v.tensor_mul(
            wrouter[:, tt * 8:(tt + 1) * 8], probs, t2), waits=(tok_ge,))
        def fn_tw(t, tt=tt):
            return t.transpose(ctx.pb[2][0:8, 0:P], wrouter[:, tt * 8:(tt + 1) * 8],
                               ctx.c["ident_f32"])
        tok_tw = kb.op("T", fn_tw, waits=(tok_w, ctx.pb_tok[2], ctx.pb_wtok[2]))
        ctx.pb_wtok[2] = tok_tw
        tok_wt = ctx.evac("V", 2, lambda v: v.tensor_copy(wtb[0:8, :],
                                                          ctx.pb[2][0:8, 0:P]))
        for half in range(2):
            bk = 4 + half
            tok_mdb = ctx.mm(bk, [wtb[0:8, :]],
                             [db_all[0:8, half * 512:(half + 1) * 512]],
                             waits=(tok_wt, tok_dbl))
            xap = x_sb[:, tt * D + half * 512:tt * D + (half + 1) * 512]
            tok_xadd = ctx.evac("V", bk, lambda v, xap=xap, bk=bk: v.tensor_add(
                xap, xap, ctx.pb[bk][:, 0:512]))
            wdb_toks.append(tok_xadd)
    w_done = wdb_toks[-1]

    gw_tok = {}
    dp_tok = None
    x_done = w_done
    for e in range(NE):
        tok_gr = kb.dma("Y", gub_stage[0:32, :],
                        gub_in[e].rearrange("(g p) -> g p", p=P),
                        waits=(gw_tok.get("gub"), entry))

        def fn_tg(t):
            return t.transpose(ctx.pb[2][:, 0:32], gub_stage[0:32, :],
                               ctx.c["ident_f32"][0:32, 0:32])
        tok_tg = kb.op("T", fn_tg, waits=(tok_gr, ctx.pb_tok[2], ctx.pb_wtok[2]))
        ctx.pb_wtok[2] = tok_tg
        tok_gub = ctx.evac("V", 2, lambda v: v.tensor_copy(gub_sb,
                                                           ctx.pb[2][:, 0:32]))
        gw_tok["gub"] = tok_gub
        tok_dp = kb.dma("G", dp3, dp_in[e].rearrange("(i p) h -> p i h", p=P),
                        waits=(dp_tok, entry))
        act_done = []
        prev_down = gw_tok.get("prev_down")
        pend = gw_tok.setdefault("pend", {})
        for th in range(2):
            t0 = th * 1024
            for ic in range(NIC):
                bufs = {}
                for gi, guc in ((0, ic), (1, NIC + ic)):
                    wb = gwb[2 * (ic % 2) + gi]
                    tok_lw = kb.dma("G",
                                    wb.rearrange("p (c s) -> p c s", c=FC),
                                    gup_in[e, :, guc * P:(guc + 1) * P].rearrange(
                                        "(c p) n -> p c n", p=P),
                                    waits=(gw_tok.get((ic % 2, gi)), entry))
                    bufs[gi] = (wb, tok_lw)
                for tc in range(2):
                    A, Bb, Cc = A2[tc], B2[tc], C2[tc]
                    mov = [lnT3[:, c, t0 + tc * 512:t0 + (tc + 1) * 512]
                           for c in range(FC)]
                    wb, tok_lw = bufs[0]
                    tok_g = ctx.mm(tc, [wb[:, c * P:(c + 1) * P] for c in range(FC)],
                                   mov, waits=(tok_lw, lnT_done))
                    wb, tok_lw = bufs[1]
                    tok_u = ctx.mm(2 + tc,
                                   [wb[:, c * P:(c + 1) * P] for c in range(FC)],
                                   mov, waits=(tok_lw, lnT_done))
                    gub_ap = gub_sb[:, ic:ic + 1]
                    gub_ap2 = gub_sb[:, NIC + ic:NIC + ic + 1]
                    pa = None
                    if tc in pend:
                        pa = _l0f_flush(ctx, kb, act3, pend.pop(tc), prev_down)
                        act_done.append(pa)
                    tok_mv = ctx.evac("V", tc, lambda v, tc=tc, gub_ap=gub_ap, A=A:
                                      v.tensor_scalar(A, ctx.pb[tc][:, 0:512],
                                                      gub_ap, 7.0, ALU.add, ALU.min),
                                      waits=(tok_gub, pa))
                    tok_sg = kb.op("S", lambda s, A=A, Bb=Bb: s.activation(
                        Bb, A, AF.Sigmoid, bias=0.0, scale=1.702), waits=(tok_mv,))
                    tok_u1 = ctx.evac("V", 2 + tc,
                                      lambda v, tc=tc, gub_ap2=gub_ap2, Cc=Cc:
                                      v.tensor_scalar(Cc, ctx.pb[2 + tc][:, 0:512],
                                                      gub_ap2, 7.0, ALU.add, ALU.min),
                                      waits=(tok_gub, pa))
                    tok_u22 = kb.op("G", lambda g, Cc=Cc: g.tensor_scalar(
                        Cc, Cc, -7.0, 1.0, ALU.max, ALU.add), waits=(tok_u1,))
                    pend[tc] = (ic, tc, A, Bb, Cc, tok_sg, tok_u22)
                    gw_tok[(ic % 2, 0)] = tok_g
                    gw_tok[(ic % 2, 1)] = tok_u
            for tc in list(pend):
                act_done.append(_l0f_flush(ctx, kb, act3, pend.pop(tc), prev_down))
            for tti in range(8):
                tt = th * 8 + tti
                for half in range(2):
                    bk = 4 + (tti % 2) * 2 + half
                    stat = [act3[:, ic, tti * P:(tti + 1) * P] for ic in range(NIC)]
                    mov = [dp3[:, ic, half * 512:(half + 1) * 512] for ic in range(NIC)]
                    tok_d = ctx.mm(bk, stat, mov,
                                   waits=(act_done[-1], tok_dp))
                    prev_down = tok_d
                    xap = x_sb[:, tt * D + half * 512:tt * D + (half + 1) * 512]
                    wap = wrouter[:, tt * 8 + e:tt * 8 + e + 1]
                    x_done = ctx.evac("V", bk, lambda v, xap=xap, bk=bk, wap=wap:
                                      v.scalar_tensor_tensor(
                                          xap, ctx.pb[bk][:, 0:512], wap, xap,
                                          ALU.mult, ALU.add), waits=(w_done,))
            act_done = []
        gw_tok["prev_down"] = prev_down
        dp_tok = tok_d
    ar.pop()
    return x_done



MAGIC = 12582912.0
INV2PI = float(np.float32(1.0 / (2.0 * np.pi)))
C1 = 6.28125
C2 = float(np.float32(2.0 * np.pi - 6.28125))
PI2 = float(np.float32(np.pi / 2))
PI_LO = float(np.nextafter(np.float32(np.pi), np.float32(0.0)))


def emit_pe_add(ctx, xnt, tt, A0, S1, S3, s_ap, ang0_in, prev_tok, x_ready):
    kb = ctx.kb
    tok_a = kb.dma("Y", A0, ang0_in[:, tt, :], waits=(prev_tok, x_ready))
    tok_s = kb.op("V", lambda v: v.tensor_scalar(A0, A0, s_ap, None, ALU.mult),
                  waits=(tok_a, prev_tok))
    out_toks = []
    for trig in range(2):
        def fn_n(v, trig=trig):
            if trig == 0:
                v.tensor_scalar(S3, A0, INV2PI, MAGIC, ALU.mult, ALU.add)
            else:
                v.tensor_scalar(S3, A0, INV2PI, 0.25, ALU.mult, ALU.add)
                v.tensor_scalar(S3, S3, MAGIC, None, ALU.add)
            v.tensor_scalar(S3, S3, MAGIC, None, ALU.subtract)
            v.scalar_tensor_tensor(S1, S3, C1, A0, ALU.mult, ALU.subtract)
            v.scalar_tensor_tensor(S1, S3, -C2, S1, ALU.mult, ALU.subtract)
            if trig == 1:
                v.tensor_scalar(S1, S1, PI2, None, ALU.add)
            return v.tensor_scalar(S1, S1, PI_LO, -PI_LO, ALU.min, ALU.max)
        tok_r = kb.op("V", fn_n, waits=(tok_s,) + tuple(out_toks))
        tok_sin = kb.op("S", lambda s: s.activation(
            S3, S1, AF.Sin, bias=0.0, scale=1.0), waits=(tok_r,))
        tok_add = kb.op("V", lambda v, trig=trig: v.tensor_add(
            xnt.rearrange("p (i two) -> p i two", two=2)[:, :, trig],
            xnt.rearrange("p (i two) -> p i two", two=2)[:, :, trig], S3),
            waits=(tok_sin,))
        out_toks.append(tok_add)
    return out_toks[-1]


def emit_pe_add3(ctx, xnt, A0, S3, pbank, s_ap, dma_toks, x_ready, pb_guard):
    kb = ctx.kb
    pn = ctx.pb[pbank][:, 0:512]
    out = None
    for trig, src in ((0, A0), (1, S3)):
        def fn_n(v, trig=trig, src=src):
            v.tensor_scalar(src, src, s_ap, None, ALU.mult)
            if trig == 0:
                v.tensor_scalar(pn, src, INV2PI, MAGIC, ALU.mult, ALU.add)
            else:
                v.tensor_scalar(pn, src, INV2PI, 0.25, ALU.mult, ALU.add)
                v.tensor_scalar(pn, pn, MAGIC, None, ALU.add)
            v.tensor_scalar(pn, pn, MAGIC, None, ALU.subtract)
            v.scalar_tensor_tensor(src, pn, -C1, src, ALU.mult, ALU.add)
            v.scalar_tensor_tensor(src, pn, -C2, src, ALU.mult, ALU.add)
            if trig == 1:
                v.tensor_scalar(src, src, PI2, None, ALU.add)
            return v.tensor_scalar(src, src, PI_LO, -PI_LO, ALU.min, ALU.max)
        tok_r = kb.op("V", fn_n, waits=dma_toks + (x_ready, out, pb_guard))
        tok_sin = kb.op("S", lambda s, src=src: s.activation(
            pn, src, AF.Sin, bias=0.0, scale=1.0), waits=(tok_r,))
        out = kb.op("V", lambda v, trig=trig: v.tensor_add(
            xnt.rearrange("p (i two) -> p i two", two=2)[:, :, trig],
            xnt.rearrange("p (i two) -> p i two", two=2)[:, :, trig], pn),
            waits=(tok_sin,))
    return out


def emit_pe_add2(ctx, xnt, tt, A0, S3, s_ap, ang0_in, prev_tok, x_ready):
    kb = ctx.kb
    out = None
    for trig in range(2):
        tok_a = kb.dma("Y", A0, ang0_in[:, tt, :],
                       waits=(prev_tok, x_ready, out))
        tok_s = kb.op("V", lambda v: v.tensor_scalar(A0, A0, s_ap, None, ALU.mult),
                      waits=(tok_a,))

        def fn_n(v, trig=trig):
            if trig == 0:
                v.tensor_scalar(S3, A0, INV2PI, MAGIC, ALU.mult, ALU.add)
            else:
                v.tensor_scalar(S3, A0, INV2PI, 0.25, ALU.mult, ALU.add)
                v.tensor_scalar(S3, S3, MAGIC, None, ALU.add)
            v.tensor_scalar(S3, S3, MAGIC, None, ALU.subtract)
            v.scalar_tensor_tensor(A0, S3, -C1, A0, ALU.mult, ALU.add)
            v.scalar_tensor_tensor(A0, S3, -C2, A0, ALU.mult, ALU.add)
            if trig == 1:
                v.tensor_scalar(A0, A0, PI2, None, ALU.add)
            return v.tensor_scalar(A0, A0, PI_LO, -PI_LO, ALU.min, ALU.max)
        tok_r = kb.op("V", fn_n, waits=(tok_s,))
        tok_sin = kb.op("S", lambda s: s.activation(
            S3, A0, AF.Sin, bias=0.0, scale=1.0), waits=(tok_r,))
        out = kb.op("V", lambda v, trig=trig: v.tensor_add(
            xnt.rearrange("p (i two) -> p i two", two=2)[:, :, trig],
            xnt.rearrange("p (i two) -> p i two", two=2)[:, :, trig], S3),
            waits=(tok_sin,))
    return out


def emit_l1_attn(ctx, x_sb, x_tok, wqT_in, wkT_in, wvT_in, woT_in, bq_cm_in,
                 bk_cm_in, bv_b_in, bo_b_in, gam_b_in, bet_b_in, rby_in, ang0_in,
                 entry=None, stop_after=None):
    kb, ar = ctx.kb, ctx.arena
    NH, DK = 16, 64
    ar.push()
    qT = ar.alloc_b16(NT * D)
    kT = ar.alloc_b16(NT * D)
    vv = ar.alloc_b16(NT * D)
    lnpeT = ar.alloc_b16(NT * D)
    stats = ar.alloc_f32(62)
    mr = ar.alloc_f32(2 * NT)
    wqk = ar.alloc_b16(FC * P)
    bqcm = ar.alloc_f32(8)
    bkcm = ar.alloc_f32(8)
    rbuf = ar.alloc_f32(16)
    rsb = ar.alloc_f32(8)

    qf = qT.bitcast(F32)
    xnt1 = qf[:, 0:D]
    A0a = qf[:, D:D + 512]
    S1a = qf[:, D + 512:D + 1024]
    S3a = qf[:, D + 1024:D + 1536]
    tpeb = qT[:, 5120:6144]
    wvT_sb = qT[:, 0:FC * D]
    bvb = qT[:, FC * D:FC * D + 2 * D].bitcast(F32)
    wx = ar.alloc_b16(1024)
    S3b = ar.alloc_f32(512)
    A0b = wx.bitcast(F32)
    woT_buf = [lnpeT[:, i * 512:(i + 1) * 512] for i in range(4)] + [
        lnpeT[:, 7168 + i * 512:7168 + (i + 1) * 512] for i in range(2)]
    ptile2 = [lnpeT[:, 2048:4096], lnpeT[:, 4096:6144]]
    ptT = lnpeT[:, 6144:7168]
    ot2b = ar.alloc_b16(2 * D)
    xn2 = ar.alloc_f32(D)
    otile2 = [wqk[:, 0:D], ot2b[:, 0:D]]
    otT2 = [tpeb, ot2b[:, D:2 * D]]
    gamb = lnpeT[:, 8192:10240].bitcast(F32)
    betb = lnpeT[:, 10240:12288].bitcast(F32)
    bob = lnpeT[:, 12288:14336].bitcast(F32)
    xnt6_2 = [lnpeT[:, 14336:16384].bitcast(F32), xn2]

    lnpeT3 = lnpeT.rearrange("p (c s) -> p c s", c=FC)
    qT3 = qT.rearrange("p (h s) -> p h s", h=FC)
    kT3 = kT.rearrange("p (h s) -> p h s", h=FC)
    vv3 = vv.rearrange("p (t f) -> p t f", t=NT)
    wvT4 = wvT_sb.rearrange("p (c f) -> p c f", c=FC)

    tok_rby = kb.dma("Y", rbuf[0:1, 0:3], rby_in.rearrange("(o r) -> o r", o=1),
                     waits=(entry,))

    def fn_sc(v):
        r, b, y = rbuf[0:1, 0:1], rbuf[0:1, 1:2], rbuf[0:1, 2:3]
        row = rbuf[0:1, 4:12]
        v.memset(row, 0.0)
        v.tensor_scalar(row[:, 0:1], y, 0.1, 1.0, ALU.mult, ALU.add)
        v.reciprocal(row[:, 0:1], row[:, 0:1])
        v.tensor_scalar(rbuf[0:1, 12:13], r, 0.1, 1.0, ALU.mult, ALU.add)
        v.scalar_tensor_tensor(rbuf[0:1, 12:13], b, 0.05, rbuf[0:1, 12:13],
                               ALU.mult, ALU.add)
        v.scalar_tensor_tensor(rbuf[0:1, 12:13], y, 0.15, rbuf[0:1, 12:13],
                               ALU.mult, ALU.add)
        v.tensor_scalar(row[:, 1:2], rbuf[0:1, 12:13], 0.125, None, ALU.mult)
        v.tensor_scalar(row[:, 2:3], y, 1.2246467991473532e-18, 1.0,
                        ALU.mult, ALU.add)
        v.tensor_scalar(row[:, 3:4], b, 0.1, 1.0, ALU.mult, ALU.add)
        v.tensor_scalar(row[:, 3:4], row[:, 3:4], 1e-5, None, ALU.mult)
        v.tensor_scalar(row[:, 4:5], r, 0.1, 1.0, ALU.mult, ALU.add)
        return v.scalar_tensor_tensor(row[:, 4:5], y, 0.05, row[:, 4:5],
                                      ALU.mult, ALU.add)
    tok_sc = kb.op("V", fn_sc, waits=(tok_rby,))
    tok_bc = ctx.mm(5, [ctx.c["ones_row"]], [rbuf[0:1, 4:12]], waits=(tok_sc,), n=8)
    tok_rsb = ctx.evac("V", 5, lambda v: v.tensor_copy(rsb, ctx.pb[5][:, 0:8]))
    s_pe = rsb[:, 0:1]
    s_ls = rsb[:, 1:2]
    s_vf = rsb[:, 2:3]
    s_eps = rsb[:, 3:4]
    s_gs = rsb[:, 4:5]

    tok_bq = kb.dma("Y", bqcm, bq_cm_in, waits=(entry,))
    tok_bk = kb.dma("Y", bkcm, bk_cm_in, waits=(entry,))

    ln_toks = []
    prev_s = None
    for tt in range(NT):
        xap = x_sb[:, tt * D:(tt + 1) * D]

        def cb(mean, rstd, tok_s, tt=tt, xap=xap):
            def fn_mr(v, tt=tt):
                v.tensor_copy(mr[:, 2 * tt:2 * tt + 1], stats[:, 0:1])
                return v.tensor_copy(mr[:, 2 * tt + 1:2 * tt + 2], stats[:, 3:4])
            kb.op("V", fn_mr, waits=(tok_s,))
            tok_a = kb.op("V", lambda v: v.tensor_scalar(
                xnt1, xap, mean, rstd, ALU.subtract, ALU.mult),
                waits=(tok_s, ln_toks[-1] if ln_toks else None))
            tok_pe = emit_pe_add(ctx, xnt1, tt, A0a, S1a, S3a, s_pe, ang0_in,
                                 ln_toks[-1] if ln_toks else tok_rsb, tok_a)
            tok_c = kb.op("V", lambda v: v.tensor_copy(tpeb, xnt1), waits=(tok_pe,))
            bank = 6 + (tt % 2)
            pairs = [(c * P, tpeb[:, c * P:(c + 1) * P]) for c in range(FC)]
            tok_t = ctx.transposes(bank, pairs, waits=(tok_c, ctx.c["load_tok"]))
            tok_e = ctx.evac("V", bank, lambda v, tt=tt, bank=bank: v.tensor_copy(
                lnpeT3[:, :, tt * P:(tt + 1) * P],
                ctx.pbb[bank][:, 0:FC * P].rearrange("p (c s) -> p c s", s=P)))
            ln_toks.append(tok_e)
        prev_s = emit_ln(ctx, stats, xap, cb,
                         waits=(x_tok, entry, ln_toks[-1] if ln_toks else None),
                         prev_s=prev_s, sq_scratch=xnt1)
    lnpeT_done = ln_toks[-1]
    ctx.dbg = dict(lnpeT=lnpeT, kT=kT, vv=vv, qT=qT)
    if stop_after == "ln":
        ar.pop()
        return lnpeT_done

    tok_wv = kb.dma("G", wvT4, wvT_in.rearrange("(c p) f -> p c f", p=P),
                    waits=(entry, lnpeT_done))
    tok_bv = kb.dma("Y", bvb, bv_b_in, waits=(entry, lnpeT_done))
    v_toks = []
    for tt in range(NT):
        for half in range(2):
            bank = 2 + (tt * 2 + half) % 2
            stat = [lnpeT3[:, c, tt * P:(tt + 1) * P] for c in range(FC)]
            mov = [wvT4[:, c, half * 512:(half + 1) * 512] for c in range(FC)]
            tok_m = ctx.mm(bank, stat, mov, waits=(tok_wv, lnpeT_done))
            tok_e = ctx.evac("V", bank, lambda v, tt=tt, half=half, bank=bank:
                             v.tensor_add(vv3[:, tt, half * 512:(half + 1) * 512],
                                          ctx.pb[bank][:, 0:512],
                                          bvb[:, half * 512:(half + 1) * 512]),
                             waits=(tok_bv,))
            v_toks.append(tok_e)
    v_done = v_toks[-1]

    wk_tok = None
    proj_done = None
    for wT_in, dT3, bcm, btok in ((wkT_in, kT3, bkcm, tok_bk),
                                  (wqT_in, qT3, bqcm, tok_bq)):
        for oc in range(FC):
            src = wT_in[:, oc * P:(oc + 1) * P].rearrange("(c p) n -> p c n", p=P)
            tok_w = kb.dma("G", wqk.rearrange("p (c s) -> p c s", c=FC), src,
                           waits=(wk_tok, entry))
            for tn in range(SEQ // 512):
                bank = tn % 2
                stat = [wqk[:, c * P:(c + 1) * P] for c in range(FC)]
                mov = [lnpeT3[:, c, tn * 512:(tn + 1) * 512] for c in range(FC)]
                tok_m = ctx.mm(bank, stat, mov, waits=(tok_w, lnpeT_done, v_done))
                tok_e = ctx.evac("S", bank, lambda s_, oc=oc, tn=tn, bank=bank,
                                 dT3=dT3, bcm=bcm: s_.activation(
                    dT3[:, oc, tn * 512:(tn + 1) * 512], ctx.pb[bank][:, 0:512],
                    AF.Identity, bias=bcm[:, oc:oc + 1], scale=1.0), waits=(btok,))
                wk_tok = tok_e
                proj_done = tok_e

    if stop_after == "proj":
        ar.pop()
        return proj_done

    tok_gm = kb.dma("Y", gamb, gam_b_in, waits=(proj_done,))
    tok_gm2 = kb.op("V", lambda v: v.tensor_scalar(gamb, gamb, s_gs, None, ALU.mult),
                    waits=(tok_gm,))
    tok_bt = kb.dma("Y", betb, bet_b_in, waits=(proj_done,))
    tok_bo = kb.dma("Y", bob, bo_b_in, waits=(proj_done,))

    wo_tok = {}
    x_done = None

    def tail_l1(hh_state):
        h, ptile, ex_toks, tok_r, bs, otile = hh_state
        av_tok = None
        for g in range(2):
            bank = bs + g
            pairs = [(i * P, ptile[:, (g * 8 + i) * P:(g * 8 + i + 1) * P])
                     for i in range(8)]
            tok_t = ctx.transposes(bank, pairs, waits=tuple(ex_toks))
            if g == 0:
                tok_pte = ctx.evac("V", bank, lambda v, bank=bank: v.tensor_copy(
                    ptT, ctx.pbb[bank][:, 0:8 * P]))
            else:
                tok_pte = ctx.evac("S", bank, lambda s_, bank=bank: s_.copy(
                    ptT, ctx.pbb[bank][:, 0:8 * P]))
            stat = [ptT[:, i * P:(i + 1) * P] for i in range(8)]
            mov = [vv3[:, g * 8 + i, h * DK:(h + 1) * DK] for i in range(8)]
            av_tok = ctx.mm(bs + 2, stat, mov, waits=(tok_pte, v_done), n=DK,
                            start=(g == 0), stop=(g == 1))
        return ctx.evac("V", bs + 2, lambda v, h=h, otile=otile: v.tensor_scalar(
            otile[:, h * DK:(h + 1) * DK], ctx.pb[bs + 2][:, 0:DK],
            stats[:, 60 + h % 2:61 + h % 2], s_vf, ALU.mult, ALU.mult),
            waits=(tok_r,))

    pe_tok = None
    pe_dma = (kb.dma("Y", A0b, ang0_in[:, 0, :], waits=(proj_done,)),
              kb.dma("Y", S3b, ang0_in[:, 0, :], waits=(proj_done,)))
    qt_tail = None
    for qt in range(NT):
        xap = x_sb[:, qt * D:(qt + 1) * D]
        xnt6 = xnt6_2[qt % 2]

        def fn_ln1(v, qt=qt, xap=xap, xnt6=xnt6):
            return v.tensor_scalar(xnt6, xap, mr[:, 2 * qt:2 * qt + 1],
                                   mr[:, 2 * qt + 1:2 * qt + 2],
                                   ALU.subtract, ALU.mult)
        tok_a1 = kb.op("V", fn_ln1, waits=(pe_tok, proj_done))
        pe_tok = emit_pe_add3(ctx, xnt6, A0b, S3b, 7, s_pe, pe_dma, tok_a1,
                              ctx.pb_tok[7])
        ctx.pb_tok[7] = pe_tok
        ctx.pb_wtok[7] = pe_tok
        if qt + 1 < NT:
            pe_dma = (kb.dma("Y", A0b, ang0_in[:, qt + 1, :], waits=(pe_tok,)),
                      kb.dma("Y", S3b, ang0_in[:, qt + 1, :], waits=(pe_tok,)))
        hstate = None
        for h in range(NH):
            hp, hh = h // 2, h % 2
            ptile = ptile2[h % 2]
            bs = 4 * (h % 2)
            mb = 32 + 12 * (h % 2)
            sc_toks = []
            for kc4 in range(4):
                bank = bs + kc4
                stat = [qT3[hh * 64:(hh + 1) * 64, hp, qt * P:(qt + 1) * P]]
                mov = [kT3[hh * 64:(hh + 1) * 64, hp, kc4 * 512:(kc4 + 1) * 512]]
                sc_toks.append(ctx.mm(bank, stat, mov, waits=(proj_done,)))
            tok_e = kb.op("S", lambda s, bs=bs, ptile=ptile, mb=mb: s.activation(
                ptile, ctx.pbig[:, bs * 512:(bs + 4) * 512],
                AF.Exp, bias=0.0, scale=s_ls,
                accum_out=stats[:, mb + 10:mb + 11]),
                waits=tuple(sc_toks) + (tok_rsb, tok_gm2, tok_bt, tok_bo))
            for kc4 in range(4):
                ctx.pb_tok[bs + kc4] = tok_e
            ex_toks = [tok_e]
            tok_r = kb.op("V", lambda v, h=h, mb=mb: v.reciprocal(
                stats[:, 60 + h % 2:61 + h % 2], stats[:, mb + 10:mb + 11]),
                waits=(tok_e,))
            if h == 2 and qt_tail is not None:
                x_done = qt_tail()
                qt_tail = None
            if hstate is not None:
                tail_l1(hstate)
            hstate = (h, ptile, ex_toks, tok_r, bs, otile2[qt % 2])
        tok_o = tail_l1(hstate)

        def mk_tail(qt=qt, tok_o=tok_o, xap=xap, xnt6=xnt6, pe_tok=pe_tok):
            otile, otT = otile2[qt % 2], otT2[qt % 2]

            def emit():
                pairs = [(c * P, otile[:, c * P:(c + 1) * P]) for c in range(FC)]
                ctx.transposes(3, pairs, waits=(tok_o,))
                tok_e = ctx.evac("V", 3, lambda v: v.tensor_copy(
                    otT, ctx.pbb[3][:, 0:FC * P]))
                u_evac = []
                for half in range(2):
                    bank = half
                    for c in range(FC):
                        wb = woT_buf[(c % 3) * 2 + half]
                        tok_w = kb.dma(
                            "G", wb,
                            woT_in[c * P:(c + 1) * P, half * 512:(half + 1) * 512],
                            waits=(wo_tok.get((c % 3, half)), proj_done))
                        tok_m = ctx.mm(bank, [otT[:, c * P:(c + 1) * P]], [wb],
                                       waits=(tok_e, tok_w), start=(c == 0),
                                       stop=(c == FC - 1))
                        wo_tok[(c % 3, half)] = tok_m
                    u_evac.append((half, bank))
                u_toks = []
                for half, bank in u_evac:
                    tok_u = ctx.evac("V", bank, lambda v, half=half, bank=bank:
                                     v.tensor_add(
                        xnt6[:, half * 512:(half + 1) * 512],
                        xnt6[:, half * 512:(half + 1) * 512],
                        ctx.pb[bank][:, 0:512]), waits=(pe_tok,))
                    u_toks.append(tok_u)
                tok_u2 = kb.op("V", lambda v: v.tensor_add(xnt6, xnt6, bob),
                               waits=tuple(u_toks) + (tok_bo,))

                def fn_bn(v):
                    v.bn_stats(stats[:, 8:14], xnt6[:, 0:512])
                    v.bn_aggr(stats[:, 20:22], stats[:, 8:14])
                    v.bn_stats(stats[:, 8:14], xnt6[:, 512:1024])
                    v.bn_aggr(stats[:, 22:24], stats[:, 8:14])
                    v.tensor_add(stats[:, 0:1], stats[:, 20:21], stats[:, 22:23])
                    v.tensor_scalar(stats[:, 0:1], stats[:, 0:1], 0.5, None,
                                    ALU.mult)
                    v.tensor_sub(stats[:, 6:7], stats[:, 20:21], stats[:, 22:23])
                    v.tensor_scalar(stats[:, 6:7], stats[:, 6:7], stats[:, 6:7],
                                    0.25, ALU.mult, ALU.mult)
                    v.tensor_add(stats[:, 1:2], stats[:, 21:22], stats[:, 23:24])
                    v.tensor_scalar(stats[:, 1:2], stats[:, 1:2], 0.5, None,
                                    ALU.mult)
                    return v.tensor_add(stats[:, 1:2], stats[:, 1:2], stats[:, 6:7])
                tok_v = kb.op("V", fn_bn, waits=(tok_u2,))

                def fn_rstd(s):
                    s.activation(stats[:, 2:3], stats[:, 1:2], AF.Ln, bias=s_eps,
                                 scale=1.0)
                    return s.activation(stats[:, 3:4], stats[:, 2:3], AF.Exp,
                                        bias=0.0, scale=-0.5)
                tok_rs = kb.op("S", fn_rstd, waits=(tok_v, tok_rsb))

                def fn_z(v, xap=xap):
                    v.tensor_scalar(xnt6, xnt6, stats[:, 0:1], stats[:, 3:4],
                                    ALU.subtract, ALU.mult)
                    v.tensor_mul(xnt6, xnt6, gamb)
                    v.tensor_add(xnt6, xnt6, betb)
                    return v.tensor_add(xap, xap, xnt6)
                return kb.op("V", fn_z, waits=(tok_rs, tok_gm2, tok_bt))
            return emit
        if qt == NT - 1:
            x_done = mk_tail()()
        else:
            qt_tail = mk_tail()

    ar.pop()
    return x_done



def emit_l1_ffn(ctx, x_sb, x_tok, wqkv_in, qs_b_in, ks_b_in, wo_in, lg_b_in,
                lb_b_in, w1_in, b1_b_in, w2_in, b2_b_in, rot_in, mask_in,
                xspill, entry=None, stop_after=None):
    kb, ar = ctx.kb, ctx.arena
    ar.push()
    stats0 = ar.alloc_f32(32)
    sq0 = ar.alloc_b16(D)
    tok_sp = kb.dma("Y", xspill.rearrange("(t p) f -> p t f", p=P),
                    x_sb.rearrange("p (t f) -> p t f", t=NT),
                    waits=(x_tok, entry))
    prev_s = None
    ln_toks = []
    for tt in range(NT):
        xap = x_sb[:, tt * D:(tt + 1) * D]

        def cb(mean, rstd, tok_s, xap=xap):
            tok_a = kb.op("V", lambda v, xap=xap: v.tensor_scalar(
                xap, xap, mean, rstd, ALU.subtract, ALU.mult),
                waits=(tok_s, tok_sp))
            ln_toks.append(tok_a)
        prev_s = emit_ln(ctx, stats0, xap, cb, waits=(x_tok, entry, tok_sp),
                         prev_s=prev_s, sq_scratch=sq0)
    ar.pop()
    done = ln_toks[-1]
    for layer in range(2):
        done = _l1f_layer(ctx, x_sb, done, layer, wqkv_in, qs_b_in, ks_b_in,
                          wo_in, lg_b_in, lb_b_in, w1_in, b1_b_in, w2_in,
                          b2_b_in, rot_in, mask_in, done, stop_after)
        if stop_after is not None:
            return done
    ar.push()
    xb = ar.alloc_f32(D)
    prev = None
    for tt in range(NT):
        tok_l = kb.dma("Y", xb, xspill[tt * P:(tt + 1) * P, :],
                       waits=(done, prev))
        prev = kb.op("V", lambda v, tt=tt: v.tensor_add(
            x_sb[:, tt * D:(tt + 1) * D], x_sb[:, tt * D:(tt + 1) * D], xb),
            waits=(tok_l, done))
    ar.pop()
    return prev


def _l1f_layer(ctx, x_sb, x_tok, layer, wqkv_in, qs_b_in, ks_b_in, wo_in,
               lg_b_in, lb_b_in, w1_in, b1_b_in, w2_in, b2_b_in, rot_in,
               mask_in, entry, stop_after):
    kb, ar = ctx.kb, ctx.arena
    NHH, DKH, INNER, HID, CH = 8, 64, 512, 2730, 455
    ar.push()
    qkv = ar.alloc_b16(NT * 1536)
    xTr = ar.alloc_b16(NT * D)
    scr = ar.alloc_b16(16 * 1024)
    sh = ar.alloc_f32(D)
    shb = ar.alloc_b16(D)
    stats = ar.alloc_f32(64)
    qsb = ar.alloc_f32(DKH)
    ksb = ar.alloc_f32(DKH)
    scr2 = ar.alloc_b16(2048)
    qkv3 = qkv.rearrange("p (t f) -> p t f", t=NT)
    xT3 = xTr.rearrange("p (c s) -> p c s", c=FC)
    qrT = xTr[:, 0:4 * SEQ]
    krT = xTr[:, 4 * SEQ:8 * SEQ]
    qrT3 = qrT.rearrange("p (c s) -> p c s", c=4)
    krT3 = krT.rearrange("p (c s) -> p c s", c=4)
    wqkv_sb = scr[:, 0:FC * 1536]
    sqs = scr[:, 0:1024].bitcast(F32)
    qh = scr[:, 1024:2048].bitcast(F32)
    qr_t = scr[:, 2048:3072].bitcast(F32)
    rtile = scr[:, 3072:3584].bitcast(F32)
    norms = scr[:, 3584:3600].bitcast(F32)
    qrb = scr[:, 4096:4608]
    mask_buf = [scr[:, 4096:6144].bitcast(F32), scr[:, 6144:8192].bitcast(F32)]
    sf32_2 = [scr[:, 8192:10240].bitcast(F32), scr2.bitcast(F32)]
    ptile2_c = [scr[:, 10240:11264], scr[:, 13312:14336]]
    ptT2_c = [scr[:, 11264:12288], scr[:, 14336:15360]]
    otile2_c = [scr[:, 12288:12800], scr[:, 15360:15872]]
    otT2_c = [scr[:, 12800:13312], scr[:, 15872:16384]]
    wo_sb = scr[:, 0:4 * 1024]
    wo4 = wo_sb.rearrange("p (c f) -> p c f", c=4)

    W = 512

    sh_toks = []
    tok_wq = kb.dma("G", wqkv_sb.rearrange("p (c f) -> p c f", c=FC),
                    wqkv_in[layer].rearrange("(c p) f -> p c f", p=P),
                    waits=(entry,))
    tok_qs = kb.dma("Y", qsb, qs_b_in[layer], waits=(entry,))
    tok_ks = kb.dma("Y", ksb, ks_b_in[layer], waits=(entry,))
    qkv_toks = []
    prev_sh = [None, None]
    for tt in range(NT):
        tok_c1 = kb.op("V", lambda v, tt=tt: v.tensor_copy(
            sh[:, 0:W], x_sb[:, tt * D:tt * D + W]),
            waits=(x_tok, entry, prev_sh[0]))
        if tt == 0:
            tok_c2 = kb.op("V", lambda v: v.memset(sh[0:1, W:D], 0.0),
                           waits=(prev_sh[1],))
            tok_c3 = kb.dma("G", sh[1:P, W:D], x_sb[0:P - 1, W:D],
                            waits=(x_tok, entry, tok_c2))
        else:
            tok_c2 = kb.dma("G", sh[0:1, W:D],
                            x_sb[P - 1:P, (tt - 1) * D + W:(tt - 1) * D + D],
                            waits=(x_tok, prev_sh[1]))
            tok_c3 = kb.dma("G", sh[1:P, W:D],
                            x_sb[0:P - 1, tt * D + W:tt * D + D],
                            waits=(x_tok, tok_c2))
        tok_cast = kb.op("V", lambda v: v.tensor_copy(shb, sh),
                         waits=(tok_c1, tok_c3))
        prev_sh[0] = tok_cast
        prev_sh[1] = tok_cast
        bank = 6 + tt % 2
        pairs = [(c * P, shb[:, c * P:(c + 1) * P]) for c in range(FC)]
        tok_t = ctx.transposes(bank, pairs, waits=(tok_cast, ctx.c["load_tok"]))
        tok_e = ctx.evac("V", bank, lambda v, tt=tt, bank=bank: v.tensor_copy(
            xT3[:, :, tt * P:(tt + 1) * P],
            ctx.pbb[bank][:, 0:FC * P].rearrange("p (c s) -> p c s", s=P)))
        sh_toks.append(tok_e)
        for ch in range(3):
            bank = (tt % 2) * 3 + ch
            stat = [xT3[:, c, tt * P:(tt + 1) * P] for c in range(FC)]
            mov = [wqkv_sb.rearrange("p (c f) -> p c f", c=FC)[
                :, c, ch * 512:(ch + 1) * 512] for c in range(FC)]
            tok_m = ctx.mm(bank, stat, mov, waits=(tok_e, tok_wq))
            if ch % 2 == 0:
                tok_q = ctx.evac("V", bank, lambda v, tt=tt, ch=ch, bank=bank:
                                 v.tensor_copy(qkv3[:, tt, ch * 512:(ch + 1) * 512],
                                               ctx.pb[bank][:, 0:512]))
            else:
                tok_q = ctx.evac("S", bank, lambda s_, tt=tt, ch=ch, bank=bank:
                                 s_.copy(qkv3[:, tt, ch * 512:(ch + 1) * 512],
                                         ctx.pb[bank][:, 0:512]))
            qkv_toks.append(tok_q)
    qkv_done = qkv_toks[-1]
    if stop_after == "qkv":
        ctx.dbg = dict(qkv=qkv)
        ar.pop()
        return qkv_done

    rot_toks = []
    prev_r = None
    for tt in range(NT):
        tok_rt = kb.dma("Y", rtile.rearrange("p (r k) -> p r k", r=4),
                        rot_in.rearrange("r (t p) k -> p t r k", p=P)[:, tt, :, :],
                        waits=(prev_r, qkv_done))
        for qk in range(2):
            base = qk * 512
            srcap = qkv3[:, tt, base:base + 512]
            tok_sq = kb.op("S", lambda s, srcap=srcap: s.activation(
                sqs, srcap, AF.Square, bias=0.0, scale=1.0),
                waits=(qkv_done, prev_r))

            def fn_ss(v, tt=tt):
                last = None
                for h in range(NHH):
                    last = v.reduce_sum(stats[:, 16 + h:17 + h],
                                        sqs[:, h * DKH:(h + 1) * DKH], axis=AX.X)
                last = v.tensor_scalar(stats[:, 16:24], stats[:, 16:24], 1e-24,
                                       None, ALU.max)
                return last
            tok_ss = kb.op("V", fn_ss, waits=(tok_sq,))

            def fn_rn(s):
                s.activation(stats[:, 24:32], stats[:, 16:24], AF.Ln, bias=0.0,
                             scale=1.0)
                return s.activation(stats[:, 24:32], stats[:, 24:32], AF.Exp,
                                    bias=0.0, scale=-0.5)
            tok_rn = kb.op("S", fn_rn, waits=(tok_ss,))
            def fn_qh(v, srcap=srcap, qk=qk):
                last = None
                for h in range(NHH):
                    last = v.tensor_scalar(qh[:, h * DKH:(h + 1) * DKH],
                                           srcap[:, h * DKH:(h + 1) * DKH],
                                           stats[:, 24 + h:25 + h], None, ALU.mult)
                return last
            tok_qh = kb.op("V", fn_qh, waits=(tok_rn,))
            scb = qsb if qk == 0 else ksb
            sc_tok = tok_qs if qk == 0 else tok_ks
            qh3 = qh.rearrange("p (h k) -> p h k", h=NHH)
            qr3 = qr_t.rearrange("p (h k) -> p h k", h=NHH)
            cs = rtile[:, qk * 128:qk * 128 + 64]
            ss = rtile[:, qk * 128 + 64:qk * 128 + 128]

            def fn_rot2(v, qh3=qh3, qr3=qr3, cs=cs, ss=ss, scb=scb):
                scb3 = scb.rearrange("p (o k) -> p o k", o=1).broadcast_to(
                    (P, NHH, DKH))
                cs3 = cs.rearrange("p (o k) -> p o k", o=1).broadcast_to(
                    (P, NHH, DKH))
                ss3 = ss.rearrange("p (o k) -> p o k", o=1).broadcast_to(
                    (P, NHH, DKH))
                v.tensor_mul(qh3, qh3, scb3)
                v.tensor_mul(qr3, qh3, cs3)
                v.tensor_mul(sqs.rearrange("p (h k) -> p h k", h=NHH)[:, :, 0:32],
                             qh3[:, :, 32:64], ss3[:, :, 0:32])
                v.tensor_mul(sqs.rearrange("p (h k) -> p h k", h=NHH)[:, :, 32:64],
                             qh3[:, :, 0:32], ss3[:, :, 32:64])
                v.tensor_sub(qr3[:, :, 0:32], qr3[:, :, 0:32],
                             sqs.rearrange("p (h k) -> p h k", h=NHH)[:, :, 0:32])
                return v.tensor_add(
                    qr3[:, :, 32:64], qr3[:, :, 32:64],
                    sqs.rearrange("p (h k) -> p h k", h=NHH)[:, :, 32:64])
            tok_rot = kb.op("V", fn_rot2, waits=(tok_qh, tok_rt, sc_tok))
            tok_cb = kb.op("V", lambda v: v.tensor_copy(qrb, qr_t),
                           waits=(tok_rot,))
            bank = 6 + qk
            pairs = [(c * P, qrb[:, c * P:(c + 1) * P]) for c in range(4)]
            tok_t = ctx.transposes(bank, pairs, waits=(tok_cb,))
            dstT = qrT3 if qk == 0 else krT3
            tok_e = ctx.evac("V", bank, lambda v, tt=tt, bank=bank, dstT=dstT:
                             v.tensor_copy(
                dstT[:, :, tt * P:(tt + 1) * P],
                ctx.pbb[bank][:, 0:4 * P].rearrange("p (c s) -> p c s", s=P)))
            rot_toks.append(tok_e)
            prev_r = tok_e
    rot_done = rot_toks[-1]
    if stop_after == "rot":
        ctx.dbg = dict(qrT=qrT, krT=krT, qkv=qkv)
        ar.pop()
        return rot_done

    tok_wo = kb.dma("G", wo4, wo_in[layer].rearrange("(c p) f -> p c f", p=P),
                    waits=(rot_done,))
    QKS = 8.0
    x_done = None
    m_toks = {}
    war_sf = [None, None]
    war_pt = [None, None]
    qt_tail = None

    def tail_c(hh_state):
        h, par, ex_toks, tok_r, otile_p, w0_, kb0_ = hh_state
        ptile_p, ptT_p = ptile2_c[par], ptT2_c[par]
        tb, ab = 4 + 2 * par, 5 + 2 * par
        pairs = [(i * P, ptile_p[:, i * P:(i + 1) * P]) for i in range(8)]
        tok_t = ctx.transposes(tb, pairs, waits=tuple(ex_toks))
        war_pt[par] = tok_t
        tok_pte = ctx.evac("V", tb, lambda v, tb=tb, ptT_p=ptT_p: v.tensor_copy(
            ptT_p, ctx.pbb[tb][:, 0:8 * P]))
        cstart = 0 if w0_ > 0 else 4
        stat = [ptT_p[:, i * P:(i + 1) * P] for i in range(cstart, 8)]
        mov = [qkv3[:, kb0_ // P + i, 1024 + h * DKH:1024 + (h + 1) * DKH]
               for i in range(cstart, 8)]
        ctx.mm(ab, stat, mov, waits=(tok_pte, qkv_done), n=DKH)
        return ctx.evac("V", ab, lambda v, h=h, ab=ab, par=par, otile_p=otile_p:
                        v.tensor_scalar(
            otile_p[:, h * DKH:(h + 1) * DKH], ctx.pb[ab][:, 0:DKH],
            stats[:, 60 + par:61 + par], None, ALU.mult), waits=(tok_r,))

    for qt in range(NT):
        w0 = qt // 4
        kb0 = (w0 - 1) * 512
        valid_g = (0, 1) if w0 > 0 else (1,)
        mb_m = mask_buf[qt % 2]
        tok_mk = kb.dma("Y", mb_m, mask_in[qt], waits=(m_toks.get(qt % 2), rot_done))
        hstate = None
        last_s = None
        for h in range(NHH):
            hp, hh = h // 2, h % 2
            par = h % 2
            sb = 2 * par
            cb = 16 + 12 * par
            sf = sf32_2[par]
            ptile_p = ptile2_c[par]
            ms_toks = []
            for g in valid_g:
                bank = sb + g
                stat = [qrT3[hh * 64:(hh + 1) * 64, hp, qt * P:(qt + 1) * P]]
                kbase = kb0 + g * 512
                mov = [krT3[hh * 64:(hh + 1) * 64, hp, kbase:kbase + 512]]
                tok_m = ctx.mm(bank, stat, mov, waits=(rot_done,))
                tok_s = ctx.evac("V", bank, lambda v, g=g, bank=bank, mb_m=mb_m,
                                 sf=sf: v.tensor_add(
                    sf[:, g * 512:(g + 1) * 512], ctx.pb[bank][:, 0:512],
                    mb_m[:, g * 512:(g + 1) * 512]),
                    waits=(tok_mk, war_sf[par]))
                ms_toks.append(tok_s)
                last_s = tok_s
            ex_toks = []
            if w0 == 0:
                ex_toks.append(kb.op("S", lambda s, ptile_p=ptile_p: s.memzero(
                    ptile_p[:, 0:512]), waits=(war_pt[par],)))
            for g in valid_g:
                tok_e = kb.op("S", lambda s, g=g, sf=sf, cb=cb, ptile_p=ptile_p:
                              s.activation(
                    ptile_p[:, g * 512:(g + 1) * 512],
                    sf[:, g * 512:(g + 1) * 512],
                    AF.Exp, bias=0.0, scale=QKS,
                    accum_out=stats[:, cb + 6 + g:cb + 7 + g]),
                    waits=tuple(ms_toks) + (war_pt[par],))
                ex_toks.append(tok_e)
            war_sf[par] = ex_toks[-1]
            if w0 > 0:
                tok_r = kb.op("V", lambda v, cb=cb: v.tensor_add(
                    stats[:, cb + 10:cb + 11], stats[:, cb + 6:cb + 7],
                    stats[:, cb + 7:cb + 8]), waits=tuple(ex_toks))
            else:
                tok_r = kb.op("V", lambda v, cb=cb: v.tensor_copy(
                    stats[:, cb + 10:cb + 11], stats[:, cb + 7:cb + 8]),
                    waits=tuple(ex_toks))
            tok_r = kb.op("V", lambda v, cb=cb, par=par: v.reciprocal(
                stats[:, 60 + par:61 + par], stats[:, cb + 10:cb + 11]),
                waits=(tok_r,))
            if h == 2 and qt_tail is not None:
                x_done = qt_tail()
                qt_tail = None
            if hstate is not None:
                tail_c(hstate)
            hstate = (h, par, ex_toks, tok_r, otile2_c[qt % 2], w0, kb0)
        tok_o = tail_c(hstate)
        m_toks[qt % 2] = last_s

        def mk_tail(qt=qt, tok_o=tok_o):
            otile_p, otT_p = otile2_c[qt % 2], otT2_c[qt % 2]

            def emit():
                pairs = [(c * P, otile_p[:, c * P:(c + 1) * P]) for c in range(4)]
                ctx.transposes(6, pairs, waits=(tok_o,))
                tok_e = ctx.evac("V", 6, lambda v: v.tensor_copy(
                    otT_p, ctx.pbb[6][:, 0:4 * P]))
                xd = None
                for half in range(2):
                    bank = 5 if half == 0 else 7
                    stat = [otT_p[:, c * P:(c + 1) * P] for c in range(4)]
                    mov = [wo4[:, c, half * 512:(half + 1) * 512] for c in range(4)]
                    ctx.mm(bank, stat, mov, waits=(tok_e, tok_wo))
                    xap = x_sb[:, qt * D + half * 512:qt * D + (half + 1) * 512]
                    xd = ctx.evac("V", bank, lambda v, xap=xap, bank=bank:
                                  v.tensor_add(xap, xap, ctx.pb[bank][:, 0:512]))
                return xd
            return emit
        if qt == NT - 1:
            x_done = mk_tail()()
        else:
            qt_tail = mk_tail()
    attn_done = x_done
    if stop_after == "attn":
        ar.pop()
        return attn_done

    ar.pop()
    ar.push()
    xnT = ar.alloc_b16(NT * D)
    b1b = ar.alloc_f32(2 * HID)
    lgb = ar.alloc_f32(D)
    lbb = ar.alloc_f32(D)
    b2b = ar.alloc_f32(D)
    w1buf = ar.alloc_b16(FC * 2 * CH)
    w2buf = ar.alloc_b16(4 * D)
    Aa = ar.alloc_f32(CH)
    Bg = ar.alloc_f32(CH)
    Cg = ar.alloc_f32(CH)
    h4 = ar.alloc_b16(CH)
    h4T = ar.alloc_b16(4 * P)
    sh2 = ar.alloc_f32(D)
    sh2b = ar.alloc_b16(D)
    stats2 = ar.alloc_f32(32)
    xnT3 = xnT.rearrange("p (c s) -> p c s", c=FC)
    w1b3 = w1buf.rearrange("p (c f) -> p c f", c=FC)

    tok_b1 = kb.dma("Y", b1b, b1_b_in[layer], waits=(attn_done,))
    tok_lg = kb.dma("Y", lgb, lg_b_in[layer], waits=(attn_done,))
    tok_lb = kb.dma("Y", lbb, lb_b_in[layer], waits=(attn_done,))
    tok_b2 = kb.dma("Y", b2b, b2_b_in[layer], waits=(attn_done,))

    xn_toks = []
    prev = [None, None]
    prev_s = None
    for tt in range(NT):
        tok_c1 = kb.op("V", lambda v, tt=tt: v.tensor_copy(
            sh2[:, 0:512], x_sb[:, tt * D:tt * D + 512]),
            waits=(attn_done, prev[0]))
        if tt == 0:
            tok_c2 = kb.op("V", lambda v: v.memset(sh2[0:1, 512:D], 0.0),
                           waits=(prev[1],))
            tok_c3 = kb.dma("G", sh2[1:P, 512:D], x_sb[0:P - 1, 512:D],
                            waits=(attn_done, tok_c2))
        else:
            tok_c2 = kb.dma("G", sh2[0:1, 512:D],
                            x_sb[P - 1:P, (tt - 1) * D + 512:(tt - 1) * D + D],
                            waits=(attn_done, prev[1]))
            tok_c3 = kb.dma("G", sh2[1:P, 512:D],
                            x_sb[0:P - 1, tt * D + 512:tt * D + D],
                            waits=(attn_done, tok_c2))
        sh_ready = (tok_c1, tok_c3)

        def cb(mean, rstd, tok_s, tt=tt):
            def fn_ap(v):
                v.tensor_scalar(sh2, sh2, mean, rstd, ALU.subtract, ALU.mult)
                v.tensor_mul(sh2, sh2, lgb)
                v.tensor_add(sh2, sh2, lbb)
                return v.tensor_copy(sh2b, sh2)
            tok_a = kb.op("V", fn_ap, waits=(tok_s, tok_lg, tok_lb))
            bank = 6 + tt % 2
            pairs = [(c * P, sh2b[:, c * P:(c + 1) * P]) for c in range(FC)]
            tok_t = ctx.transposes(bank, pairs, waits=(tok_a,))
            tok_e = ctx.evac("V", bank, lambda v, tt=tt, bank=bank: v.tensor_copy(
                xnT3[:, :, tt * P:(tt + 1) * P],
                ctx.pbb[bank][:, 0:FC * P].rearrange("p (c s) -> p c s", s=P)))
            xn_toks.append(tok_e)
            prev[0] = tok_e
            prev[1] = tok_e
        prev_s = emit_ln(ctx, stats2, sh2, cb, waits=sh_ready, prev_s=prev_s,
                         sq_scratch=sh2b)
    xn_done = xn_toks[-1]

    w1_tok = None
    w2_tok = None
    x_done2 = xn_done
    prev_h4 = None
    prev_ht = None
    for j in range(6):
        tok_w1a = kb.dma("G",
                         w1b3[:, :, 0:CH],
                         w1_in[layer, :, j * CH:(j + 1) * CH].rearrange(
                             "(c p) f -> p c f", p=P), waits=(w1_tok, xn_done))
        tok_w1g = kb.dma("G",
                         w1b3[:, :, CH:2 * CH],
                         w1_in[layer, :, HID + j * CH:HID + (j + 1) * CH].rearrange(
                             "(c p) f -> p c f", p=P), waits=(w1_tok, xn_done))
        rb_sizes = (P, P, P, CH - 3 * P)
        roff = 0
        w2aps = []
        w2l_toks = []
        for bi, rsz in enumerate(rb_sizes):
            w2l_toks.append(kb.dma("G", w2buf.rearrange("p (b f) -> p b f", b=4)[
                0:rsz, bi, :],
                w2_in[layer, j * CH + roff:j * CH + roff + rsz, :],
                waits=(w2_tok, xn_done)))
            w2aps.append((rsz, w2buf.rearrange("p (b f) -> p b f", b=4)[0:rsz, bi, :]))
            roff += rsz
        for tt in range(NT):
            stat = [xnT3[:, c, tt * P:(tt + 1) * P] for c in range(FC)]
            movA = [w1b3[:, c, 0:CH] for c in range(FC)]
            movG = [w1b3[:, c, CH:2 * CH] for c in range(FC)]
            tok_mA = ctx.mm(0, stat, movA, waits=(tok_w1a, xn_done))
            tok_mG = ctx.mm(1, stat, movG, waits=(tok_w1g, xn_done))
            tok_a = ctx.evac("V", 0, lambda v, j=j: v.tensor_add(
                Aa, ctx.pb[0][:, 0:CH],
                b1b[:, j * CH:(j + 1) * CH]), waits=(tok_b1, prev_h4))
            tok_g = ctx.evac("V", 1, lambda v, j=j: v.tensor_add(
                Bg, ctx.pb[1][:, 0:CH],
                b1b[:, HID + j * CH:HID + (j + 1) * CH]), waits=(tok_b1, prev_h4))
            tok_gl = kb.op("S", lambda s: s.activation(
                Cg, Bg, AF.Gelu, bias=0.0, scale=1.0), waits=(tok_g,))
            tok_h4 = kb.op("G", lambda g: g.tensor_mul(h4, Aa, Cg),
                           waits=(tok_gl, tok_a, prev_ht))
            prev_h4 = tok_h4
            pairs = []
            roff = 0
            for bi, rsz in enumerate(rb_sizes):
                pairs.append((bi * P, h4[:, roff:roff + rsz]))
                roff += rsz
            tok_t = ctx.transposes(2, pairs, waits=(tok_h4,))
            prev_ht = tok_t

            def fn_he(s_, rb_sizes=rb_sizes):
                last = None
                for bi, rsz in enumerate(rb_sizes):
                    last = s_.copy(h4T[0:rsz, bi * P:bi * P + P],
                                   ctx.pbb[2][0:rsz, bi * P:bi * P + P])
                return last
            tok_he = ctx.evac("S", 2, fn_he)
            for half in range(2):
                bank = 3 + (tt % 2) * 2 + half
                stat2 = []
                mov2 = []
                for bi, rsz in enumerate(rb_sizes):
                    stat2.append(h4T[0:rsz, bi * P:bi * P + P])
                    mov2.append(w2aps[bi][1][:, half * 512:(half + 1) * 512])
                tok_m2 = ctx.mm(bank, stat2, mov2,
                                waits=(tok_he,) + tuple(w2l_toks))
                xap = x_sb[:, tt * D + half * 512:tt * D + (half + 1) * 512]
                x_done2 = ctx.evac("V", bank, lambda v, xap=xap, bank=bank:
                                   v.tensor_add(xap, xap, ctx.pb[bank][:, 0:512]),
                                   waits=(x_done2,))
            w1_tok = tok_mG if tt == NT - 1 else w1_tok
            w2_tok = tok_m2 if tt == NT - 1 else w2_tok
    fin = x_done2
    for tt in range(NT):
        fin = kb.op("V", lambda v, tt=tt: v.tensor_add(
            x_sb[:, tt * D:(tt + 1) * D], x_sb[:, tt * D:(tt + 1) * D], b2b),
            waits=(fin, tok_b2))
    ar.pop()
    return fin




def build_program(n_batch=NB_PER_CORE, blocks=("l0a",), heap_kib=None, dbg_dump=(), stop_after=None):
    nc = bass.Bass("TRN2", target_bir_lowering=False)
    kb = KB(nc)
    ar = Arena(nc, "heap", heap_kib)
    pbig = nc.alloc_psum_tensor("pbig", [P, 4096], F32).ap()
    pb = [pbig[:, i * 512:(i + 1) * 512] for i in range(8)]

    x_in = nc.dram_tensor("x", [n_batch, SEQ, D], F32, kind="ExternalInput").ap()
    y_out = nc.dram_tensor("y", [n_batch, SEQ, D], F32, kind="ExternalOutput").ap()
    ident_in = nc.dram_tensor("cst_ident", [P, P], F32, kind="ExternalInput").ap()

    wq_in = wo_in = None
    if "l0a" in blocks:
        wq_in = nc.dram_tensor("L0_attn_w_qkv", [16, 3, D, 64], F32,
                               kind="ExternalInput").ap()
        wo_in = nc.dram_tensor("L0_attn_w_o", [D, D], F32, kind="ExternalInput").ap()
    if "l1a" in blocks:
        wqT_in = nc.dram_tensor("L1_WqT", [D, D], F32, kind="ExternalInput").ap()
        wkT_in = nc.dram_tensor("L1_WkT", [D, D], F32, kind="ExternalInput").ap()
        wvT_in = nc.dram_tensor("L1_WvT", [D, D], F32, kind="ExternalInput").ap()
        woT_in = nc.dram_tensor("L1_WoT", [D, D], F32, kind="ExternalInput").ap()
        bqcm_in = nc.dram_tensor("L1_bq_cm", [P, 8], F32, kind="ExternalInput").ap()
        bkcm_in = nc.dram_tensor("L1_bk_cm", [P, 8], F32, kind="ExternalInput").ap()
        bvb_in = nc.dram_tensor("L1_bv_b", [P, D], F32, kind="ExternalInput").ap()
        bob_in = nc.dram_tensor("L1_bo_b", [P, D], F32, kind="ExternalInput").ap()
        gamb_in = nc.dram_tensor("L1_gam_b", [P, D], F32, kind="ExternalInput").ap()
        betb_in = nc.dram_tensor("L1_bet_b", [P, D], F32, kind="ExternalInput").ap()
        rby_in = nc.dram_tensor("L1_attn_rby", [3], F32, kind="ExternalInput").ap()
        ang0_in = nc.dram_tensor("cst_ang0", [SEQ, 512], F32,
                                 kind="ExternalInput").ap().rearrange(
            "(t p) i -> p t i", p=P)
    if "l1f" in blocks:
        wqkv_in = nc.dram_tensor("L1_ffn_wqkv", [2, D, 1536], F32,
                                 kind="ExternalInput").ap()
        qsb_in = nc.dram_tensor("L1f_qs_b", [2, P, 64], F32,
                                kind="ExternalInput").ap()
        ksb_in = nc.dram_tensor("L1f_ks_b", [2, P, 64], F32,
                                kind="ExternalInput").ap()
        wol_in = nc.dram_tensor("L1_ffn_wo", [2, 512, D], F32,
                                kind="ExternalInput").ap()
        lgb_in = nc.dram_tensor("L1f_lg_b", [2, P, D], F32,
                                kind="ExternalInput").ap()
        lbb_in = nc.dram_tensor("L1f_lb_b", [2, P, D], F32,
                                kind="ExternalInput").ap()
        w1l_in = nc.dram_tensor("L1_ffn_w1", [2, D, 5460], F32,
                                kind="ExternalInput").ap()
        b1b_in = nc.dram_tensor("L1f_b1_b", [2, P, 5460], F32,
                                kind="ExternalInput").ap()
        w2l_in = nc.dram_tensor("L1_ffn_w2", [2, 2730, D], F32,
                                kind="ExternalInput").ap()
        b2b_in = nc.dram_tensor("L1f_b2_b", [2, P, D], F32,
                                kind="ExternalInput").ap()
        rot_in = nc.dram_tensor("cst_rot", [4, SEQ, 64], F32,
                                kind="ExternalInput").ap()
        mask_in = nc.dram_tensor("cst_mask", [NT, P, 1024], F32,
                                 kind="ExternalInput").ap()
        xspill = nc.dram_tensor("x_spill", [SEQ, D], F32, kind="Internal").ap()
    if "l0f" in blocks:
        rw_in = nc.dram_tensor("L0_ffn_router_weight", [8, D], F32,
                               kind="ExternalInput").ap()
        rb_in = nc.dram_tensor("L0_ffn_router_bias", [8], F32,
                               kind="ExternalInput").ap()
        gup_in = nc.dram_tensor("L0_ffn_gate_up_proj", [8, D, 4096], F32,
                                kind="ExternalInput").ap()
        gub_in = nc.dram_tensor("L0_ffn_gate_up_bias", [8, 4096], F32,
                                kind="ExternalInput").ap()
        dp_in = nc.dram_tensor("L0_ffn_down_proj", [8, 2048, D], F32,
                               kind="ExternalInput").ap()
        db_in = nc.dram_tensor("L0_ffn_down_bias", [8, D], F32,
                               kind="ExternalInput").ap()

    consts = {}
    ident_sb = ar.alloc_b16(P)
    consts["ident_b16"] = ident_sb
    t1 = kb.dma("G", ident_sb, ident_in)
    ident_f32 = ar.alloc_f32(P)
    consts["ident_f32"] = ident_f32
    t2 = kb.dma("Y", ident_f32, ident_in)
    cbuf = ar.alloc_f32(8)
    consts["c_eps"] = cbuf[:, 0:1]
    onescol = ar.alloc_f32(1)
    onesrow = ar.alloc_f32(P)
    consts["ones_col"] = onescol
    consts["ones_row"] = onesrow[0:1, 0:P]

    consts["c_pi2"] = cbuf[:, 1:2]

    def fn_c(v):
        v.memset(cbuf[:, 0:1], 1e-5)
        v.memset(cbuf[:, 1:2], PI2)
        v.memset(onescol, 1.0)
        return v.memset(onesrow[0:1, 0:P], 1.0)
    tok_eps = kb.op("V", fn_c, waits=(t1, t2))
    consts["load_tok"] = tok_eps

    x_sb = ar.alloc_f32(NT * D)
    ctx = Ctx(kb, ar, pb, consts, pbig=pbig)

    store_toks = []
    done = tok_eps
    for b in range(n_batch):
        x_tok = kb.dma("Y", x_sb.rearrange("p (t f) -> p t f", t=NT),
                       x_in[b].rearrange("(t p) f -> p t f", p=P),
                       waits=(store_toks[-1] if store_toks else None, done))
        entry = done
        for blk in blocks:
            if blk == "l0a":
                done = emit_l0_attn(ctx, x_sb, x_tok, wq_in, wo_in, entry=entry,
                                    stop_after=stop_after)
            elif blk == "l0f":
                done = emit_l0_ffn(ctx, x_sb, x_tok, rw_in, rb_in, gup_in, gub_in,
                                   dp_in, db_in, entry=entry)
            elif blk == "l1a":
                done = emit_l1_attn(ctx, x_sb, x_tok, wqT_in, wkT_in, wvT_in,
                                    woT_in, bqcm_in, bkcm_in, bvb_in, bob_in,
                                    gamb_in, betb_in, rby_in, ang0_in, entry=entry,
                                    stop_after=stop_after)
            elif blk == "l1f":
                done = emit_l1_ffn(ctx, x_sb, x_tok, wqkv_in, qsb_in, ksb_in,
                                   wol_in, lgb_in, lbb_in, w1l_in, b1b_in,
                                   w2l_in, b2b_in, rot_in, mask_in, xspill,
                                   entry=entry, stop_after=stop_after)
            build_program.last_ctx = ctx
            x_tok = done
            entry = done
        st = kb.dma("Y", y_out[b].rearrange("(t p) f -> p t f", p=P),
                    x_sb.rearrange("p (t f) -> p t f", t=NT), waits=(done,))
        store_toks.append(st)

    dbg_aps = getattr(build_program, "last_ctx", None)
    for nm in dbg_dump:
        ap = dbg_aps.dbg[nm]
        dt = nc.dram_tensor("dbg_" + nm, [P, ap.shape[-1]],
                            BF16 if str(ap.dtype) == str(BF16) else F32,
                            kind="ExternalOutput").ap()
        store_toks.append(kb.dma("Y", dt, ap, waits=(done,)))

    kb.op("Y", lambda e: None, waits=(store_toks[-1],), inc=0)
    kb.emit()
    return nc


def kernel(**inputs):
    from concourse.bass_utils import run_bass_kernel_spmd
    x = inputs["L0_attn_inputs"]
    shared = make_shared_inmap(inputs)
    nc = build_program(n_batch=2, blocks=("l0a", "l0f", "l1a", "l1f"))
    in_maps = [dict(shared, x=np.ascontiguousarray(x[2 * c:2 * c + 2]))
               for c in range(8)]
    res = run_bass_kernel_spmd(nc, in_maps, core_ids=list(range(8)))
    return np.concatenate([r["y"] for r in res.results], axis=0)


def make_shared_inmap(inputs):
    f32 = np.float32
    rep = lambda a, shape: np.ascontiguousarray(np.broadcast_to(a, shape))
    cm = lambda b: np.ascontiguousarray(b.reshape(8, 128).T)
    ident = np.eye(128, dtype=f32)
    denom0 = np.power(10000.0, 2.0 * np.arange(0, 1024, 2, dtype=f32) / 1024.0
                      ).astype(f32)
    ang0 = (np.arange(2048, dtype=f32)[:, None] / denom0[None, :]).astype(f32)
    DIM_HEAD, WINDOW = 64, 512
    half_idx = np.arange(0, DIM_HEAD, 2, dtype=f32)
    inv_freq = (1.0 / 10000.0 ** (half_idx / DIM_HEAD)).astype(f32)
    t = np.arange(2048, dtype=f32)
    freqs = t[:, None] * inv_freq[None, :]
    freqs = np.concatenate([freqs, freqs], axis=-1)
    base = ((half_idx + 0.4 * DIM_HEAD) / (1.4 * DIM_HEAD)).astype(f32)
    power = (t - 1024) / float(WINDOW // 2)
    scale = (base[None, :] ** power[:, None]).astype(f32)
    scale = np.concatenate([scale, scale], axis=-1)
    cosf, sinf = np.cos(freqs), np.sin(freqs)
    rot = np.stack([cosf * scale, sinf * scale, cosf / scale, sinf / scale]
                   ).astype(f32)
    masks = np.zeros((16, 128, 1024), f32)
    for qt in range(16):
        i = qt * 128 + np.arange(128)[:, None]
        jg = (qt // 4 - 1) * 512 + np.arange(1024)[None, :]
        valid = (jg >= 0) & (jg <= i) & (i - jg <= 512)
        masks[qt] = np.where(valid, 0.0, -1e30).astype(f32)
    ii = inputs
    return {
        "L0_attn_w_qkv": ii["L0_attn_w_qkv"],
        "L0_attn_w_o": ii["L0_attn_w_o"],
        "L0_ffn_router_weight": ii["L0_ffn_router_weight"],
        "L0_ffn_router_bias": ii["L0_ffn_router_bias"],
        "L0_ffn_gate_up_proj": ii["L0_ffn_gate_up_proj"],
        "L0_ffn_gate_up_bias": ii["L0_ffn_gate_up_bias"],
        "L0_ffn_down_proj": ii["L0_ffn_down_proj"],
        "L0_ffn_down_bias": ii["L0_ffn_down_bias"],
        "L1_WqT": np.ascontiguousarray(ii["L1_attn_Wq"].T),
        "L1_WkT": np.ascontiguousarray(ii["L1_attn_Wk"].T),
        "L1_WvT": np.ascontiguousarray(ii["L1_attn_Wv"].T),
        "L1_WoT": np.ascontiguousarray(ii["L1_attn_Wo"].T),
        "L1_bq_cm": cm(ii["L1_attn_bq"]),
        "L1_bk_cm": cm(ii["L1_attn_bk"]),
        "L1_bv_b": rep(ii["L1_attn_bv"], (128, 1024)),
        "L1_bo_b": rep(ii["L1_attn_bo"], (128, 1024)),
        "L1_gam_b": rep(ii["L1_attn_ln_gamma"], (128, 1024)),
        "L1_bet_b": rep(ii["L1_attn_ln_beta"], (128, 1024)),
        "L1_attn_rby": ii["L1_attn_rby"],
        "cst_ang0": ang0,
        "L1_ffn_wqkv": ii["L1_ffn_wqkv"],
        "L1f_qs_b": rep(ii["L1_ffn_q_scale"][:, None, :], (2, 128, 64)),
        "L1f_ks_b": rep(ii["L1_ffn_k_scale"][:, None, :], (2, 128, 64)),
        "L1_ffn_wo": ii["L1_ffn_wo"],
        "L1f_lg_b": rep(ii["L1_ffn_ln_g"][:, None, :], (2, 128, 1024)),
        "L1f_lb_b": rep(ii["L1_ffn_ln_b"][:, None, :], (2, 128, 1024)),
        "L1_ffn_w1": ii["L1_ffn_w1"],
        "L1f_b1_b": rep(ii["L1_ffn_b1"][:, None, :], (2, 128, 5460)),
        "L1_ffn_w2": ii["L1_ffn_w2"],
        "L1f_b2_b": rep(ii["L1_ffn_b2"][:, None, :], (2, 128, 1024)),
        "cst_rot": rot,
        "cst_mask": masks,
        "cst_ident": ident,
    }


# version 15 (final)
# speedup vs baseline: 1.3373x; 1.3052x over previous
import numpy as np
import concourse.bass as bass
import concourse.mybir as mybir

F32 = mybir.dt.float32
BF16 = mybir.dt.bfloat16
AF = mybir.ActivationFunctionType
ALU = mybir.AluOpType
AX = mybir.AxisListType

P = 128
SEQ = 2048
D = 1024
NT = SEQ // P
FC = D // P
NB_PER_CORE = 2



class KB:
    ENGINES = ("T", "V", "S", "G", "Y")

    NPOOL = 6

    def __init__(self, nc):
        self.nc = nc
        self.plan = {e: [] for e in self.ENGINES}
        self.sem = {}
        self.cnt = {}
        for name in ("T", "V", "S", "G", "Y"):
            self.sem[name] = nc.alloc_semaphore("sem_" + name)
            self.cnt[name] = 0
        self.dma_pool = {}
        self.dma_rr = {}
        for q in ("G", "Y", "S"):
            names = []
            for i in range(self.NPOOL):
                nm = f"D{q}{i}"
                self.sem[nm] = nc.alloc_semaphore("sem_" + nm)
                self.cnt[nm] = 0
                names.append(nm)
            self.dma_pool[q] = names
            self.dma_rr[q] = 0
        self.hwm = {}

    def op(self, E, fn, waits=(), sem=None, inc=1):
        if sem is None:
            sem = E
        ws = []
        for w in waits:
            if w is None:
                continue
            s, v = w
            k = (E, s)
            if self.hwm.get(k, 0) >= v:
                continue
            if not s.startswith("D"):
                self.hwm[k] = v
            ws.append((s, v))
        self.plan[E].append((fn, ws, sem, inc))
        self.cnt[sem] += inc
        return (sem, self.cnt[sem])

    def dma(self, E, out, in_, waits=(), **kw):
        pool = self.dma_pool[E]
        sem = pool[self.dma_rr[E] % len(pool)]
        self.dma_rr[E] += 1
        w = tuple(waits)
        if self.cnt[sem]:
            w = w + ((sem, self.cnt[sem]),)
        return self.op(E, lambda e: e.dma_start(out=out, in_=in_, **kw),
                       waits=w, sem=sem, inc=16)

    CHAIN = {"tensor_scalar", "tensor_copy", "tensor_tensor", "tensor_add",
             "tensor_sub", "tensor_mul", "tensor_max", "reduce_max", "reduce_sum",
             "tensor_reduce", "reciprocal", "memset", "memzero", "bn_stats",
             "bn_aggr", "scalar_tensor_tensor", "select", "activation",
             "tensor_single_scalar", "tensor_scalar_mul", "tensor_scalar_add",
             "iota", "affine_select", "tensor_relu", "copy", "mul", "add",
             "square", "sqrt"}

    class _Proxy:
        def __init__(self, kb, eng, name):
            self._kb, self._eng, self._name = kb, eng, name

        def __getattr__(self, attr):
            f = getattr(self._eng, attr)
            if attr not in KB.CHAIN:
                return f
            kb, name = self._kb, self._name

            def wrapped(*a, **k):
                inst = f(*a, **k)
                c = kb.chain_cnt[name]
                if c:
                    inst._wait_ge(kb.chain_sem[name], c)
                inst.then_inc(kb.chain_sem[name], 1)
                kb.chain_cnt[name] += 1
                return inst
            return wrapped

    def emit(self):
        nc = self.nc
        self.chain_sem = {}
        self.chain_cnt = {}
        for e in ("V", "S", "G"):
            self.chain_sem[e] = nc.alloc_semaphore("chain_" + e)
            self.chain_cnt[e] = 0
        with nc.Block() as block:
            def run(eng_name):
                def body(raw):
                    chained = eng_name in ("V", "S", "G")
                    eng = KB._Proxy(self, raw, eng_name) if chained else raw
                    for fn, ws, sem, inc in self.plan[eng_name]:
                        for s, v in ws:
                            raw.wait_ge(self.sem[s], v)
                        c0 = self.chain_cnt.get(eng_name, 0)
                        inst = fn(eng)
                        if inst is not None and inc:
                            if chained and self.chain_cnt[eng_name] > c0:
                                tail = raw.nop()
                                tail._wait_ge(self.chain_sem[eng_name],
                                              self.chain_cnt[eng_name])
                                tail.then_inc(self.sem[sem], inc)
                            else:
                                inst.then_inc(self.sem[sem], inc)
                return body
            block.tensor(run("T"))
            block.vector(run("V"))
            block.scalar(run("S"))
            block.gpsimd(run("G"))
            block.sync(run("Y"))


class Arena:
    def __init__(self, nc, name, kib_per_part=None):
        if kib_per_part is None:
            self.bytes = nc.sbuf_bytes_remaining & ~63
        else:
            self.bytes = kib_per_part * 1024
        self.h = nc.alloc_sbuf_tensor(name, [P, self.bytes // 4], F32)
        self.f32 = self.h.ap()
        self.b16 = self.f32.bitcast(BF16)
        self.off = 0
        self.marks = []

    def alloc_f32(self, cols):
        o = (self.off + 3) & ~3
        self.off = o + 4 * cols
        assert self.off <= self.bytes, f"arena overflow {self.off}>{self.bytes}"
        return self.f32[:, o // 4: o // 4 + cols]

    def alloc_b16(self, cols):
        o = (self.off + 3) & ~3
        self.off = o + 2 * cols
        assert self.off <= self.bytes, f"arena overflow {self.off}>{self.bytes}"
        return self.b16[:, o // 2: o // 2 + cols]

    def push(self):
        self.marks.append(self.off)

    def pop(self):
        self.off = self.marks.pop()


class Ctx:
    def __init__(self, kb, arena, pb, consts, pbig=None):
        self.kb = kb
        self.arena = arena
        self.pb = pb
        self.pbig = pbig
        self.pbb = [b.bitcast(BF16) for b in pb]
        self.c = consts
        self.pb_tok = [None] * 8
        self.pb_wtok = [None] * 8

    def mm(self, bank, stat_list, mov_list, waits=(), n=None, start=True,
           stop=True):
        kb = self.kb
        ps = self.pb[bank]
        if n is None:
            n = mov_list[0].shape[-1]

        mo = stat_list[0].shape[-1]

        def fn(t):
            last = None
            m = len(stat_list)
            for i, (st, mv) in enumerate(zip(stat_list, mov_list)):
                last = t.matmul(ps[0:mo, 0:n], st, mv,
                                start=(start and i == 0),
                                stop=(stop and i == m - 1))
            return last
        w = tuple(waits)
        if start:
            w = w + (self.pb_tok[bank],)
        tok = kb.op("T", fn, waits=w)
        self.pb_wtok[bank] = tok
        return tok

    def transposes(self, bank, pairs, waits=()):
        kb = self.kb
        psb = self.pbb[bank]
        ident = self.c["ident_b16"]

        def fn(t):
            last = None
            for off, src in pairs:
                ncols = src.shape[-1]
                last = t.transpose(psb[0:ncols, off:off + src.shape[0]], src, ident)
            return last
        tok = kb.op("T", fn, waits=tuple(waits) + (self.pb_tok[bank],))
        self.pb_wtok[bank] = tok
        return tok

    def evac(self, E, bank, fn, waits=()):
        tok = self.kb.op(E, fn, waits=tuple(waits) + (self.pb_wtok[bank],))
        self.pb_tok[bank] = tok
        return tok



def emit_ln(ctx, stats, x_ap, out_cb, waits=(), width=D, eps=1e-5, prev_s=None,
            sq_scratch=None):
    kb = ctx.kb
    rw = 1.0 / width
    tok_sum = kb.op("V", lambda v: v.reduce_sum(stats[:, 4:5], x_ap, axis=AX.X),
                    waits=tuple(waits) + (prev_s,))
    tok_sq = kb.op("S", lambda s: s.activation(
        sq_scratch[:, 0:width], x_ap, AF.Square, bias=0.0, scale=1.0,
        accum_out=stats[:, 5:6]), waits=tuple(waits) + (prev_s,))

    def fn_var(v):
        v.tensor_scalar(stats[:, 0:1], stats[:, 4:5], rw, None, ALU.mult)
        v.tensor_scalar(stats[:, 9:10], stats[:, 4:5], -rw, None, ALU.mult)
        v.tensor_scalar(stats[:, 6:7], stats[:, 5:6], rw, None, ALU.mult)
        v.tensor_scalar(stats[:, 7:8], stats[:, 0:1], stats[:, 0:1], None, ALU.mult)
        return v.tensor_sub(stats[:, 1:2], stats[:, 6:7], stats[:, 7:8])
    tok_v = kb.op("V", fn_var, waits=(tok_sum, tok_sq))

    def fn_rstd(s):
        s.activation(stats[:, 2:3], stats[:, 1:2], AF.Ln, bias=ctx.c["c_eps"], scale=1.0)
        s.activation(stats[:, 3:4], stats[:, 2:3], AF.Exp, bias=0.0, scale=-0.5)
        return s.activation(stats[:, 10:11], stats[:, 9:10], AF.Identity,
                            bias=0.0, scale=stats[:, 3:4])
    tok_s = kb.op("S", fn_rstd, waits=(tok_v,))
    out_cb(stats[:, 0:1], stats[:, 3:4], tok_s)
    return tok_s



def emit_l0_attn(ctx, x_sb, x_tok, wq_in, wo_in, entry=None, stop_after=None):
    kb, ar = ctx.kb, ctx.arena
    NH, DK = 16, 64
    ar.push()
    qT = ar.alloc_b16(NT * D)
    kT = ar.alloc_b16(NT * D)
    vv = ar.alloc_b16(NT * D)
    stats = ar.alloc_f32(64)
    xnt = ar.alloc_b16(D)
    wqk = [ar.alloc_b16(FC * P) for _ in range(2)]
    lnT = ar.alloc_b16(NT * D)
    ot2 = ar.alloc_b16(4 * D)
    wo_sb = lnT[:, 0:FC * D]
    ptile2 = [lnT[:, FC * D:FC * D + SEQ],
              lnT[:, FC * D + 2 * SEQ + 2 * D:FC * D + 3 * SEQ + 2 * D]]
    ptT = lnT[:, FC * D + SEQ:FC * D + 2 * SEQ]
    otile2 = [ot2[:, 0:D], ot2[:, 2 * D:3 * D]]
    otT2 = [ot2[:, D:2 * D], ot2[:, 3 * D:4 * D]]

    lnT3 = lnT.rearrange("p (c s) -> p c s", c=FC)
    qT3 = qT.rearrange("p (h s) -> p h s", h=FC)
    kT3 = kT.rearrange("p (h s) -> p h s", h=FC)
    vv3 = vv.rearrange("p (t f) -> p t f", t=NT)

    ln_toks = []
    prev_s = None
    for tt in range(NT):
        xap = x_sb[:, tt * D:(tt + 1) * D]

        def cb(mean, rstd, tok_s, tt=tt, xap=xap):
            tok_a = kb.op("V", lambda v: v.tensor_scalar(
                xnt, xap, mean, rstd, ALU.subtract, ALU.mult),
                waits=(tok_s, ln_toks[-1] if ln_toks else None))
            bank = 6 + (tt % 2)
            pairs = [(c * P, xnt[:, c * P:(c + 1) * P]) for c in range(FC)]
            tok_t = ctx.transposes(bank, pairs, waits=(tok_a, ctx.c["load_tok"]))
            tok_e = ctx.evac("V", bank, lambda v, tt=tt, bank=bank: v.tensor_copy(
                lnT3[:, :, tt * P:(tt + 1) * P],
                ctx.pbb[bank][:, 0:FC * P].rearrange("p (c s) -> p c s", s=P)))
            ln_toks.append(tok_e)
        prev_s = emit_ln(ctx, stats, xap, cb,
                         waits=(x_tok, entry, ln_toks[-1] if ln_toks else None),
                         prev_s=prev_s, sq_scratch=xnt)
    lnT_done = ln_toks[-1]
    if stop_after == "ln":
        ctx.dbg = dict(lnT=lnT)
        ar.pop()
        return lnT_done

    wv4 = qT[:, 0:FC * D].rearrange("p (c f) -> p c f", c=FC)
    wv_toks = []
    for c in range(FC):
        wv_toks.append(kb.dma("G", wv4[:, c, :].rearrange("p (h n) -> p h n", h=NH),
                       wq_in[:, 2, c * P:(c + 1) * P, :].rearrange("h p n -> p h n"),
                       waits=(entry,)))
    v_toks = []
    for tt in range(NT):
        for half in range(2):
            bank = 2 + (tt * 2 + half) % 2
            stat = [lnT3[:, c, tt * P:(tt + 1) * P] for c in range(FC)]
            mov = [wv4[:, c, half * 512:(half + 1) * 512] for c in range(FC)]
            tok_m = ctx.mm(bank, stat, mov, waits=tuple(wv_toks) + (lnT_done,))
            if half == 0:
                tok_e = ctx.evac("V", bank, lambda v, tt=tt, half=half, bank=bank:
                                 v.tensor_copy(vv3[:, tt, half * 512:(half + 1) * 512],
                                               ctx.pb[bank][:, 0:512]))
            else:
                tok_e = ctx.evac("S", bank, lambda s_, tt=tt, half=half, bank=bank:
                                 s_.copy(vv3[:, tt, half * 512:(half + 1) * 512],
                                         ctx.pb[bank][:, 0:512]))
            v_toks.append(tok_e)
    v_done = v_toks[-1]

    wk_tok = {}
    proj_done = None
    for ai, dT3 in ((1, kT3), (0, qT3)):
        for hp in range(NH // 2):
            wb = wqk[hp % 2]
            wb4 = wb.rearrange("p (c h n) -> p c h n", c=FC, h=2)
            w_toks = []
            for h2 in range(2):
                w_toks.append(kb.dma(
                    "G", wb4[:, :, h2, :],
                    wq_in[2 * hp + h2, ai, :, :].rearrange("(c p) n -> p c n", p=P),
                    waits=(wk_tok.get(hp % 2), entry)))
            for tn in range(SEQ // 512):
                bank = tn % 2
                stat = [wb[:, c * P:(c + 1) * P] for c in range(FC)]
                mov = [lnT3[:, c, tn * 512:(tn + 1) * 512] for c in range(FC)]
                tok_m = ctx.mm(bank, stat, mov,
                               waits=tuple(w_toks) + (lnT_done, v_done))
                if tn % 2 == 0:
                    tok_e = ctx.evac("V", bank, lambda v, hp=hp, tn=tn, bank=bank,
                                     dT3=dT3: v.tensor_copy(
                        dT3[:, hp, tn * 512:(tn + 1) * 512], ctx.pb[bank][:, 0:512]))
                else:
                    tok_e = ctx.evac("S", bank, lambda s_, hp=hp, tn=tn, bank=bank,
                                     dT3=dT3: s_.copy(
                        dT3[:, hp, tn * 512:(tn + 1) * 512], ctx.pb[bank][:, 0:512]))
                wk_tok[hp % 2] = tok_e
                proj_done = tok_e

    wo4 = wo_sb.rearrange("p (c f) -> p c f", c=FC)
    tok_wo = kb.dma("G", wo4, wo_in.rearrange("(c p) f -> p c f", p=P),
                    waits=(proj_done,))

    SC = 0.125
    x_done = None

    def tail_l0(hh_state):
        h, ptile, ex_toks, tok_r, bs, otile = hh_state
        pt_toks = []
        for g in range(2):
            bank = bs + g
            pairs = [(i * P, ptile[:, (g * 8 + i) * P:(g * 8 + i + 1) * P])
                     for i in range(8)]
            ctx.transposes(bank, pairs, waits=tuple(ex_toks))
            if g == 0:
                tok_e = ctx.evac("V", bank, lambda v, g=g, bank=bank: v.tensor_copy(
                    ptT[:, g * 8 * P:(g + 1) * 8 * P], ctx.pbb[bank][:, 0:8 * P]))
            else:
                tok_e = ctx.evac("S", bank, lambda s, g=g, bank=bank: s.copy(
                    ptT[:, g * 8 * P:(g + 1) * 8 * P], ctx.pbb[bank][:, 0:8 * P]))
            pt_toks.append(tok_e)
        stat = [ptT[:, kc * P:(kc + 1) * P] for kc in range(NT)]
        mov = [vv3[:, kc, h * DK:(h + 1) * DK] for kc in range(NT)]
        ctx.mm(bs + 2, stat, mov, waits=tuple(pt_toks) + (v_done,), n=DK)
        return ctx.evac("S", bs + 2, lambda s, h=h, otile=otile: s.activation(
            otile[:, h * DK:(h + 1) * DK], ctx.pb[bs + 2][:, 0:DK],
            AF.Identity, bias=0.0, scale=stats[:, 51 + h % 2:52 + h % 2]),
            waits=(tok_r,))

    qt_tail = None
    for qt in range(NT):
        hstate = None
        for h in range(NH):
            hp, hh = h // 2, h % 2
            ptile = ptile2[h % 2]
            bs = 4 * (h % 2)
            mb = 16 + 12 * (h % 2)
            sc_toks = []
            for kc4 in range(4):
                bank = bs + kc4
                stat = [qT3[hh * 64:(hh + 1) * 64, hp, qt * P:(qt + 1) * P]]
                mov = [kT3[hh * 64:(hh + 1) * 64, hp, kc4 * 512:(kc4 + 1) * 512]]
                sc_toks.append(ctx.mm(bank, stat, mov, waits=(proj_done,)))
            tok_e = kb.op("S", lambda s, bs=bs, ptile=ptile, mb=mb: s.activation(
                ptile, ctx.pbig[:, bs * 512:(bs + 4) * 512],
                AF.Exp, bias=0.0, scale=SC,
                accum_out=stats[:, mb + 10:mb + 11]),
                waits=tuple(sc_toks) + (tok_wo,))
            for kc4 in range(4):
                ctx.pb_tok[bs + kc4] = tok_e
            ex_toks = [tok_e]
            tok_r = kb.op("V", lambda v, h=h, mb=mb: v.reciprocal(
                stats[:, 51 + h % 2:52 + h % 2], stats[:, mb + 10:mb + 11]),
                waits=(tok_e,))
            if h == 2 and qt_tail is not None:
                x_done = qt_tail()
                qt_tail = None
            if hstate is not None:
                tail_l0(hstate)
            hstate = (h, ptile, ex_toks, tok_r, bs, otile2[qt % 2])
        tok_o = tail_l0(hstate)

        def mk_tail(qt=qt, tok_o=tok_o):
            otile, otT = otile2[qt % 2], otT2[qt % 2]

            def emit():
                pairs = [(c * P, otile[:, c * P:(c + 1) * P]) for c in range(FC)]
                ctx.transposes(3, pairs, waits=(tok_o,))
                tok_e = ctx.evac("V", 3, lambda v: v.tensor_copy(
                    otT, ctx.pbb[3][:, 0:FC * P]))
                for half in range(2):
                    bank = half
                    stat = [otT[:, c * P:(c + 1) * P] for c in range(FC)]
                    mov = [wo4[:, c, half * 512:(half + 1) * 512] for c in range(FC)]
                    ctx.mm(bank, stat, mov, waits=(tok_e, tok_wo))
                    xap = x_sb[:, qt * D + half * 512: qt * D + (half + 1) * 512]
                    tok_x = ctx.evac("V", bank, lambda v, xap=xap, bank=bank:
                                     v.tensor_add(xap, xap, ctx.pb[bank][:, 0:512]))
                return tok_x
            return emit
        if qt == NT - 1:
            x_done = mk_tail()()
        else:
            qt_tail = mk_tail()
    ar.pop()
    return x_done




def emit_lnT_phase(ctx, x_sb, lnT3, xnt, stats, x_tok, entry, save_mr=None):
    kb = ctx.kb
    ln_toks = []
    prev_s = None
    for tt in range(NT):
        xap = x_sb[:, tt * D:(tt + 1) * D]

        def cb(mean, rstd, tok_s, tt=tt, xap=xap):
            if save_mr is not None:
                def fn_mr(v, tt=tt):
                    v.tensor_copy(save_mr[:, 2 * tt:2 * tt + 1], stats[:, 0:1])
                    return v.tensor_copy(save_mr[:, 2 * tt + 1:2 * tt + 2], stats[:, 3:4])
                kb.op("V", fn_mr, waits=(tok_s,))
            tok_a = kb.op("S", lambda s_, rstd=rstd: s_.activation(
                xnt, xap, AF.Identity, bias=stats[:, 10:11], scale=rstd),
                waits=(tok_s, ln_toks[-1] if ln_toks else None))
            bank = 6 + (tt % 2)
            pairs = [(c * P, xnt[:, c * P:(c + 1) * P]) for c in range(FC)]
            tok_t = ctx.transposes(bank, pairs, waits=(tok_a, ctx.c["load_tok"]))
            eng = "V" if tt % 2 == 0 else "S"
            if eng == "V":
                tok_e = ctx.evac("V", bank, lambda v, tt=tt, bank=bank: v.tensor_copy(
                    lnT3[:, :, tt * P:(tt + 1) * P],
                    ctx.pbb[bank][:, 0:FC * P].rearrange("p (c s) -> p c s", s=P)))
            else:
                tok_e = ctx.evac("S", bank, lambda s_, tt=tt, bank=bank: s_.copy(
                    lnT3[:, :, tt * P:(tt + 1) * P],
                    ctx.pbb[bank][:, 0:FC * P].rearrange("p (c s) -> p c s", s=P)))
            ln_toks.append(tok_e)
        prev_s = emit_ln(ctx, stats, xap, cb,
                         waits=(x_tok, entry, ln_toks[-1] if ln_toks else None),
                         prev_s=prev_s, sq_scratch=xnt)
    return ln_toks[-1]


def _l0f_flush(ctx, kb, act3, unit, prev_down):
    ic, tc, A, Bb, Cc, tok_sg, tok_u2 = unit
    tok_gl = kb.op("V", lambda v, A=A, Bb=Bb: v.tensor_mul(A, A, Bb),
                   waits=(tok_sg, tok_u2))
    return kb.op("V", lambda v, ic=ic, tc=tc, A=A, Cc=Cc: v.tensor_mul(
        act3[:, ic, tc * 512:(tc + 1) * 512], A, Cc),
        waits=(tok_gl, prev_down))


def emit_l0_ffn(ctx, x_sb, x_tok, rw_in, rb_in, gup_in, gub_in, dp_in, db_in,
                entry=None):
    kb, ar = ctx.kb, ctx.arena
    NE, NIC = 8, 16
    ar.push()
    lnT = ar.alloc_b16(NT * D)
    act = ar.alloc_b16(NIC * 1024)
    dpsb = ar.alloc_b16(NIC * 1024)
    stats = ar.alloc_f32(64)
    xnt = ar.alloc_b16(D)
    mr = ar.alloc_f32(2 * NT)
    A2 = [ar.alloc_f32(512) for _ in range(2)]
    B2 = [ar.alloc_f32(512) for _ in range(2)]
    C2 = [ar.alloc_f32(512) for _ in range(2)]
    wrouter = ar.alloc_f32(NT * 8)
    rwT = ar.alloc_f32(FC * 8)
    rw_stage = ar.alloc_f32(1024)
    rb_row = ar.alloc_f32(8)
    Sb = ar.alloc_f32(8)
    rb_b = ar.alloc_f32(8)
    db_all = ar.alloc_f32(1024)
    xT = ar.alloc_f32(FC * P)
    wtb = ar.alloc_f32(P)
    rsc = ar.alloc_f32(40)
    gwb = [ar.alloc_b16(FC * P) for _ in range(4)]
    gub_stage = ar.alloc_f32(P)
    gub_sb = ar.alloc_f32(32)
    lnT3 = lnT.rearrange("p (c s) -> p c s", c=FC)
    act3 = act.rearrange("p (i s) -> p i s", i=NIC)
    dp3 = dpsb.rearrange("p (i s) -> p i s", i=NIC)
    rwT3 = rwT.rearrange("p (c e) -> p c e", c=FC)
    xT3 = xT.rearrange("p (c s) -> p c s", c=FC)
    probs0, probs, t1, t2, lgt = (rsc[:, 0:8], rsc[:, 8:16], rsc[:, 16:24],
                                  rsc[:, 24:32], rsc[:, 32:40])

    lnT_done = emit_lnT_phase(ctx, x_sb, lnT3, xnt, stats, x_tok, entry, save_mr=mr)

    tok_rw = kb.dma("Y", rw_stage[0:8, :], rw_in, waits=(entry,))
    tok_rb = kb.dma("Y", rb_row[0:1, :], rb_in.rearrange("(o e) -> o e", o=1), waits=(entry,))
    tok_dbl = kb.dma("Y", db_all[0:8, :], db_in, waits=(entry,))

    def fn_trw(t):
        last = None
        for c in range(FC):
            last = t.transpose(ctx.pb[2][:, c * 8:(c + 1) * 8],
                               rw_stage[0:8, c * P:(c + 1) * P],
                               ctx.c["ident_f32"][0:8, 0:8])
        return last
    tok_trw = kb.op("T", fn_trw, waits=(tok_rw, ctx.c["load_tok"], ctx.pb_tok[2],
                                        ctx.pb_wtok[2]))
    ctx.pb_wtok[2] = tok_trw
    tok_rwT = ctx.evac("V", 2, lambda v: v.tensor_copy(rwT, ctx.pb[2][:, 0:FC * 8]))
    tok_S1 = ctx.mm(3, [rwT3[:, c, :] for c in range(FC)],
                    [ctx.c["ones_col"] for _ in range(FC)], waits=(tok_rwT,), n=1)
    tok_S2 = ctx.evac("V", 3, lambda v: v.tensor_copy(stats[0:8, 32:33],
                                                      ctx.pb[3][0:8, 0:1]))

    def fn_tS(t):
        return t.transpose(ctx.pb[2][0:1, 0:8], stats[0:8, 32:33],
                           ctx.c["ident_f32"][0:8, 0:8])
    tok_tS = kb.op("T", fn_tS, waits=(tok_S2, ctx.pb_tok[2], ctx.pb_wtok[2]))
    ctx.pb_wtok[2] = tok_tS
    tok_Srow = ctx.evac("V", 2, lambda v: v.tensor_copy(stats[0:1, 33:41],
                                                        ctx.pb[2][0:1, 0:8]))
    tok_bc = ctx.mm(3, [ctx.c["ones_row"]], [stats[0:1, 33:41]], waits=(tok_Srow,), n=8)
    tok_Sb = ctx.evac("V", 3, lambda v: v.tensor_copy(Sb, ctx.pb[3][:, 0:8]))
    tok_bc2 = ctx.mm(3, [ctx.c["ones_row"]], [rb_row[0:1, 0:8]], waits=(tok_rb,), n=8)
    tok_rbb = ctx.evac("V", 3, lambda v: v.tensor_copy(rb_b, ctx.pb[3][:, 0:8]))

    wdb_toks = []
    for tt in range(NT):
        for bk, cs in ((0, (0, 1, 2, 3)), (1, (4, 5, 6, 7))):
            def fn_tx(t, tt=tt, bk=bk, cs=cs):
                last = None
                for i, c in enumerate(cs):
                    last = t.transpose(ctx.pb[bk][:, i * P:(i + 1) * P],
                                       x_sb[:, tt * D + c * P:tt * D + (c + 1) * P],
                                       ctx.c["ident_f32"])
                return last
            tok_t = kb.op("T", fn_tx, waits=(x_tok, ctx.pb_tok[bk], ctx.pb_wtok[bk],
                                             ctx.c["load_tok"]))
            ctx.pb_wtok[bk] = tok_t
            tok_e = ctx.evac("V", bk, lambda v, bk=bk, cs=cs: v.tensor_copy(
                xT[:, cs[0] * P:(cs[0] + 4) * P], ctx.pb[bk][:, 0:4 * P]))
        tok_xT = tok_e
        tok_G = ctx.mm(3, [xT3[:, c, :] for c in range(FC)],
                       [rwT3[:, c, :] for c in range(FC)], waits=(tok_xT,), n=8)
        tok_nr = kb.op("V", lambda v, tt=tt: v.tensor_scalar(
            stats[:, 42:43], mr[:, 2 * tt:2 * tt + 1], mr[:, 2 * tt + 1:2 * tt + 2],
            -1.0, ALU.mult, ALU.mult))
        tok_l1 = ctx.evac("V", 3, lambda v, tt=tt: v.tensor_scalar(
            lgt, ctx.pb[3][:, 0:8], mr[:, 2 * tt + 1:2 * tt + 2], None, ALU.mult))
        tok_l2 = kb.op("V", lambda v: v.scalar_tensor_tensor(
            t1, Sb, stats[:, 42:43], lgt, ALU.mult, ALU.add),
            waits=(tok_l1, tok_nr, tok_Sb))
        tok_l3 = kb.op("V", lambda v: v.tensor_add(lgt, t1, rb_b),
                       waits=(tok_l2, tok_rbb))
        tok_m = kb.op("V", lambda v: v.reduce_max(stats[:, 43:44], lgt, axis=AX.X,
                                                  negate=True), waits=(tok_l3,))
        tok_ex = kb.op("S", lambda s: s.activation(
            probs0, lgt, AF.Exp, bias=stats[:, 43:44], scale=1.0,
            accum_out=stats[:, 44:45]), waits=(tok_m,))
        tok_ri = kb.op("V", lambda v: v.reciprocal(stats[:, 45:46], stats[:, 44:45]),
                       waits=(tok_ex,))
        tok_p = kb.op("V", lambda v: v.tensor_scalar(
            probs, probs0, stats[:, 45:46], None, ALU.mult), waits=(tok_ri,))
        tok_pm = kb.op("V", lambda v: v.reduce_max(stats[:, 46:47], probs, axis=AX.X),
                       waits=(tok_p,))
        tok_eq = kb.op("V", lambda v: v.tensor_scalar(
            t1, probs, stats[:, 46:47], None, ALU.is_equal), waits=(tok_pm,))
        tok_iv = kb.op("V", lambda v: v.tensor_scalar(
            t2, t1, -1.0, 1.0, ALU.mult, ALU.add), waits=(tok_eq,))
        tok_p2 = kb.op("V", lambda v: v.tensor_mul(t1, probs, t2), waits=(tok_iv,))
        tok_m2 = kb.op("V", lambda v: v.reduce_max(stats[:, 47:48], t1, axis=AX.X),
                       waits=(tok_p2,))
        tok_ge = kb.op("V", lambda v: v.tensor_scalar(
            t2, probs, stats[:, 47:48], None, ALU.is_ge), waits=(tok_m2,))
        tok_w = kb.op("V", lambda v, tt=tt: v.tensor_mul(
            wrouter[:, tt * 8:(tt + 1) * 8], probs, t2), waits=(tok_ge,))
        def fn_tw(t, tt=tt):
            return t.transpose(ctx.pb[2][0:8, 0:P], wrouter[:, tt * 8:(tt + 1) * 8],
                               ctx.c["ident_f32"])
        tok_tw = kb.op("T", fn_tw, waits=(tok_w, ctx.pb_tok[2], ctx.pb_wtok[2]))
        ctx.pb_wtok[2] = tok_tw
        tok_wt = ctx.evac("V", 2, lambda v: v.tensor_copy(wtb[0:8, :],
                                                          ctx.pb[2][0:8, 0:P]))
        for half in range(2):
            bk = 4 + half
            tok_mdb = ctx.mm(bk, [wtb[0:8, :]],
                             [db_all[0:8, half * 512:(half + 1) * 512]],
                             waits=(tok_wt, tok_dbl))
            xap = x_sb[:, tt * D + half * 512:tt * D + (half + 1) * 512]
            tok_xadd = ctx.evac("V", bk, lambda v, xap=xap, bk=bk: v.tensor_add(
                xap, xap, ctx.pb[bk][:, 0:512]))
            wdb_toks.append(tok_xadd)
    w_done = wdb_toks[-1]

    gw_tok = {}
    dp_tok = None
    x_done = w_done
    for e in range(NE):
        tok_gr = kb.dma("Y", gub_stage[0:32, :],
                        gub_in[e].rearrange("(g p) -> g p", p=P),
                        waits=(gw_tok.get("gub"), entry))

        def fn_tg(t):
            return t.transpose(ctx.pb[2][:, 0:32], gub_stage[0:32, :],
                               ctx.c["ident_f32"][0:32, 0:32])
        tok_tg = kb.op("T", fn_tg, waits=(tok_gr, ctx.pb_tok[2], ctx.pb_wtok[2]))
        ctx.pb_wtok[2] = tok_tg
        tok_gub = ctx.evac("V", 2, lambda v: v.tensor_copy(gub_sb,
                                                           ctx.pb[2][:, 0:32]))
        gw_tok["gub"] = tok_gub
        tok_dp = kb.dma("G", dp3, dp_in[e].rearrange("(i p) h -> p i h", p=P),
                        waits=(dp_tok, entry))
        act_done = []
        prev_down = gw_tok.get("prev_down")
        pend = gw_tok.setdefault("pend", {})
        for th in range(2):
            t0 = th * 1024
            for ic in range(NIC):
                bufs = {}
                for gi, guc in ((0, ic), (1, NIC + ic)):
                    wb = gwb[2 * (ic % 2) + gi]
                    tok_lw = kb.dma("G",
                                    wb.rearrange("p (c s) -> p c s", c=FC),
                                    gup_in[e, :, guc * P:(guc + 1) * P].rearrange(
                                        "(c p) n -> p c n", p=P),
                                    waits=(gw_tok.get((ic % 2, gi)), entry))
                    bufs[gi] = (wb, tok_lw)
                for tc in range(2):
                    A, Bb, Cc = A2[tc], B2[tc], C2[tc]
                    mov = [lnT3[:, c, t0 + tc * 512:t0 + (tc + 1) * 512]
                           for c in range(FC)]
                    wb, tok_lw = bufs[0]
                    tok_g = ctx.mm(tc, [wb[:, c * P:(c + 1) * P] for c in range(FC)],
                                   mov, waits=(tok_lw, lnT_done))
                    wb, tok_lw = bufs[1]
                    tok_u = ctx.mm(2 + tc,
                                   [wb[:, c * P:(c + 1) * P] for c in range(FC)],
                                   mov, waits=(tok_lw, lnT_done))
                    gub_ap = gub_sb[:, ic:ic + 1]
                    gub_ap2 = gub_sb[:, NIC + ic:NIC + ic + 1]
                    pa = None
                    if tc in pend:
                        pa = _l0f_flush(ctx, kb, act3, pend.pop(tc), prev_down)
                        act_done.append(pa)
                    tok_mv = ctx.evac("V", tc, lambda v, tc=tc, gub_ap=gub_ap, A=A:
                                      v.tensor_scalar(A, ctx.pb[tc][:, 0:512],
                                                      gub_ap, 7.0, ALU.add, ALU.min),
                                      waits=(tok_gub, pa))
                    tok_sg = kb.op("S", lambda s, A=A, Bb=Bb: s.activation(
                        Bb, A, AF.Sigmoid, bias=0.0, scale=1.702), waits=(tok_mv,))
                    tok_u1 = ctx.evac("V", 2 + tc,
                                      lambda v, tc=tc, gub_ap2=gub_ap2, Cc=Cc:
                                      v.tensor_scalar(Cc, ctx.pb[2 + tc][:, 0:512],
                                                      gub_ap2, 7.0, ALU.add, ALU.min),
                                      waits=(tok_gub, pa))
                    tok_u22 = kb.op("V", lambda v, Cc=Cc: v.tensor_scalar(
                        Cc, Cc, -7.0, 1.0, ALU.max, ALU.add), waits=(tok_u1,))
                    pend[tc] = (ic, tc, A, Bb, Cc, tok_sg, tok_u22)
                    gw_tok[(ic % 2, 0)] = tok_g
                    gw_tok[(ic % 2, 1)] = tok_u
            for tc in list(pend):
                act_done.append(_l0f_flush(ctx, kb, act3, pend.pop(tc), prev_down))
            for tti in range(8):
                tt = th * 8 + tti
                for half in range(2):
                    bk = 4 + (tti % 2) * 2 + half
                    stat = [act3[:, ic, tti * P:(tti + 1) * P] for ic in range(NIC)]
                    mov = [dp3[:, ic, half * 512:(half + 1) * 512] for ic in range(NIC)]
                    tok_d = ctx.mm(bk, stat, mov,
                                   waits=(act_done[-1], tok_dp))
                    prev_down = tok_d
                    xap = x_sb[:, tt * D + half * 512:tt * D + (half + 1) * 512]
                    wap = wrouter[:, tt * 8 + e:tt * 8 + e + 1]
                    x_done = ctx.evac("V", bk, lambda v, xap=xap, bk=bk, wap=wap:
                                      v.scalar_tensor_tensor(
                                          xap, ctx.pb[bk][:, 0:512], wap, xap,
                                          ALU.mult, ALU.add), waits=(w_done,))
            act_done = []
        gw_tok["prev_down"] = prev_down
        dp_tok = tok_d
    ar.pop()
    return x_done



MAGIC = 12582912.0
INV2PI = float(np.float32(1.0 / (2.0 * np.pi)))
C1 = 6.28125
C2 = float(np.float32(2.0 * np.pi - 6.28125))
PI2 = float(np.float32(np.pi / 2))
PI_LO = float(np.nextafter(np.float32(np.pi), np.float32(0.0)))


def emit_pe_add(ctx, xnt, tt, A0, S1, S3, s_ap, ang0_in, prev_tok, x_ready):
    kb = ctx.kb
    tok_a = kb.dma("Y", A0, ang0_in[:, tt, :], waits=(prev_tok, x_ready))
    tok_s = kb.op("V", lambda v: v.tensor_scalar(A0, A0, s_ap, None, ALU.mult),
                  waits=(tok_a, prev_tok))
    out_toks = []
    for trig in range(2):
        def fn_n(v, trig=trig):
            if trig == 0:
                v.tensor_scalar(S3, A0, INV2PI, MAGIC, ALU.mult, ALU.add)
            else:
                v.tensor_scalar(S3, A0, INV2PI, 0.25, ALU.mult, ALU.add)
                v.tensor_scalar(S3, S3, MAGIC, None, ALU.add)
            v.tensor_scalar(S3, S3, MAGIC, None, ALU.subtract)
            v.scalar_tensor_tensor(S1, S3, C1, A0, ALU.mult, ALU.subtract)
            v.scalar_tensor_tensor(S1, S3, -C2, S1, ALU.mult, ALU.subtract)
            if trig == 1:
                v.tensor_scalar(S1, S1, PI2, None, ALU.add)
            return v.tensor_scalar(S1, S1, PI_LO, -PI_LO, ALU.min, ALU.max)
        tok_r = kb.op("V", fn_n, waits=(tok_s,) + tuple(out_toks))
        tok_sin = kb.op("S", lambda s: s.activation(
            S3, S1, AF.Sin, bias=0.0, scale=1.0), waits=(tok_r,))
        tok_add = kb.op("V", lambda v, trig=trig: v.tensor_add(
            xnt.rearrange("p (i two) -> p i two", two=2)[:, :, trig],
            xnt.rearrange("p (i two) -> p i two", two=2)[:, :, trig], S3),
            waits=(tok_sin,))
        out_toks.append(tok_add)
    return out_toks[-1]


def emit_pe_add3(ctx, xnt, A0, S3, pbank, s_ap, dma_toks, x_ready, pb_guard):
    kb = ctx.kb
    pn = ctx.pb[pbank][:, 0:512]
    out = None
    for trig, src in ((0, A0), (1, S3)):
        def fn_n(v, trig=trig, src=src):
            v.tensor_scalar(src, src, s_ap, None, ALU.mult)
            if trig == 0:
                v.tensor_scalar(pn, src, INV2PI, MAGIC, ALU.mult, ALU.add)
            else:
                v.tensor_scalar(pn, src, INV2PI, 0.25, ALU.mult, ALU.add)
                v.tensor_scalar(pn, pn, MAGIC, None, ALU.add)
            v.tensor_scalar(pn, pn, MAGIC, None, ALU.subtract)
            v.scalar_tensor_tensor(src, pn, -C1, src, ALU.mult, ALU.add)
            v.scalar_tensor_tensor(src, pn, -C2, src, ALU.mult, ALU.add)
            if trig == 1:
                v.tensor_scalar(src, src, PI2, None, ALU.add)
            return v.tensor_scalar(src, src, PI_LO, -PI_LO, ALU.min, ALU.max)
        tok_r = kb.op("V", fn_n, waits=dma_toks + (x_ready, out, pb_guard))
        tok_sin = kb.op("S", lambda s, src=src: s.activation(
            pn, src, AF.Sin, bias=0.0, scale=1.0), waits=(tok_r,))
        out = kb.op("V", lambda v, trig=trig: v.tensor_add(
            xnt.rearrange("p (i two) -> p i two", two=2)[:, :, trig],
            xnt.rearrange("p (i two) -> p i two", two=2)[:, :, trig], pn),
            waits=(tok_sin,))
    return out


def emit_pe_add2(ctx, xnt, tt, A0, S3, s_ap, ang0_in, prev_tok, x_ready):
    kb = ctx.kb
    out = None
    for trig in range(2):
        tok_a = kb.dma("Y", A0, ang0_in[:, tt, :],
                       waits=(prev_tok, x_ready, out))
        tok_s = kb.op("V", lambda v: v.tensor_scalar(A0, A0, s_ap, None, ALU.mult),
                      waits=(tok_a,))

        def fn_n(v, trig=trig):
            if trig == 0:
                v.tensor_scalar(S3, A0, INV2PI, MAGIC, ALU.mult, ALU.add)
            else:
                v.tensor_scalar(S3, A0, INV2PI, 0.25, ALU.mult, ALU.add)
                v.tensor_scalar(S3, S3, MAGIC, None, ALU.add)
            v.tensor_scalar(S3, S3, MAGIC, None, ALU.subtract)
            v.scalar_tensor_tensor(A0, S3, -C1, A0, ALU.mult, ALU.add)
            v.scalar_tensor_tensor(A0, S3, -C2, A0, ALU.mult, ALU.add)
            if trig == 1:
                v.tensor_scalar(A0, A0, PI2, None, ALU.add)
            return v.tensor_scalar(A0, A0, PI_LO, -PI_LO, ALU.min, ALU.max)
        tok_r = kb.op("V", fn_n, waits=(tok_s,))
        tok_sin = kb.op("S", lambda s: s.activation(
            S3, A0, AF.Sin, bias=0.0, scale=1.0), waits=(tok_r,))
        out = kb.op("V", lambda v, trig=trig: v.tensor_add(
            xnt.rearrange("p (i two) -> p i two", two=2)[:, :, trig],
            xnt.rearrange("p (i two) -> p i two", two=2)[:, :, trig], S3),
            waits=(tok_sin,))
    return out


def emit_l1_attn(ctx, x_sb, x_tok, wqT_in, wkT_in, wvT_in, woT_in, bq_cm_in,
                 bk_cm_in, bv_b_in, bo_b_in, gam_b_in, bet_b_in, rby_in, ang0_in,
                 entry=None, stop_after=None):
    kb, ar = ctx.kb, ctx.arena
    NH, DK = 16, 64
    ar.push()
    qT = ar.alloc_b16(NT * D)
    kT = ar.alloc_b16(NT * D)
    vv = ar.alloc_b16(NT * D)
    lnpeT = ar.alloc_b16(NT * D)
    stats = ar.alloc_f32(62)
    mr = ar.alloc_f32(2 * NT)
    wqk = ar.alloc_b16(FC * P)
    bqcm = ar.alloc_f32(8)
    bkcm = ar.alloc_f32(8)
    rbuf = ar.alloc_f32(16)
    rsb = ar.alloc_f32(8)

    qf = qT.bitcast(F32)
    xnt1 = qf[:, 0:D]
    A0a = qf[:, D:D + 512]
    S1a = qf[:, D + 512:D + 1024]
    S3a = qf[:, D + 1024:D + 1536]
    tpeb = qT[:, 5120:6144]
    wvT_sb = qT[:, 0:FC * D]
    bvb = qT[:, FC * D:FC * D + 2 * D].bitcast(F32)
    wx = ar.alloc_b16(1024)
    S3b = ar.alloc_f32(512)
    A0b = wx.bitcast(F32)
    woT_buf = [lnpeT[:, i * 512:(i + 1) * 512] for i in range(4)] + [
        lnpeT[:, 7168 + i * 512:7168 + (i + 1) * 512] for i in range(2)]
    ptile2 = [lnpeT[:, 2048:4096], lnpeT[:, 4096:6144]]
    ptT = lnpeT[:, 6144:7168]
    ot2b = ar.alloc_b16(2 * D)
    xn2 = ar.alloc_f32(D)
    otile2 = [wqk[:, 0:D], ot2b[:, 0:D]]
    otT2 = [tpeb, ot2b[:, D:2 * D]]
    gamb = lnpeT[:, 8192:10240].bitcast(F32)
    betb = lnpeT[:, 10240:12288].bitcast(F32)
    bob = lnpeT[:, 12288:14336].bitcast(F32)
    xnt6_2 = [lnpeT[:, 14336:16384].bitcast(F32), xn2]

    lnpeT3 = lnpeT.rearrange("p (c s) -> p c s", c=FC)
    qT3 = qT.rearrange("p (h s) -> p h s", h=FC)
    kT3 = kT.rearrange("p (h s) -> p h s", h=FC)
    vv3 = vv.rearrange("p (t f) -> p t f", t=NT)
    wvT4 = wvT_sb.rearrange("p (c f) -> p c f", c=FC)

    tok_rby = kb.dma("Y", rbuf[0:1, 0:3], rby_in.rearrange("(o r) -> o r", o=1),
                     waits=(entry,))

    def fn_sc(v):
        r, b, y = rbuf[0:1, 0:1], rbuf[0:1, 1:2], rbuf[0:1, 2:3]
        row = rbuf[0:1, 4:12]
        v.memset(row, 0.0)
        v.tensor_scalar(row[:, 0:1], y, 0.1, 1.0, ALU.mult, ALU.add)
        v.reciprocal(row[:, 0:1], row[:, 0:1])
        v.tensor_scalar(rbuf[0:1, 12:13], r, 0.1, 1.0, ALU.mult, ALU.add)
        v.scalar_tensor_tensor(rbuf[0:1, 12:13], b, 0.05, rbuf[0:1, 12:13],
                               ALU.mult, ALU.add)
        v.scalar_tensor_tensor(rbuf[0:1, 12:13], y, 0.15, rbuf[0:1, 12:13],
                               ALU.mult, ALU.add)
        v.tensor_scalar(row[:, 1:2], rbuf[0:1, 12:13], 0.125, None, ALU.mult)
        v.tensor_scalar(row[:, 2:3], y, 1.2246467991473532e-18, 1.0,
                        ALU.mult, ALU.add)
        v.tensor_scalar(row[:, 3:4], b, 0.1, 1.0, ALU.mult, ALU.add)
        v.tensor_scalar(row[:, 3:4], row[:, 3:4], 1e-5, None, ALU.mult)
        v.tensor_scalar(row[:, 4:5], r, 0.1, 1.0, ALU.mult, ALU.add)
        return v.scalar_tensor_tensor(row[:, 4:5], y, 0.05, row[:, 4:5],
                                      ALU.mult, ALU.add)
    tok_sc = kb.op("V", fn_sc, waits=(tok_rby,))
    tok_bc = ctx.mm(5, [ctx.c["ones_row"]], [rbuf[0:1, 4:12]], waits=(tok_sc,), n=8)
    tok_rsb = ctx.evac("V", 5, lambda v: v.tensor_copy(rsb, ctx.pb[5][:, 0:8]))
    s_pe = rsb[:, 0:1]
    s_ls = rsb[:, 1:2]
    s_vf = rsb[:, 2:3]
    s_eps = rsb[:, 3:4]
    s_gs = rsb[:, 4:5]

    tok_bq = kb.dma("Y", bqcm, bq_cm_in, waits=(entry,))
    tok_bk = kb.dma("Y", bkcm, bk_cm_in, waits=(entry,))

    ln_toks = []
    prev_s = None
    for tt in range(NT):
        xap = x_sb[:, tt * D:(tt + 1) * D]

        def cb(mean, rstd, tok_s, tt=tt, xap=xap):
            def fn_mr(v, tt=tt):
                v.tensor_copy(mr[:, 2 * tt:2 * tt + 1], stats[:, 0:1])
                return v.tensor_copy(mr[:, 2 * tt + 1:2 * tt + 2], stats[:, 3:4])
            kb.op("V", fn_mr, waits=(tok_s,))
            tok_a = kb.op("V", lambda v: v.tensor_scalar(
                xnt1, xap, mean, rstd, ALU.subtract, ALU.mult),
                waits=(tok_s, ln_toks[-1] if ln_toks else None))
            tok_pe = emit_pe_add(ctx, xnt1, tt, A0a, S1a, S3a, s_pe, ang0_in,
                                 ln_toks[-1] if ln_toks else tok_rsb, tok_a)
            tok_c = kb.op("V", lambda v: v.tensor_copy(tpeb, xnt1), waits=(tok_pe,))
            bank = 6 + (tt % 2)
            pairs = [(c * P, tpeb[:, c * P:(c + 1) * P]) for c in range(FC)]
            tok_t = ctx.transposes(bank, pairs, waits=(tok_c, ctx.c["load_tok"]))
            tok_e = ctx.evac("V", bank, lambda v, tt=tt, bank=bank: v.tensor_copy(
                lnpeT3[:, :, tt * P:(tt + 1) * P],
                ctx.pbb[bank][:, 0:FC * P].rearrange("p (c s) -> p c s", s=P)))
            ln_toks.append(tok_e)
        prev_s = emit_ln(ctx, stats, xap, cb,
                         waits=(x_tok, entry, ln_toks[-1] if ln_toks else None),
                         prev_s=prev_s, sq_scratch=xnt1)
    lnpeT_done = ln_toks[-1]
    ctx.dbg = dict(lnpeT=lnpeT, kT=kT, vv=vv, qT=qT)
    if stop_after == "ln":
        ar.pop()
        return lnpeT_done

    tok_wv = kb.dma("G", wvT4, wvT_in.rearrange("(c p) f -> p c f", p=P),
                    waits=(entry, lnpeT_done))
    tok_bv = kb.dma("Y", bvb, bv_b_in, waits=(entry, lnpeT_done))
    v_toks = []
    for tt in range(NT):
        for half in range(2):
            bank = 2 + (tt * 2 + half) % 2
            stat = [lnpeT3[:, c, tt * P:(tt + 1) * P] for c in range(FC)]
            mov = [wvT4[:, c, half * 512:(half + 1) * 512] for c in range(FC)]
            tok_m = ctx.mm(bank, stat, mov, waits=(tok_wv, lnpeT_done))
            tok_e = ctx.evac("V", bank, lambda v, tt=tt, half=half, bank=bank:
                             v.tensor_add(vv3[:, tt, half * 512:(half + 1) * 512],
                                          ctx.pb[bank][:, 0:512],
                                          bvb[:, half * 512:(half + 1) * 512]),
                             waits=(tok_bv,))
            v_toks.append(tok_e)
    v_done = v_toks[-1]

    wk_tok = None
    proj_done = None
    for wT_in, dT3, bcm, btok in ((wkT_in, kT3, bkcm, tok_bk),
                                  (wqT_in, qT3, bqcm, tok_bq)):
        for oc in range(FC):
            src = wT_in[:, oc * P:(oc + 1) * P].rearrange("(c p) n -> p c n", p=P)
            tok_w = kb.dma("G", wqk.rearrange("p (c s) -> p c s", c=FC), src,
                           waits=(wk_tok, entry))
            for tn in range(SEQ // 512):
                bank = tn % 2
                stat = [wqk[:, c * P:(c + 1) * P] for c in range(FC)]
                mov = [lnpeT3[:, c, tn * 512:(tn + 1) * 512] for c in range(FC)]
                tok_m = ctx.mm(bank, stat, mov, waits=(tok_w, lnpeT_done, v_done))
                tok_e = ctx.evac("S", bank, lambda s_, oc=oc, tn=tn, bank=bank,
                                 dT3=dT3, bcm=bcm: s_.activation(
                    dT3[:, oc, tn * 512:(tn + 1) * 512], ctx.pb[bank][:, 0:512],
                    AF.Identity, bias=bcm[:, oc:oc + 1], scale=1.0), waits=(btok,))
                wk_tok = tok_e
                proj_done = tok_e

    if stop_after == "proj":
        ar.pop()
        return proj_done

    tok_gm = kb.dma("Y", gamb, gam_b_in, waits=(proj_done,))
    tok_gm2 = kb.op("V", lambda v: v.tensor_scalar(gamb, gamb, s_gs, None, ALU.mult),
                    waits=(tok_gm,))
    tok_bt = kb.dma("Y", betb, bet_b_in, waits=(proj_done,))
    tok_bo = kb.dma("Y", bob, bo_b_in, waits=(proj_done,))

    wo_tok = {}
    x_done = None

    def tail_l1(hh_state):
        h, ptile, ex_toks, tok_r, bs, otile = hh_state
        av_tok = None
        for g in range(2):
            bank = bs + g
            pairs = [(i * P, ptile[:, (g * 8 + i) * P:(g * 8 + i + 1) * P])
                     for i in range(8)]
            tok_t = ctx.transposes(bank, pairs, waits=tuple(ex_toks))
            if g == 0:
                tok_pte = ctx.evac("V", bank, lambda v, bank=bank: v.tensor_copy(
                    ptT, ctx.pbb[bank][:, 0:8 * P]))
            else:
                tok_pte = ctx.evac("S", bank, lambda s_, bank=bank: s_.copy(
                    ptT, ctx.pbb[bank][:, 0:8 * P]))
            stat = [ptT[:, i * P:(i + 1) * P] for i in range(8)]
            mov = [vv3[:, g * 8 + i, h * DK:(h + 1) * DK] for i in range(8)]
            av_tok = ctx.mm(bs + 2, stat, mov, waits=(tok_pte, v_done), n=DK,
                            start=(g == 0), stop=(g == 1))
        return ctx.evac("V", bs + 2, lambda v, h=h, otile=otile: v.tensor_scalar(
            otile[:, h * DK:(h + 1) * DK], ctx.pb[bs + 2][:, 0:DK],
            stats[:, 60 + h % 2:61 + h % 2], s_vf, ALU.mult, ALU.mult),
            waits=(tok_r,))

    pe_tok = None
    pe_dma = (kb.dma("Y", A0b, ang0_in[:, 0, :], waits=(proj_done,)),
              kb.dma("Y", S3b, ang0_in[:, 0, :], waits=(proj_done,)))
    qt_tail = None
    for qt in range(NT):
        xap = x_sb[:, qt * D:(qt + 1) * D]
        xnt6 = xnt6_2[qt % 2]

        def fn_ln1(v, qt=qt, xap=xap, xnt6=xnt6):
            return v.tensor_scalar(xnt6, xap, mr[:, 2 * qt:2 * qt + 1],
                                   mr[:, 2 * qt + 1:2 * qt + 2],
                                   ALU.subtract, ALU.mult)
        tok_a1 = kb.op("V", fn_ln1, waits=(pe_tok, proj_done))
        pe_tok = emit_pe_add3(ctx, xnt6, A0b, S3b, 7, s_pe, pe_dma, tok_a1,
                              ctx.pb_tok[7])
        ctx.pb_tok[7] = pe_tok
        ctx.pb_wtok[7] = pe_tok
        if qt + 1 < NT:
            pe_dma = (kb.dma("Y", A0b, ang0_in[:, qt + 1, :], waits=(pe_tok,)),
                      kb.dma("Y", S3b, ang0_in[:, qt + 1, :], waits=(pe_tok,)))
        hstate = None
        for h in range(NH):
            hp, hh = h // 2, h % 2
            ptile = ptile2[h % 2]
            bs = 4 * (h % 2)
            mb = 32 + 12 * (h % 2)
            sc_toks = []
            for kc4 in range(4):
                bank = bs + kc4
                stat = [qT3[hh * 64:(hh + 1) * 64, hp, qt * P:(qt + 1) * P]]
                mov = [kT3[hh * 64:(hh + 1) * 64, hp, kc4 * 512:(kc4 + 1) * 512]]
                sc_toks.append(ctx.mm(bank, stat, mov, waits=(proj_done,)))
            tok_e = kb.op("S", lambda s, bs=bs, ptile=ptile, mb=mb: s.activation(
                ptile, ctx.pbig[:, bs * 512:(bs + 4) * 512],
                AF.Exp, bias=0.0, scale=s_ls,
                accum_out=stats[:, mb + 10:mb + 11]),
                waits=tuple(sc_toks) + (tok_rsb, tok_gm2, tok_bt, tok_bo))
            for kc4 in range(4):
                ctx.pb_tok[bs + kc4] = tok_e
            ex_toks = [tok_e]
            tok_r = kb.op("V", lambda v, h=h, mb=mb: v.reciprocal(
                stats[:, 60 + h % 2:61 + h % 2], stats[:, mb + 10:mb + 11]),
                waits=(tok_e,))
            if h == 2 and qt_tail is not None:
                x_done = qt_tail()
                qt_tail = None
            if hstate is not None:
                tail_l1(hstate)
            hstate = (h, ptile, ex_toks, tok_r, bs, otile2[qt % 2])
        tok_o = tail_l1(hstate)

        def mk_tail(qt=qt, tok_o=tok_o, xap=xap, xnt6=xnt6, pe_tok=pe_tok):
            otile, otT = otile2[qt % 2], otT2[qt % 2]

            def emit():
                pairs = [(c * P, otile[:, c * P:(c + 1) * P]) for c in range(FC)]
                ctx.transposes(3, pairs, waits=(tok_o,))
                tok_e = ctx.evac("V", 3, lambda v: v.tensor_copy(
                    otT, ctx.pbb[3][:, 0:FC * P]))
                u_evac = []
                for half in range(2):
                    bank = half
                    for c in range(FC):
                        wb = woT_buf[(c % 3) * 2 + half]
                        tok_w = kb.dma(
                            "G", wb,
                            woT_in[c * P:(c + 1) * P, half * 512:(half + 1) * 512],
                            waits=(wo_tok.get((c % 3, half)), proj_done))
                        tok_m = ctx.mm(bank, [otT[:, c * P:(c + 1) * P]], [wb],
                                       waits=(tok_e, tok_w), start=(c == 0),
                                       stop=(c == FC - 1))
                        wo_tok[(c % 3, half)] = tok_m
                    u_evac.append((half, bank))
                u_toks = []
                for half, bank in u_evac:
                    tok_u = ctx.evac("V", bank, lambda v, half=half, bank=bank:
                                     v.tensor_add(
                        xnt6[:, half * 512:(half + 1) * 512],
                        xnt6[:, half * 512:(half + 1) * 512],
                        ctx.pb[bank][:, 0:512]), waits=(pe_tok,))
                    u_toks.append(tok_u)
                tok_u2 = kb.op("V", lambda v: v.tensor_add(xnt6, xnt6, bob),
                               waits=tuple(u_toks) + (tok_bo,))

                def fn_bn(v):
                    v.bn_stats(stats[:, 8:14], xnt6[:, 0:512])
                    v.bn_aggr(stats[:, 20:22], stats[:, 8:14])
                    v.bn_stats(stats[:, 8:14], xnt6[:, 512:1024])
                    v.bn_aggr(stats[:, 22:24], stats[:, 8:14])
                    v.tensor_add(stats[:, 0:1], stats[:, 20:21], stats[:, 22:23])
                    v.tensor_scalar(stats[:, 0:1], stats[:, 0:1], 0.5, None,
                                    ALU.mult)
                    v.tensor_sub(stats[:, 6:7], stats[:, 20:21], stats[:, 22:23])
                    v.tensor_scalar(stats[:, 6:7], stats[:, 6:7], stats[:, 6:7],
                                    0.25, ALU.mult, ALU.mult)
                    v.tensor_add(stats[:, 1:2], stats[:, 21:22], stats[:, 23:24])
                    v.tensor_scalar(stats[:, 1:2], stats[:, 1:2], 0.5, None,
                                    ALU.mult)
                    return v.tensor_add(stats[:, 1:2], stats[:, 1:2], stats[:, 6:7])
                tok_v = kb.op("V", fn_bn, waits=(tok_u2,))

                def fn_rstd(s):
                    s.activation(stats[:, 2:3], stats[:, 1:2], AF.Ln, bias=s_eps,
                                 scale=1.0)
                    return s.activation(stats[:, 3:4], stats[:, 2:3], AF.Exp,
                                        bias=0.0, scale=-0.5)
                tok_rs = kb.op("S", fn_rstd, waits=(tok_v, tok_rsb))

                def fn_z(v, xap=xap):
                    v.tensor_scalar(xnt6, xnt6, stats[:, 0:1], stats[:, 3:4],
                                    ALU.subtract, ALU.mult)
                    v.tensor_mul(xnt6, xnt6, gamb)
                    v.tensor_add(xnt6, xnt6, betb)
                    return v.tensor_add(xap, xap, xnt6)
                return kb.op("V", fn_z, waits=(tok_rs, tok_gm2, tok_bt))
            return emit
        if qt == NT - 1:
            x_done = mk_tail()()
        else:
            qt_tail = mk_tail()

    ar.pop()
    return x_done



def emit_l1_ffn(ctx, x_sb, x_tok, wqkv_in, qs_b_in, ks_b_in, wo_in, lg_b_in,
                lb_b_in, w1_in, b1_b_in, w2_in, b2_b_in, rot_in, mask_in,
                xspill, entry=None, stop_after=None):
    kb, ar = ctx.kb, ctx.arena
    ar.push()
    stats0 = ar.alloc_f32(32)
    sq0 = ar.alloc_b16(D)
    tok_sp = kb.dma("Y", xspill.rearrange("(t p) f -> p t f", p=P),
                    x_sb.rearrange("p (t f) -> p t f", t=NT),
                    waits=(x_tok, entry))
    prev_s = None
    ln_toks = []
    for tt in range(NT):
        xap = x_sb[:, tt * D:(tt + 1) * D]

        def cb(mean, rstd, tok_s, xap=xap):
            tok_a = kb.op("V", lambda v, xap=xap: v.tensor_scalar(
                xap, xap, mean, rstd, ALU.subtract, ALU.mult),
                waits=(tok_s, tok_sp))
            ln_toks.append(tok_a)
        prev_s = emit_ln(ctx, stats0, xap, cb, waits=(x_tok, entry, tok_sp),
                         prev_s=prev_s, sq_scratch=sq0)
    ar.pop()
    done = ln_toks[-1]
    for layer in range(2):
        done = _l1f_layer(ctx, x_sb, done, layer, wqkv_in, qs_b_in, ks_b_in,
                          wo_in, lg_b_in, lb_b_in, w1_in, b1_b_in, w2_in,
                          b2_b_in, rot_in, mask_in, done, stop_after)
        if stop_after is not None:
            return done
    ar.push()
    xb = ar.alloc_f32(D)
    prev = None
    for tt in range(NT):
        tok_l = kb.dma("Y", xb, xspill[tt * P:(tt + 1) * P, :],
                       waits=(done, prev))
        prev = kb.op("V", lambda v, tt=tt: v.tensor_add(
            x_sb[:, tt * D:(tt + 1) * D], x_sb[:, tt * D:(tt + 1) * D], xb),
            waits=(tok_l, done))
    ar.pop()
    return prev


def _l1f_layer(ctx, x_sb, x_tok, layer, wqkv_in, qs_b_in, ks_b_in, wo_in,
               lg_b_in, lb_b_in, w1_in, b1_b_in, w2_in, b2_b_in, rot_in,
               mask_in, entry, stop_after):
    kb, ar = ctx.kb, ctx.arena
    NHH, DKH, INNER, HID, CH = 8, 64, 512, 2730, 455
    ar.push()
    qkv = ar.alloc_b16(NT * 1536)
    xTr = ar.alloc_b16(NT * D)
    scr = ar.alloc_b16(16 * 1024)
    sh = ar.alloc_f32(D)
    shb = ar.alloc_b16(D)
    stats = ar.alloc_f32(64)
    qsb = ar.alloc_f32(DKH)
    ksb = ar.alloc_f32(DKH)
    scr2 = ar.alloc_b16(2048)
    qkv3 = qkv.rearrange("p (t f) -> p t f", t=NT)
    xT3 = xTr.rearrange("p (c s) -> p c s", c=FC)
    qrT = xTr[:, 0:4 * SEQ]
    krT = xTr[:, 4 * SEQ:8 * SEQ]
    qrT3 = qrT.rearrange("p (c s) -> p c s", c=4)
    krT3 = krT.rearrange("p (c s) -> p c s", c=4)
    wqkv_sb = scr[:, 0:FC * 1536]
    sqs = scr[:, 0:1024].bitcast(F32)
    qh = scr[:, 1024:2048].bitcast(F32)
    qr_t = scr[:, 2048:3072].bitcast(F32)
    rtile = scr[:, 3072:3584].bitcast(F32)
    norms = scr[:, 3584:3600].bitcast(F32)
    qrb = scr[:, 4096:4608]
    mask_buf = [scr[:, 4096:6144].bitcast(F32), scr[:, 6144:8192].bitcast(F32)]
    sf32_2 = [scr[:, 8192:10240].bitcast(F32), scr2.bitcast(F32)]
    ptile2_c = [scr[:, 10240:11264], scr[:, 13312:14336]]
    ptT2_c = [scr[:, 11264:12288], scr[:, 14336:15360]]
    otile2_c = [scr[:, 12288:12800], scr[:, 15360:15872]]
    otT2_c = [scr[:, 12800:13312], scr[:, 15872:16384]]
    wo_sb = scr[:, 0:4 * 1024]
    wo4 = wo_sb.rearrange("p (c f) -> p c f", c=4)

    W = 512

    sh_toks = []
    tok_wq = kb.dma("G", wqkv_sb.rearrange("p (c f) -> p c f", c=FC),
                    wqkv_in[layer].rearrange("(c p) f -> p c f", p=P),
                    waits=(entry,))
    tok_qs = kb.dma("Y", qsb, qs_b_in[layer], waits=(entry,))
    tok_ks = kb.dma("Y", ksb, ks_b_in[layer], waits=(entry,))
    qkv_toks = []
    prev_sh = [None, None]
    for tt in range(NT):
        tok_c1 = kb.op("V", lambda v, tt=tt: v.tensor_copy(
            sh[:, 0:W], x_sb[:, tt * D:tt * D + W]),
            waits=(x_tok, entry, prev_sh[0]))
        if tt == 0:
            tok_c2 = kb.op("V", lambda v: v.memset(sh[0:1, W:D], 0.0),
                           waits=(prev_sh[1],))
            tok_c3 = kb.dma("G", sh[1:P, W:D], x_sb[0:P - 1, W:D],
                            waits=(x_tok, entry, tok_c2))
        else:
            tok_c2 = kb.dma("G", sh[0:1, W:D],
                            x_sb[P - 1:P, (tt - 1) * D + W:(tt - 1) * D + D],
                            waits=(x_tok, prev_sh[1]))
            tok_c3 = kb.dma("G", sh[1:P, W:D],
                            x_sb[0:P - 1, tt * D + W:tt * D + D],
                            waits=(x_tok, tok_c2))
        tok_cast = kb.op("V", lambda v: v.tensor_copy(shb, sh),
                         waits=(tok_c1, tok_c3))
        prev_sh[0] = tok_cast
        prev_sh[1] = tok_cast
        bank = 6 + tt % 2
        pairs = [(c * P, shb[:, c * P:(c + 1) * P]) for c in range(FC)]
        tok_t = ctx.transposes(bank, pairs, waits=(tok_cast, ctx.c["load_tok"]))
        tok_e = ctx.evac("V", bank, lambda v, tt=tt, bank=bank: v.tensor_copy(
            xT3[:, :, tt * P:(tt + 1) * P],
            ctx.pbb[bank][:, 0:FC * P].rearrange("p (c s) -> p c s", s=P)))
        sh_toks.append(tok_e)
        for ch in range(3):
            bank = (tt % 2) * 3 + ch
            stat = [xT3[:, c, tt * P:(tt + 1) * P] for c in range(FC)]
            mov = [wqkv_sb.rearrange("p (c f) -> p c f", c=FC)[
                :, c, ch * 512:(ch + 1) * 512] for c in range(FC)]
            tok_m = ctx.mm(bank, stat, mov, waits=(tok_e, tok_wq))
            if ch % 2 == 0:
                tok_q = ctx.evac("V", bank, lambda v, tt=tt, ch=ch, bank=bank:
                                 v.tensor_copy(qkv3[:, tt, ch * 512:(ch + 1) * 512],
                                               ctx.pb[bank][:, 0:512]))
            else:
                tok_q = ctx.evac("S", bank, lambda s_, tt=tt, ch=ch, bank=bank:
                                 s_.copy(qkv3[:, tt, ch * 512:(ch + 1) * 512],
                                         ctx.pb[bank][:, 0:512]))
            qkv_toks.append(tok_q)
    qkv_done = qkv_toks[-1]
    if stop_after == "qkv":
        ctx.dbg = dict(qkv=qkv)
        ar.pop()
        return qkv_done

    rot_toks = []
    prev_r = None
    for tt in range(NT):
        tok_rt = kb.dma("Y", rtile.rearrange("p (r k) -> p r k", r=4),
                        rot_in.rearrange("r (t p) k -> p t r k", p=P)[:, tt, :, :],
                        waits=(prev_r, qkv_done))
        for qk in range(2):
            base = qk * 512
            srcap = qkv3[:, tt, base:base + 512]
            tok_sq = kb.op("S", lambda s, srcap=srcap: s.activation(
                sqs, srcap, AF.Square, bias=0.0, scale=1.0),
                waits=(qkv_done, prev_r))

            def fn_ss(v, tt=tt):
                last = None
                for h in range(NHH):
                    last = v.reduce_sum(stats[:, 16 + h:17 + h],
                                        sqs[:, h * DKH:(h + 1) * DKH], axis=AX.X)
                last = v.tensor_scalar(stats[:, 16:24], stats[:, 16:24], 1e-24,
                                       None, ALU.max)
                return last
            tok_ss = kb.op("V", fn_ss, waits=(tok_sq,))

            def fn_rn(s):
                s.activation(stats[:, 24:32], stats[:, 16:24], AF.Ln, bias=0.0,
                             scale=1.0)
                return s.activation(stats[:, 24:32], stats[:, 24:32], AF.Exp,
                                    bias=0.0, scale=-0.5)
            tok_rn = kb.op("S", fn_rn, waits=(tok_ss,))
            def fn_qh(v, srcap=srcap, qk=qk):
                last = None
                for h in range(NHH):
                    last = v.tensor_scalar(qh[:, h * DKH:(h + 1) * DKH],
                                           srcap[:, h * DKH:(h + 1) * DKH],
                                           stats[:, 24 + h:25 + h], None, ALU.mult)
                return last
            tok_qh = kb.op("V", fn_qh, waits=(tok_rn,))
            scb = qsb if qk == 0 else ksb
            sc_tok = tok_qs if qk == 0 else tok_ks
            qh3 = qh.rearrange("p (h k) -> p h k", h=NHH)
            qr3 = qr_t.rearrange("p (h k) -> p h k", h=NHH)
            cs = rtile[:, qk * 128:qk * 128 + 64]
            ss = rtile[:, qk * 128 + 64:qk * 128 + 128]

            def fn_rot2(v, qh3=qh3, qr3=qr3, cs=cs, ss=ss, scb=scb):
                scb3 = scb.rearrange("p (o k) -> p o k", o=1).broadcast_to(
                    (P, NHH, DKH))
                cs3 = cs.rearrange("p (o k) -> p o k", o=1).broadcast_to(
                    (P, NHH, DKH))
                ss3 = ss.rearrange("p (o k) -> p o k", o=1).broadcast_to(
                    (P, NHH, DKH))
                v.tensor_mul(qh3, qh3, scb3)
                v.tensor_mul(qr3, qh3, cs3)
                v.tensor_mul(sqs.rearrange("p (h k) -> p h k", h=NHH)[:, :, 0:32],
                             qh3[:, :, 32:64], ss3[:, :, 0:32])
                v.tensor_mul(sqs.rearrange("p (h k) -> p h k", h=NHH)[:, :, 32:64],
                             qh3[:, :, 0:32], ss3[:, :, 32:64])
                v.tensor_sub(qr3[:, :, 0:32], qr3[:, :, 0:32],
                             sqs.rearrange("p (h k) -> p h k", h=NHH)[:, :, 0:32])
                return v.tensor_add(
                    qr3[:, :, 32:64], qr3[:, :, 32:64],
                    sqs.rearrange("p (h k) -> p h k", h=NHH)[:, :, 32:64])
            tok_rot = kb.op("V", fn_rot2, waits=(tok_qh, tok_rt, sc_tok))
            tok_cb = kb.op("V", lambda v: v.tensor_copy(qrb, qr_t),
                           waits=(tok_rot,))
            bank = 6 + qk
            pairs = [(c * P, qrb[:, c * P:(c + 1) * P]) for c in range(4)]
            tok_t = ctx.transposes(bank, pairs, waits=(tok_cb,))
            dstT = qrT3 if qk == 0 else krT3
            tok_e = ctx.evac("V", bank, lambda v, tt=tt, bank=bank, dstT=dstT:
                             v.tensor_copy(
                dstT[:, :, tt * P:(tt + 1) * P],
                ctx.pbb[bank][:, 0:4 * P].rearrange("p (c s) -> p c s", s=P)))
            rot_toks.append(tok_e)
            prev_r = tok_e
    rot_done = rot_toks[-1]
    if stop_after == "rot":
        ctx.dbg = dict(qrT=qrT, krT=krT, qkv=qkv)
        ar.pop()
        return rot_done

    tok_wo = kb.dma("G", wo4, wo_in[layer].rearrange("(c p) f -> p c f", p=P),
                    waits=(rot_done,))
    QKS = 8.0
    x_done = None
    m_toks = {}
    war_sf = [None, None]
    war_pt = [None, None]
    qt_tail = None

    def tail_c(hh_state):
        h, par, ex_toks, tok_r, otile_p, w0_, kb0_ = hh_state
        ptile_p, ptT_p = ptile2_c[par], ptT2_c[par]
        tb, ab = 4 + 2 * par, 5 + 2 * par
        pairs = [(i * P, ptile_p[:, i * P:(i + 1) * P]) for i in range(8)]
        tok_t = ctx.transposes(tb, pairs, waits=tuple(ex_toks))
        war_pt[par] = tok_t
        tok_pte = ctx.evac("V", tb, lambda v, tb=tb, ptT_p=ptT_p: v.tensor_copy(
            ptT_p, ctx.pbb[tb][:, 0:8 * P]))
        cstart = 0 if w0_ > 0 else 4
        stat = [ptT_p[:, i * P:(i + 1) * P] for i in range(cstart, 8)]
        mov = [qkv3[:, kb0_ // P + i, 1024 + h * DKH:1024 + (h + 1) * DKH]
               for i in range(cstart, 8)]
        ctx.mm(ab, stat, mov, waits=(tok_pte, qkv_done), n=DKH)
        return ctx.evac("V", ab, lambda v, h=h, ab=ab, par=par, otile_p=otile_p:
                        v.tensor_scalar(
            otile_p[:, h * DKH:(h + 1) * DKH], ctx.pb[ab][:, 0:DKH],
            stats[:, 60 + par:61 + par], None, ALU.mult), waits=(tok_r,))

    for qt in range(NT):
        w0 = qt // 4
        kb0 = (w0 - 1) * 512
        valid_g = (0, 1) if w0 > 0 else (1,)
        mb_m = mask_buf[qt % 2]
        tok_mk = kb.dma("Y", mb_m, mask_in[qt], waits=(m_toks.get(qt % 2), rot_done))
        hstate = None
        last_s = None
        for h in range(NHH):
            hp, hh = h // 2, h % 2
            par = h % 2
            sb = 2 * par
            cb = 16 + 12 * par
            sf = sf32_2[par]
            ptile_p = ptile2_c[par]
            ms_toks = []
            for g in valid_g:
                bank = sb + g
                stat = [qrT3[hh * 64:(hh + 1) * 64, hp, qt * P:(qt + 1) * P]]
                kbase = kb0 + g * 512
                mov = [krT3[hh * 64:(hh + 1) * 64, hp, kbase:kbase + 512]]
                tok_m = ctx.mm(bank, stat, mov, waits=(rot_done,))
                tok_s = ctx.evac("V", bank, lambda v, g=g, bank=bank, mb_m=mb_m,
                                 sf=sf: v.tensor_add(
                    sf[:, g * 512:(g + 1) * 512], ctx.pb[bank][:, 0:512],
                    mb_m[:, g * 512:(g + 1) * 512]),
                    waits=(tok_mk, war_sf[par]))
                ms_toks.append(tok_s)
                last_s = tok_s
            ex_toks = []
            if w0 == 0:
                ex_toks.append(kb.op("S", lambda s, ptile_p=ptile_p: s.memzero(
                    ptile_p[:, 0:512]), waits=(war_pt[par],)))
            for g in valid_g:
                tok_e = kb.op("S", lambda s, g=g, sf=sf, cb=cb, ptile_p=ptile_p:
                              s.activation(
                    ptile_p[:, g * 512:(g + 1) * 512],
                    sf[:, g * 512:(g + 1) * 512],
                    AF.Exp, bias=0.0, scale=QKS,
                    accum_out=stats[:, cb + 6 + g:cb + 7 + g]),
                    waits=tuple(ms_toks) + (war_pt[par],))
                ex_toks.append(tok_e)
            war_sf[par] = ex_toks[-1]
            if w0 > 0:
                tok_r = kb.op("V", lambda v, cb=cb: v.tensor_add(
                    stats[:, cb + 10:cb + 11], stats[:, cb + 6:cb + 7],
                    stats[:, cb + 7:cb + 8]), waits=tuple(ex_toks))
            else:
                tok_r = kb.op("V", lambda v, cb=cb: v.tensor_copy(
                    stats[:, cb + 10:cb + 11], stats[:, cb + 7:cb + 8]),
                    waits=tuple(ex_toks))
            tok_r = kb.op("V", lambda v, cb=cb, par=par: v.reciprocal(
                stats[:, 60 + par:61 + par], stats[:, cb + 10:cb + 11]),
                waits=(tok_r,))
            if h == 2 and qt_tail is not None:
                x_done = qt_tail()
                qt_tail = None
            if hstate is not None:
                tail_c(hstate)
            hstate = (h, par, ex_toks, tok_r, otile2_c[qt % 2], w0, kb0)
        tok_o = tail_c(hstate)
        m_toks[qt % 2] = last_s

        def mk_tail(qt=qt, tok_o=tok_o):
            otile_p, otT_p = otile2_c[qt % 2], otT2_c[qt % 2]

            def emit():
                pairs = [(c * P, otile_p[:, c * P:(c + 1) * P]) for c in range(4)]
                ctx.transposes(6, pairs, waits=(tok_o,))
                tok_e = ctx.evac("V", 6, lambda v: v.tensor_copy(
                    otT_p, ctx.pbb[6][:, 0:4 * P]))
                xd = None
                for half in range(2):
                    bank = 5 if half == 0 else 7
                    stat = [otT_p[:, c * P:(c + 1) * P] for c in range(4)]
                    mov = [wo4[:, c, half * 512:(half + 1) * 512] for c in range(4)]
                    ctx.mm(bank, stat, mov, waits=(tok_e, tok_wo))
                    xap = x_sb[:, qt * D + half * 512:qt * D + (half + 1) * 512]
                    xd = ctx.evac("V", bank, lambda v, xap=xap, bank=bank:
                                  v.tensor_add(xap, xap, ctx.pb[bank][:, 0:512]))
                return xd
            return emit
        if qt == NT - 1:
            x_done = mk_tail()()
        else:
            qt_tail = mk_tail()
    attn_done = x_done
    if stop_after == "attn":
        ar.pop()
        return attn_done

    ar.pop()
    ar.push()
    xnT = ar.alloc_b16(NT * D)
    b1b = ar.alloc_f32(2 * HID)
    lgb = ar.alloc_f32(D)
    lbb = ar.alloc_f32(D)
    b2b = ar.alloc_f32(D)
    w1buf = ar.alloc_b16(FC * 2 * CH)
    w2buf = ar.alloc_b16(4 * D)
    Aa = ar.alloc_f32(CH)
    Bg = ar.alloc_f32(CH)
    Cg = ar.alloc_f32(CH)
    h4 = ar.alloc_b16(CH)
    h4T = ar.alloc_b16(4 * P)
    sh2 = ar.alloc_f32(D)
    sh2b = ar.alloc_b16(D)
    stats2 = ar.alloc_f32(32)
    xnT3 = xnT.rearrange("p (c s) -> p c s", c=FC)
    w1b3 = w1buf.rearrange("p (c f) -> p c f", c=FC)

    tok_b1 = kb.dma("Y", b1b, b1_b_in[layer], waits=(attn_done,))
    tok_lg = kb.dma("Y", lgb, lg_b_in[layer], waits=(attn_done,))
    tok_lb = kb.dma("Y", lbb, lb_b_in[layer], waits=(attn_done,))
    tok_b2 = kb.dma("Y", b2b, b2_b_in[layer], waits=(attn_done,))

    xn_toks = []
    prev = [None, None]
    prev_s = None
    for tt in range(NT):
        tok_c1 = kb.op("V", lambda v, tt=tt: v.tensor_copy(
            sh2[:, 0:512], x_sb[:, tt * D:tt * D + 512]),
            waits=(attn_done, prev[0]))
        if tt == 0:
            tok_c2 = kb.op("V", lambda v: v.memset(sh2[0:1, 512:D], 0.0),
                           waits=(prev[1],))
            tok_c3 = kb.dma("G", sh2[1:P, 512:D], x_sb[0:P - 1, 512:D],
                            waits=(attn_done, tok_c2))
        else:
            tok_c2 = kb.dma("G", sh2[0:1, 512:D],
                            x_sb[P - 1:P, (tt - 1) * D + 512:(tt - 1) * D + D],
                            waits=(attn_done, prev[1]))
            tok_c3 = kb.dma("G", sh2[1:P, 512:D],
                            x_sb[0:P - 1, tt * D + 512:tt * D + D],
                            waits=(attn_done, tok_c2))
        sh_ready = (tok_c1, tok_c3)

        def cb(mean, rstd, tok_s, tt=tt):
            def fn_ap(v):
                v.tensor_scalar(sh2, sh2, mean, rstd, ALU.subtract, ALU.mult)
                v.tensor_mul(sh2, sh2, lgb)
                v.tensor_add(sh2, sh2, lbb)
                return v.tensor_copy(sh2b, sh2)
            tok_a = kb.op("V", fn_ap, waits=(tok_s, tok_lg, tok_lb))
            bank = 6 + tt % 2
            pairs = [(c * P, sh2b[:, c * P:(c + 1) * P]) for c in range(FC)]
            tok_t = ctx.transposes(bank, pairs, waits=(tok_a,))
            tok_e = ctx.evac("V", bank, lambda v, tt=tt, bank=bank: v.tensor_copy(
                xnT3[:, :, tt * P:(tt + 1) * P],
                ctx.pbb[bank][:, 0:FC * P].rearrange("p (c s) -> p c s", s=P)))
            xn_toks.append(tok_e)
            prev[0] = tok_e
            prev[1] = tok_e
        prev_s = emit_ln(ctx, stats2, sh2, cb, waits=sh_ready, prev_s=prev_s,
                         sq_scratch=sh2b)
    xn_done = xn_toks[-1]

    w1_tok = None
    w2_tok = None
    x_done2 = xn_done
    prev_h4 = None
    prev_ht = None
    for j in range(6):
        tok_w1a = kb.dma("G",
                         w1b3[:, :, 0:CH],
                         w1_in[layer, :, j * CH:(j + 1) * CH].rearrange(
                             "(c p) f -> p c f", p=P), waits=(w1_tok, xn_done))
        tok_w1g = kb.dma("G",
                         w1b3[:, :, CH:2 * CH],
                         w1_in[layer, :, HID + j * CH:HID + (j + 1) * CH].rearrange(
                             "(c p) f -> p c f", p=P), waits=(w1_tok, xn_done))
        rb_sizes = (P, P, P, CH - 3 * P)
        roff = 0
        w2aps = []
        w2l_toks = []
        for bi, rsz in enumerate(rb_sizes):
            w2l_toks.append(kb.dma("G", w2buf.rearrange("p (b f) -> p b f", b=4)[
                0:rsz, bi, :],
                w2_in[layer, j * CH + roff:j * CH + roff + rsz, :],
                waits=(w2_tok, xn_done)))
            w2aps.append((rsz, w2buf.rearrange("p (b f) -> p b f", b=4)[0:rsz, bi, :]))
            roff += rsz
        for tt in range(NT):
            stat = [xnT3[:, c, tt * P:(tt + 1) * P] for c in range(FC)]
            movA = [w1b3[:, c, 0:CH] for c in range(FC)]
            movG = [w1b3[:, c, CH:2 * CH] for c in range(FC)]
            tok_mA = ctx.mm(0, stat, movA, waits=(tok_w1a, xn_done))
            tok_mG = ctx.mm(1, stat, movG, waits=(tok_w1g, xn_done))
            tok_a = ctx.evac("V", 0, lambda v, j=j: v.tensor_add(
                Aa, ctx.pb[0][:, 0:CH],
                b1b[:, j * CH:(j + 1) * CH]), waits=(tok_b1, prev_h4))
            tok_g = ctx.evac("V", 1, lambda v, j=j: v.tensor_add(
                Bg, ctx.pb[1][:, 0:CH],
                b1b[:, HID + j * CH:HID + (j + 1) * CH]), waits=(tok_b1, prev_h4))
            tok_gl = kb.op("S", lambda s: s.activation(
                Cg, Bg, AF.Gelu, bias=0.0, scale=1.0), waits=(tok_g,))
            tok_h4 = kb.op("G", lambda g: g.tensor_mul(h4, Aa, Cg),
                           waits=(tok_gl, tok_a, prev_ht))
            prev_h4 = tok_h4
            pairs = []
            roff = 0
            for bi, rsz in enumerate(rb_sizes):
                pairs.append((bi * P, h4[:, roff:roff + rsz]))
                roff += rsz
            tok_t = ctx.transposes(2, pairs, waits=(tok_h4,))
            prev_ht = tok_t

            def fn_he(s_, rb_sizes=rb_sizes):
                last = None
                for bi, rsz in enumerate(rb_sizes):
                    last = s_.copy(h4T[0:rsz, bi * P:bi * P + P],
                                   ctx.pbb[2][0:rsz, bi * P:bi * P + P])
                return last
            tok_he = ctx.evac("S", 2, fn_he)
            for half in range(2):
                bank = 3 + (tt % 2) * 2 + half
                stat2 = []
                mov2 = []
                for bi, rsz in enumerate(rb_sizes):
                    stat2.append(h4T[0:rsz, bi * P:bi * P + P])
                    mov2.append(w2aps[bi][1][:, half * 512:(half + 1) * 512])
                tok_m2 = ctx.mm(bank, stat2, mov2,
                                waits=(tok_he,) + tuple(w2l_toks))
                xap = x_sb[:, tt * D + half * 512:tt * D + (half + 1) * 512]
                x_done2 = ctx.evac("V", bank, lambda v, xap=xap, bank=bank:
                                   v.tensor_add(xap, xap, ctx.pb[bank][:, 0:512]),
                                   waits=(x_done2,))
            w1_tok = tok_mG if tt == NT - 1 else w1_tok
            w2_tok = tok_m2 if tt == NT - 1 else w2_tok
    fin = x_done2
    for tt in range(NT):
        fin = kb.op("V", lambda v, tt=tt: v.tensor_add(
            x_sb[:, tt * D:(tt + 1) * D], x_sb[:, tt * D:(tt + 1) * D], b2b),
            waits=(fin, tok_b2))
    ar.pop()
    return fin




def build_program(n_batch=NB_PER_CORE, blocks=("l0a",), heap_kib=None, dbg_dump=(), stop_after=None):
    nc = bass.Bass("TRN2", target_bir_lowering=False)
    kb = KB(nc)
    ar = Arena(nc, "heap", heap_kib)
    pbig = nc.alloc_psum_tensor("pbig", [P, 4096], F32).ap()
    pb = [pbig[:, i * 512:(i + 1) * 512] for i in range(8)]

    x_in = nc.dram_tensor("x", [n_batch, SEQ, D], F32, kind="ExternalInput").ap()
    y_out = nc.dram_tensor("y", [n_batch, SEQ, D], F32, kind="ExternalOutput").ap()
    ident_in = nc.dram_tensor("cst_ident", [P, P], F32, kind="ExternalInput").ap()

    wq_in = wo_in = None
    if "l0a" in blocks:
        wq_in = nc.dram_tensor("L0_attn_w_qkv", [16, 3, D, 64], F32,
                               kind="ExternalInput").ap()
        wo_in = nc.dram_tensor("L0_attn_w_o", [D, D], F32, kind="ExternalInput").ap()
    if "l1a" in blocks:
        wqT_in = nc.dram_tensor("L1_WqT", [D, D], F32, kind="ExternalInput").ap()
        wkT_in = nc.dram_tensor("L1_WkT", [D, D], F32, kind="ExternalInput").ap()
        wvT_in = nc.dram_tensor("L1_WvT", [D, D], F32, kind="ExternalInput").ap()
        woT_in = nc.dram_tensor("L1_WoT", [D, D], F32, kind="ExternalInput").ap()
        bqcm_in = nc.dram_tensor("L1_bq_cm", [P, 8], F32, kind="ExternalInput").ap()
        bkcm_in = nc.dram_tensor("L1_bk_cm", [P, 8], F32, kind="ExternalInput").ap()
        bvb_in = nc.dram_tensor("L1_bv_b", [P, D], F32, kind="ExternalInput").ap()
        bob_in = nc.dram_tensor("L1_bo_b", [P, D], F32, kind="ExternalInput").ap()
        gamb_in = nc.dram_tensor("L1_gam_b", [P, D], F32, kind="ExternalInput").ap()
        betb_in = nc.dram_tensor("L1_bet_b", [P, D], F32, kind="ExternalInput").ap()
        rby_in = nc.dram_tensor("L1_attn_rby", [3], F32, kind="ExternalInput").ap()
        ang0_in = nc.dram_tensor("cst_ang0", [SEQ, 512], F32,
                                 kind="ExternalInput").ap().rearrange(
            "(t p) i -> p t i", p=P)
    if "l1f" in blocks:
        wqkv_in = nc.dram_tensor("L1_ffn_wqkv", [2, D, 1536], F32,
                                 kind="ExternalInput").ap()
        qsb_in = nc.dram_tensor("L1f_qs_b", [2, P, 64], F32,
                                kind="ExternalInput").ap()
        ksb_in = nc.dram_tensor("L1f_ks_b", [2, P, 64], F32,
                                kind="ExternalInput").ap()
        wol_in = nc.dram_tensor("L1_ffn_wo", [2, 512, D], F32,
                                kind="ExternalInput").ap()
        lgb_in = nc.dram_tensor("L1f_lg_b", [2, P, D], F32,
                                kind="ExternalInput").ap()
        lbb_in = nc.dram_tensor("L1f_lb_b", [2, P, D], F32,
                                kind="ExternalInput").ap()
        w1l_in = nc.dram_tensor("L1_ffn_w1", [2, D, 5460], F32,
                                kind="ExternalInput").ap()
        b1b_in = nc.dram_tensor("L1f_b1_b", [2, P, 5460], F32,
                                kind="ExternalInput").ap()
        w2l_in = nc.dram_tensor("L1_ffn_w2", [2, 2730, D], F32,
                                kind="ExternalInput").ap()
        b2b_in = nc.dram_tensor("L1f_b2_b", [2, P, D], F32,
                                kind="ExternalInput").ap()
        rot_in = nc.dram_tensor("cst_rot", [4, SEQ, 64], F32,
                                kind="ExternalInput").ap()
        mask_in = nc.dram_tensor("cst_mask", [NT, P, 1024], F32,
                                 kind="ExternalInput").ap()
        xspill = nc.dram_tensor("x_spill", [SEQ, D], F32, kind="Internal").ap()
    if "l0f" in blocks:
        rw_in = nc.dram_tensor("L0_ffn_router_weight", [8, D], F32,
                               kind="ExternalInput").ap()
        rb_in = nc.dram_tensor("L0_ffn_router_bias", [8], F32,
                               kind="ExternalInput").ap()
        gup_in = nc.dram_tensor("L0_ffn_gate_up_proj", [8, D, 4096], F32,
                                kind="ExternalInput").ap()
        gub_in = nc.dram_tensor("L0_ffn_gate_up_bias", [8, 4096], F32,
                                kind="ExternalInput").ap()
        dp_in = nc.dram_tensor("L0_ffn_down_proj", [8, 2048, D], F32,
                               kind="ExternalInput").ap()
        db_in = nc.dram_tensor("L0_ffn_down_bias", [8, D], F32,
                               kind="ExternalInput").ap()

    consts = {}
    ident_sb = ar.alloc_b16(P)
    consts["ident_b16"] = ident_sb
    t1 = kb.dma("G", ident_sb, ident_in)
    ident_f32 = ar.alloc_f32(P)
    consts["ident_f32"] = ident_f32
    t2 = kb.dma("Y", ident_f32, ident_in)
    cbuf = ar.alloc_f32(8)
    consts["c_eps"] = cbuf[:, 0:1]
    onescol = ar.alloc_f32(1)
    onesrow = ar.alloc_f32(P)
    consts["ones_col"] = onescol
    consts["ones_row"] = onesrow[0:1, 0:P]

    consts["c_pi2"] = cbuf[:, 1:2]

    def fn_c(v):
        v.memset(cbuf[:, 0:1], 1e-5)
        v.memset(cbuf[:, 1:2], PI2)
        v.memset(onescol, 1.0)
        return v.memset(onesrow[0:1, 0:P], 1.0)
    tok_eps = kb.op("V", fn_c, waits=(t1, t2))
    consts["load_tok"] = tok_eps

    x_sb = ar.alloc_f32(NT * D)
    ctx = Ctx(kb, ar, pb, consts, pbig=pbig)

    store_toks = []
    done = tok_eps
    for b in range(n_batch):
        x_tok = kb.dma("Y", x_sb.rearrange("p (t f) -> p t f", t=NT),
                       x_in[b].rearrange("(t p) f -> p t f", p=P),
                       waits=(store_toks[-1] if store_toks else None, done))
        entry = done
        for blk in blocks:
            if blk == "l0a":
                done = emit_l0_attn(ctx, x_sb, x_tok, wq_in, wo_in, entry=entry,
                                    stop_after=stop_after)
            elif blk == "l0f":
                done = emit_l0_ffn(ctx, x_sb, x_tok, rw_in, rb_in, gup_in, gub_in,
                                   dp_in, db_in, entry=entry)
            elif blk == "l1a":
                done = emit_l1_attn(ctx, x_sb, x_tok, wqT_in, wkT_in, wvT_in,
                                    woT_in, bqcm_in, bkcm_in, bvb_in, bob_in,
                                    gamb_in, betb_in, rby_in, ang0_in, entry=entry,
                                    stop_after=stop_after)
            elif blk == "l1f":
                done = emit_l1_ffn(ctx, x_sb, x_tok, wqkv_in, qsb_in, ksb_in,
                                   wol_in, lgb_in, lbb_in, w1l_in, b1b_in,
                                   w2l_in, b2b_in, rot_in, mask_in, xspill,
                                   entry=entry, stop_after=stop_after)
            build_program.last_ctx = ctx
            x_tok = done
            entry = done
        st = kb.dma("Y", y_out[b].rearrange("(t p) f -> p t f", p=P),
                    x_sb.rearrange("p (t f) -> p t f", t=NT), waits=(done,))
        store_toks.append(st)

    dbg_aps = getattr(build_program, "last_ctx", None)
    for nm in dbg_dump:
        ap = dbg_aps.dbg[nm]
        dt = nc.dram_tensor("dbg_" + nm, [P, ap.shape[-1]],
                            BF16 if str(ap.dtype) == str(BF16) else F32,
                            kind="ExternalOutput").ap()
        store_toks.append(kb.dma("Y", dt, ap, waits=(done,)))

    kb.op("Y", lambda e: None, waits=(store_toks[-1],), inc=0)
    kb.emit()
    return nc


def kernel(**inputs):
    from concourse.bass_utils import run_bass_kernel_spmd
    x = inputs["L0_attn_inputs"]
    shared = make_shared_inmap(inputs)
    nc = build_program(n_batch=2, blocks=("l0a", "l0f", "l1a", "l1f"))
    in_maps = [dict(shared, x=np.ascontiguousarray(x[2 * c:2 * c + 2]))
               for c in range(8)]
    res = run_bass_kernel_spmd(nc, in_maps, core_ids=list(range(8)))
    return np.concatenate([r["y"] for r in res.results], axis=0)


def make_shared_inmap(inputs):
    f32 = np.float32
    rep = lambda a, shape: np.ascontiguousarray(np.broadcast_to(a, shape))
    cm = lambda b: np.ascontiguousarray(b.reshape(8, 128).T)
    ident = np.eye(128, dtype=f32)
    denom0 = np.power(10000.0, 2.0 * np.arange(0, 1024, 2, dtype=f32) / 1024.0
                      ).astype(f32)
    ang0 = (np.arange(2048, dtype=f32)[:, None] / denom0[None, :]).astype(f32)
    DIM_HEAD, WINDOW = 64, 512
    half_idx = np.arange(0, DIM_HEAD, 2, dtype=f32)
    inv_freq = (1.0 / 10000.0 ** (half_idx / DIM_HEAD)).astype(f32)
    t = np.arange(2048, dtype=f32)
    freqs = t[:, None] * inv_freq[None, :]
    freqs = np.concatenate([freqs, freqs], axis=-1)
    base = ((half_idx + 0.4 * DIM_HEAD) / (1.4 * DIM_HEAD)).astype(f32)
    power = (t - 1024) / float(WINDOW // 2)
    scale = (base[None, :] ** power[:, None]).astype(f32)
    scale = np.concatenate([scale, scale], axis=-1)
    cosf, sinf = np.cos(freqs), np.sin(freqs)
    rot = np.stack([cosf * scale, sinf * scale, cosf / scale, sinf / scale]
                   ).astype(f32)
    masks = np.zeros((16, 128, 1024), f32)
    for qt in range(16):
        i = qt * 128 + np.arange(128)[:, None]
        jg = (qt // 4 - 1) * 512 + np.arange(1024)[None, :]
        valid = (jg >= 0) & (jg <= i) & (i - jg <= 512)
        masks[qt] = np.where(valid, 0.0, -1e30).astype(f32)
    ii = inputs
    return {
        "L0_attn_w_qkv": ii["L0_attn_w_qkv"],
        "L0_attn_w_o": ii["L0_attn_w_o"],
        "L0_ffn_router_weight": ii["L0_ffn_router_weight"],
        "L0_ffn_router_bias": ii["L0_ffn_router_bias"],
        "L0_ffn_gate_up_proj": ii["L0_ffn_gate_up_proj"],
        "L0_ffn_gate_up_bias": ii["L0_ffn_gate_up_bias"],
        "L0_ffn_down_proj": ii["L0_ffn_down_proj"],
        "L0_ffn_down_bias": ii["L0_ffn_down_bias"],
        "L1_WqT": np.ascontiguousarray(ii["L1_attn_Wq"].T),
        "L1_WkT": np.ascontiguousarray(ii["L1_attn_Wk"].T),
        "L1_WvT": np.ascontiguousarray(ii["L1_attn_Wv"].T),
        "L1_WoT": np.ascontiguousarray(ii["L1_attn_Wo"].T),
        "L1_bq_cm": cm(ii["L1_attn_bq"]),
        "L1_bk_cm": cm(ii["L1_attn_bk"]),
        "L1_bv_b": rep(ii["L1_attn_bv"], (128, 1024)),
        "L1_bo_b": rep(ii["L1_attn_bo"], (128, 1024)),
        "L1_gam_b": rep(ii["L1_attn_ln_gamma"], (128, 1024)),
        "L1_bet_b": rep(ii["L1_attn_ln_beta"], (128, 1024)),
        "L1_attn_rby": ii["L1_attn_rby"],
        "cst_ang0": ang0,
        "L1_ffn_wqkv": ii["L1_ffn_wqkv"],
        "L1f_qs_b": rep(ii["L1_ffn_q_scale"][:, None, :], (2, 128, 64)),
        "L1f_ks_b": rep(ii["L1_ffn_k_scale"][:, None, :], (2, 128, 64)),
        "L1_ffn_wo": ii["L1_ffn_wo"],
        "L1f_lg_b": rep(ii["L1_ffn_ln_g"][:, None, :], (2, 128, 1024)),
        "L1f_lb_b": rep(ii["L1_ffn_ln_b"][:, None, :], (2, 128, 1024)),
        "L1_ffn_w1": ii["L1_ffn_w1"],
        "L1f_b1_b": rep(ii["L1_ffn_b1"][:, None, :], (2, 128, 5460)),
        "L1_ffn_w2": ii["L1_ffn_w2"],
        "L1f_b2_b": rep(ii["L1_ffn_b2"][:, None, :], (2, 128, 1024)),
        "cst_rot": rot,
        "cst_mask": masks,
        "cst_ident": ident,
    }
